# version 13
# baseline (speedup 1.0000x reference)
import math
from contextlib import ExitStack

import numpy as np

import concourse.bass as bass
import concourse.mybir as mybir
from concourse.bass_utils import run_bass_kernel_spmd

F32 = mybir.dt.float32
BF16 = mybir.dt.bfloat16
AF = mybir.ActivationFunctionType
OP = mybir.AluOpType
AX = mybir.AxisListType.X

T = 2048
E = 1024
NT = 16
NEC = 8
NPAIR = 8
NEXP = 8
LI = 0.8 - 0.6 * math.exp(-0.3)

_STAGE = 99
_TAPS = []


class SemT:
    def __init__(self, h):
        self.h = h
        self.n = 0

    def inc(self, instr, k=1):
        instr.then_inc(self.h, k)
        self.n += k
        return instr


def _host_prep(inputs):
    Wq = np.asarray(inputs["L0_attn_Wq"], dtype=np.float32)
    Wk = np.asarray(inputs["L0_attn_Wk"], dtype=np.float32)
    Wv = np.asarray(inputs["L0_attn_Wv"], dtype=np.float32)
    Wo = np.asarray(inputs["L0_attn_Wo"], dtype=np.float32)

    def permute_qk(W):
        Wr = W.reshape(16, 32, 2, E)
        Wp = np.concatenate([Wr[:, :, 0, :], Wr[:, :, 1, :]], axis=1)
        Wt = Wp.reshape(E, E).T
        return np.ascontiguousarray(Wt.reshape(NEC, 128, E).transpose(1, 0, 2))

    d = {}
    d["wqt"] = permute_qk(Wq)
    d["wkt"] = permute_qk(Wk)
    d["wvt"] = np.ascontiguousarray(Wv.T.reshape(NEC, 128, E).transpose(1, 0, 2))
    d["wot"] = np.ascontiguousarray(Wo.T.reshape(NPAIR, 128, E).transpose(1, 0, 2))

    dd = 64
    inv_freq = 1.0 / (10000.0 ** (np.arange(0, dd, 2, dtype=np.float64) / dd))
    pos = np.arange(T, dtype=np.float64)
    freqs = np.repeat(pos[:, None] * inv_freq[None, :], 2, axis=-1)
    sv = (np.arange(0, dd, 2, dtype=np.float64) + 0.4 * dd) / (1.4 * dd)
    power = (pos - T // 2) / 512.0
    scale = sv[None, :] ** power[:, None]
    scale = np.concatenate([scale, scale], axis=-1)
    perm = np.concatenate([np.arange(0, 64, 2), np.arange(1, 64, 2)])
    swap = (np.arange(64) + 32) % 64
    sign = np.where(np.arange(64) < 32, -1.0, 1.0)
    cos_p = np.cos(freqs)[:, perm].T
    sin_p = np.sin(freqs)[:, perm].T
    sc_p = scale[:, perm].T

    def dup(t):
        return np.ascontiguousarray(np.concatenate([t, t], axis=0), dtype=np.float32)

    d["cq"] = dup(cos_p * sc_p * 0.125)
    d["sq"] = dup(((sin_p * sc_p * 0.125) * sign[:, None])[swap, :])
    d["ck"] = dup(cos_p / sc_p)
    d["sk"] = dup(((sin_p / sc_p) * sign[:, None])[swap, :])

    Wg = np.asarray(inputs["L0_ffn_W_gate"], np.float32)
    d["wg"] = np.ascontiguousarray(
        Wg.reshape(NEC, 128, NEXP).transpose(1, 0, 2).reshape(128, NEC * NEXP))
    d["bgbc"] = np.ascontiguousarray(
        np.broadcast_to(np.asarray(inputs["L0_ffn_b_gate"], np.float32)[None, :],
                        (128, NEXP)))
    W1 = np.asarray(inputs["L0_ffn_W1"], np.float32)
    d["w1"] = np.ascontiguousarray(W1.reshape(NEXP, NEC, 128, E).transpose(0, 2, 1, 3))
    W2 = np.asarray(inputs["L0_ffn_W2"], np.float32)
    d["w2"] = np.ascontiguousarray(W2.reshape(NEXP, NEC, 128, E).transpose(0, 2, 1, 3))
    b1 = np.asarray(inputs["L0_ffn_b1"], np.float32)
    d["b1c"] = np.ascontiguousarray(
        b1.reshape(NEXP, NEC, 128).transpose(2, 0, 1).reshape(128, NEXP * NEC))
    d["b2"] = np.ascontiguousarray(
        np.asarray(inputs["L0_ffn_b2"], np.float32).reshape(1, -1))

    lam = np.concatenate([
        np.asarray(inputs["L0_attn_lambda_q1"], np.float32),
        np.asarray(inputs["L0_attn_lambda_k1"], np.float32),
        np.asarray(inputs["L0_attn_lambda_q2"], np.float32),
        np.asarray(inputs["L0_attn_lambda_k2"], np.float32)])
    d["lam"] = np.ascontiguousarray(np.broadcast_to(lam[None, :], (128, 256)))

    d["cvec"] = np.ascontiguousarray(np.broadcast_to(
        np.array([1e-5, -0.5, -1.0, 1.0], np.float32)[None, :], (128, 4)))
    d["iden"] = np.eye(128, dtype=np.float32)
    d["ones"] = np.ones((1, 512), dtype=np.float32)
    d["inv128"] = np.full((128, 1), 1.0 / 128.0, dtype=np.float32)
    return d


def build(nc):
    taps = {}

    def dram(name, shape, dtype=F32, kind="ExternalInput"):
        return nc.dram_tensor(name, shape, dtype, kind=kind).ap()

    x_h = dram("x", [T, E])
    wqt_h = dram("wqt", [128, NEC, E])
    wkt_h = dram("wkt", [128, NEC, E])
    wvt_h = dram("wvt", [128, NEC, E])
    wot_h = dram("wot", [128, NPAIR, E])
    wg_h = dram("wg", [128, NEC * NEXP])
    bgbc_h = dram("bgbc", [128, NEXP])
    w1_h = dram("w1", [NEXP, 128, NEC, E])
    w2_h = dram("w2", [NEXP, 128, NEC, E])
    b1c_h = dram("b1c", [128, NEXP * NEC])
    b2_h = dram("b2", [1, NEXP * E])
    lam_h = dram("lam", [128, 256])
    cq_h = dram("cq", [128, T])
    sq_h = dram("sq", [128, T])
    ck_h = dram("ck", [128, T])
    sk_h = dram("sk", [128, T])
    iden_h = dram("iden", [128, 128])
    ones_h = dram("ones", [1, 512])
    inv128_h = dram("inv128", [128, 1])
    cvec_h = dram("cvec", [128, 4])
    y_h = dram("y", [T, E], kind="ExternalOutput")
    tap_shapes = {
        "xnT": [128, NEC, T], "wall": [128, NT, NEXP], "kbf": [128, NPAIR, T],
        "V": [128, NT, E], "UT": [128, NPAIR, T], "scales": [128, NPAIR, NT],
        "acc": [128, NT, E], "qbf0": [128, T], "e0b": [128, T], "e1b": [128, T],
        "sums0": [128, NT], "s1a": [128, NT], "s1b": [128, NT],
        "hT": [128, NEC, T], "neglam": [128, 1],
    }
    for tname in _TAPS:
        taps[tname] = dram("tap_" + tname, tap_shapes[tname], F32,
                           kind="ExternalOutput")

    es = ExitStack()
    S = lambda name: SemT(es.enter_context(nc.semaphore(name)))

    sX = S("sX"); sLN = S("sLN"); sSm = S("sSm"); sRs = S("sRs"); sXn = S("sXn")
    sTp = S("sTp"); sEv = S("sEv")
    sG = S("sG"); sGv = S("sGv")
    sKm = S("sKm"); sKc = S("sKc"); sVm = S("sVm"); sVc = S("sVc")
    sWq = S("sWq"); sQm = S("sQm"); sQc = S("sQc"); sQr = S("sQr"); sKr = S("sKr"); sSub = S("sSub")
    sS0 = S("sS0"); sS1a = S("sS1a"); sS1b = S("sS1b")
    sSt0 = S("sSt0"); sSt1 = S("sSt1"); sExp = S("sExp")
    sRho1 = S("sRho1"); sRl = S("sRl"); sRho2 = S("sRho2")
    sDiff = S("sDiff"); sTr = S("sTr"); sTe = S("sTe")
    sPv = S("sPv"); sSq = S("sSq"); sMm = S("sMm"); sMc = S("sMc")
    sScT = S("sScT"); sSm2 = S("sSm2"); sAct = S("sAct"); sRst = S("sRst")
    sWo = S("sWo"); sWm = S("sWm"); sWacc = S("sWacc")
    sW1L = S("sW1L"); sW2L = S("sW2L"); sM1h = S("sM1h"); sRelu = S("sRelu")
    sM2y = S("sM2y"); sAcc = S("sAcc"); sB2L = S("sB2L")
    sXr = S("sXr"); sY = S("sY"); sB0 = S("sB0"); sB0g = S("sB0g"); sTap = S("sTap")
    sXa = [S(f"sXa{i}") for i in range(4)]
    sWqa = [S(f"sWqa{i}") for i in range(2)]
    sWoa = [S(f"sWoa{i}") for i in range(2)]
    sW1a = [S(f"sW1a{i}") for i in range(2)]
    sW2a = [S(f"sW2a{i}") for i in range(2)]
    sB2a = [S(f"sB2a{i}") for i in range(2)]
    sXra = [S(f"sXra{i}") for i in range(4)]
    sYa = [S(f"sYa{i}") for i in range(4)]
    sCv = S("sCv"); sCs = S("sCs")

    def mkW(eng, chain_sem):
        def W(instr, sem=None):
            s = sem if sem is not None else chain_sem
            s.inc(instr)
            eng.wait_ge(s.h, s.n)
            return instr
        return W

    def emit_taps(available):
        live = [(n, taps[n], available[n]) for n in taps if n in available]
        if not live:
            return
        with nc.Block("taps") as blk:
            @blk.gpsimd
            def _(g):
                for name, hap, src in live:
                    sTap.inc(g.dma_start(hap[:], src[:]), 16)
                g.wait_ge(sTap.h, sTap.n)

    sb = lambda *a, **k: es.enter_context(nc.sbuf_tensor(*a, **k))

    xnT = sb("xnT", [128, NEC, T], BF16)
    wall = sb("wall", [128, NT, NEXP], F32)
    scales = sb("scales", [128, NPAIR, NT], F32)
    neglam = sb("neglam", [128, 1], F32)
    iden_bf = sb("iden_bf", [128, 128], BF16)
    ones_bf = sb("ones_bf", [1, 512], BF16)
    inv128_bf = sb("inv128_bf", [128, 1], BF16)
    b1c_sb = sb("b1c_sb", [128, NEXP * NEC], F32)
    cvec_sb = sb("cvec_sb", [128, 4], F32)

    arena = es.enter_context(nc.psum_tensor("arena", [128, 3584], F32))
    psT = es.enter_context(nc.psum_tensor("psT", [128, 8, 128], BF16))

    esA = ExitStack()
    sbA = lambda *a, **k: esA.enter_context(nc.sbuf_tensor(*a, **k))
    kbf = sbA("kbf", [128, NPAIR, T], BF16)
    kbf_sub = sbA("kbf_sub", [128, NPAIR, 256], BF16)
    Vsb = sbA("V", [128, NT, E], BF16)
    cq_bf = sbA("cq_bf", [128, T], BF16)
    sq_bf = sbA("sq_bf", [128, T], BF16)
    qraw = sbA("qraw", [128, T], BF16)
    t1b = sbA("t1b", [128, T], BF16)
    usw = sbA("usw", [128, T], BF16)

    esA1 = ExitStack()
    sbA1 = lambda *a, **k: esA1.enter_context(nc.sbuf_tensor(*a, **k))
    ck_bf = sbA1("ck_bf", [128, T], BF16)
    sk_bf = sbA1("sk_bf", [128, T], BF16)
    wkt_bf = sbA1("wkt_bf", [128, NEC, E], BF16)
    wvt_bf = sbA1("wvt_bf", [128, NEC, E], BF16)
    xt = [sbA1(f"xt{i}", [128, E], F32) for i in range(4)]
    xnb = [sbA1(f"xnb{i}", [128, E], BF16) for i in range(2)]
    ln_s1 = sbA1("ln_s1", [128, 1], F32)
    ln_s2 = sbA1("ln_s2", [128, 1], F32)
    ln_mean = sbA1("ln_mean", [128, 1], F32)
    ln_m2 = sbA1("ln_m2", [128, 1], F32)
    ln_var = sbA1("ln_var", [128, 1], F32)
    ln_ln = sbA1("ln_ln", [128, 1], F32)
    ln_rs = sbA1("ln_rs", [128, 1], F32)
    gsb = sbA1("gsb", [128, NEXP], F32)
    top8 = sbA1("top8", [128, NEXP], F32)
    gesb = sbA1("gesb", [128, NEXP], F32)
    wg_bf = sbA1("wg_bf", [128, NEC * NEXP], BF16)
    bgbc_sb = sbA1("bgbc_sb", [128, NEXP], F32)
    lam_sb = sbA1("lam_sb", [128, 256], F32)
    lprod = sbA1("lprod", [128, 64], F32)
    ld12 = sbA1("ld12", [128, 2], F32)
    le12 = sbA1("le12", [128, 2], F32)

    with nc.Block("b0") as blk:
        @blk.sync
        def _(sy):
            sB0.inc(sy.dma_start(bgbc_sb[:], bgbc_h[:]), 16)
            sB0.inc(sy.dma_start(lam_sb[:], lam_h[:]), 16)
            sB0.inc(sy.dma_start(b1c_sb[:], b1c_h[:]), 16)
            sB0.inc(sy.dma_start(cvec_sb[:], cvec_h[:]), 16)
            sy.wait_ge(sB0.h, sB0.n)

        @blk.gpsimd
        def _(g):
            for dst, src in ((iden_bf, iden_h), (ones_bf, ones_h),
                             (inv128_bf, inv128_h), (wg_bf, wg_h)):
                sB0g.inc(g.dma_start(dst[:], src[:]), 16)
            g.wait_ge(sB0g.h, sB0g.n)

        @blk.vector
        def _(v):
            W = mkW(v, sCv)
            v.wait_ge(sB0.h, 64)
            W(v.tensor_mul(lprod[:], lam_sb[:, 0:64], lam_sb[:, 64:128]))
            W(v.reduce_sum(ld12[:, 0:1], lprod[:], axis=AX))
            W(v.tensor_mul(lprod[:], lam_sb[:, 128:192], lam_sb[:, 192:256]))
            W(v.reduce_sum(ld12[:, 1:2], lprod[:], axis=AX), sGv)

        @blk.scalar
        def _(s):
            s.wait_ge(sGv.h, 1)
            Ws = mkW(s, sCs)
            Ws(s.activation(le12[:], ld12[:], AF.Exp), sRs)

        @blk.vector
        def _(v):
            W = mkW(v, sCv)
            v.wait_ge(sRs.h, 1)
            W(v.tensor_sub(neglam[:], le12[:, 1:2], le12[:, 0:1]), sGv)
            W(v.tensor_scalar(neglam[:], neglam[:], LI, None, OP.subtract))

    with nc.Block("b0b") as blk:
        @blk.gpsimd
        def _(g):
            for dst, src in ((cq_bf, cq_h), (sq_bf, sq_h), (ck_bf, ck_h),
                             (sk_bf, sk_h), (wkt_bf, wkt_h), (wvt_bf, wvt_h)):
                sB0g.inc(g.dma_start(dst[:], src[:]), 16)
            g.wait_ge(sB0g.h, sB0g.n)

        @blk.sync
        def _(sy):
            for m in range(4):
                sXa[m].inc(sy.dma_start(xt[m][:], x_h[m * 128:(m + 1) * 128, :]), 16)

    if _STAGE < 1:
        emit_taps({})
        esA1.close(); esA.close(); es.close()
        return taps

    with nc.Block("p1") as blk:
        @blk.sync
        def _(sy):
            for m in range(4, NT):
                sy.wait_ge(sXn.h, m - 3)
                sXa[m % 4].inc(sy.dma_start(xt[m % 4][:],
                                            x_h[m * 128:(m + 1) * 128, :]), 16)

        @blk.vector
        def _(v):
            W = mkW(v, sCv)
            for m in range(NT):
                v.wait_ge(sXa[m % 4].h, 16 * (m // 4 + 1))
                if m >= 2:
                    v.wait_ge(sTp.h, m - 1)
                W(v.reduce_sum(ln_s1[:], xt[m % 4][:], axis=AX))
                W(v.scalar_tensor_tensor(
                    xnb[m % 2][:], xt[m % 4][:], 1.0, xt[m % 4][:],
                    OP.bypass, OP.mult, accum_out=ln_s2[:]), sLN)
                W(v.tensor_scalar(ln_mean[:], ln_s1[:], 1.0 / E, None, OP.mult))
                W(v.tensor_mul(ln_m2[:], ln_mean[:], ln_mean[:]))
                W(v.scalar_tensor_tensor(
                    ln_var[:], ln_s2[:], 1.0 / E, ln_m2[:], OP.mult, OP.subtract),
                  sSm)
                v.wait_ge(sRs.h, 1 + m + 1)
                W(v.tensor_scalar(xnb[m % 2][:], xt[m % 4][:], ln_mean[:],
                                  ln_rs[:], OP.subtract, OP.mult), sXn)
                if m >= 1:
                    v.wait_ge(sTp.h, m)
                    W(v.tensor_copy(xnT[:, :, (m - 1) * 128:m * 128], psT[:]), sEv)

        @blk.scalar
        def _(s):
            Ws = mkW(s, sCs)
            for m in range(NT):
                s.wait_ge(sSm.h, m + 1)
                Ws(s.activation(ln_ln[:], ln_var[:], AF.Ln, bias=cvec_sb[:, 0:1]),
                   sAct)
                Ws(s.activation(ln_rs[:], ln_ln[:], AF.Exp, scale=cvec_sb[:, 1:2]),
                   sRs)

        @blk.tensor
        def _(t):
            for m in range(NT):
                t.wait_ge(sXn.h, m + 1)
                if m >= 1:
                    t.wait_ge(sEv.h, m)
                for k in range(NEC):
                    ins = t.transpose(psT[:, k, :],
                                      xnb[m % 2][:, k * 128:(k + 1) * 128], iden_bf[:])
                sTp.inc(ins)

        @blk.vector
        def _(v):
            v.wait_ge(sTp.h, NT)
            sEv.inc(v.tensor_copy(xnT[:, :, (NT - 1) * 128:NT * 128], psT[:]))
            v.wait_ge(sEv.h, sEv.n)

    if _STAGE < 2:
        emit_taps(dict(xnT=xnT, neglam=neglam))
        esA1.close(); esA.close(); es.close()
        return taps

    gv0 = sGv.n
    with nc.Block("p1b") as blk:
        @blk.tensor
        def _(t):
            for m in range(NT):
                slot = m % 2
                psG = arena[:, slot * 512:slot * 512 + 8]
                if m >= 2:
                    t.wait_ge(sGv.h, gv0 + 7 * (m - 1))
                for dc in range(NEC):
                    ins = t.matmul(psG, xnT[:, dc, m * 128:(m + 1) * 128],
                                   wg_bf[:, dc * 8:(dc + 1) * 8],
                                   start=(dc == 0), stop=(dc == NEC - 1))
                sG.inc(ins)

        @blk.vector
        def _(v):
            W = mkW(v, sCv)
            for m in range(NT):
                slot = m % 2
                psG = arena[:, slot * 512:slot * 512 + 8]
                v.wait_ge(sG.h, m + 1)
                W(v.tensor_add(gsb[:], psG, bgbc_sb[:]), sGv)
                W(v.tensor_scalar(gesb[:], gsb[:], 1.0, None, OP.mult, OP.max,
                                  accum_out=top8[:, 0:1]), sGv)
                W(v.tensor_scalar(gesb[:], gsb[:], top8[:, 0:1], None, OP.is_ge),
                  sGv)
                W(v.scalar_tensor_tensor(gesb[:], gesb[:], -1e30, gsb[:],
                                         OP.mult, OP.add), sGv)
                W(v.tensor_scalar(gesb[:], gesb[:], 1.0, None, OP.mult, OP.max,
                                  accum_out=top8[:, 1:2]), sGv)
                W(v.tensor_scalar(gesb[:], gsb[:], top8[:, 1:2], None, OP.is_ge),
                  sGv)
                W(v.tensor_mul(wall[:, m, :], gsb[:], gesb[:]), sGv)

    if _STAGE < 3:
        emit_taps(dict(xnT=xnT, wall=wall, neglam=neglam))
        esA1.close(); esA.close(); es.close()
        return taps

    with nc.Block("p2") as blk:
        @blk.tensor
        def _(t):
            psK = arena[:, 0:2048]
            for p in range(NPAIR):
                if p >= 1:
                    t.wait_ge(sKc.h, p)
                for tc in range(4):
                    for ec in range(NEC):
                        ins = t.matmul(psK[:, tc * 512:(tc + 1) * 512],
                                       wkt_bf[:, ec, p * 128:(p + 1) * 128],
                                       xnT[:, ec, tc * 512:(tc + 1) * 512],
                                       start=(ec == 0), stop=(ec == NEC - 1))
                sKm.inc(ins)
            for m in range(NT):
                slot = m % 2
                psV = arena[:, 2048:3072] if slot == 0 else arena[:, 0:1024]
                if m < 2:
                    t.wait_ge(sKc.h, NPAIR)
                else:
                    t.wait_ge(sVc.h, m - 1)
                for jc in range(2):
                    for ec in range(NEC):
                        ins = t.matmul(psV[:, jc * 512:(jc + 1) * 512],
                                       xnT[:, ec, m * 128:(m + 1) * 128],
                                       wvt_bf[:, ec, jc * 512:(jc + 1) * 512],
                                       start=(ec == 0), stop=(ec == NEC - 1))
                sVm.inc(ins)

        @blk.vector
        def _(v):
            W = mkW(v, sCv)
            psK = arena[:, 0:2048]
            for p in range(NPAIR):
                v.wait_ge(sKm.h, p + 1)
                W(v.tensor_copy(qraw[:], psK), sKc)
                W(v.tensor_mul(t1b[:], qraw[:], ck_bf[:]))
                W(v.tensor_mul(usw[0:32, :], qraw[32:64, :], sk_bf[32:64, :]))
                W(v.tensor_mul(usw[32:64, :], qraw[0:32, :], sk_bf[0:32, :]))
                W(v.tensor_mul(usw[64:96, :], qraw[96:128, :], sk_bf[96:128, :]))
                W(v.tensor_mul(usw[96:128, :], qraw[64:96, :], sk_bf[64:96, :]))
                W(v.tensor_add(kbf[:, p, :], t1b[:], usw[:]), sKr)
                W(v.tensor_copy(kbf_sub[:, p, :], kbf[:, p, 0:2048:8]))
            for m in range(NT):
                slot = m % 2
                psV = arena[:, 2048:3072] if slot == 0 else arena[:, 0:1024]
                v.wait_ge(sVm.h, m + 1)
                W(v.tensor_copy(Vsb[:, m, :], psV), sVc)

    if _STAGE < 4:
        emit_taps(dict(xnT=xnT, wall=wall, kbf=kbf, V=Vsb, neglam=neglam))
        esA1.close(); esA.close(); es.close()
        return taps

    esA1.close()
    esU = ExitStack()
    UT = esU.enter_context(nc.sbuf_tensor("UT", [128, NPAIR, T], BF16, side="right"))

    esA2 = ExitStack()
    sbA2 = lambda *a, **k: esA2.enter_context(nc.sbuf_tensor(*a, **k))
    qbf = [sbA2(f"qbf{i}", [128, T], BF16) for i in range(2)]
    e0b = [sbA2(f"e0b{i}", [128, T], BF16) for i in range(2)]
    e1b = [sbA2(f"e1b{i}", [128, T], BF16) for i in range(2)]
    diffb = [sbA2(f"diffb{i}", [128, T], BF16) for i in range(2)]
    diffT = [sbA2(f"diffT{i}", [128, NT, 128], BF16) for i in range(2)]
    sums0 = sbA2("sums0", [128, NT], F32)
    nb_all = sbA2("nb_all", [128, 2 * NT], F32)
    csums = [sbA2(f"csums{i}", [128, 4], F32) for i in range(2)]
    sjunk = sbA2("sjunk", [128, 256], BF16)
    s01 = sbA2("s01", [128, 2], F32)
    s1t = sbA2("s1t", [128, 1], F32)
    ab = sbA2("ab", [128, NT], F32)
    rhot = sbA2("rhot", [128, 1], F32)
    nrho = [sbA2(f"nrho{i}", [128, 1], F32) for i in range(2)]
    sqb = sbA2("sqb", [128, 128], BF16)
    s0sq = sbA2("s0sq", [128, NT], F32)
    innerb = sbA2("innerb", [128, NT], F32)
    lnb = sbA2("lnb", [128, NT], F32)
    rstb = sbA2("rstb", [128, NT], F32)
    wqb = [sbA2(f"wqb{i}", [128, NEC, 128], BF16) for i in range(2)]

    psQ = arena[:, 0:2048]
    SLOT = [arena[:, 0:1024], arena[:, 1024:2048], arena[:, 2048:3072]]
    psU = arena[:, 3072:3200]
    psMsq = arena[0:1, 3200:3328]
    psSc = arena[:, 3328:3344]

    def p3_items():
        items = [("sub", 0), ("sub", 1)]
        for m in range(NT):
            if m + 2 < NT:
                items.append(("sub", m + 2))
            for c in range(4):
                items.append(("main", m, c))
        return items

    def p3_plan():
        plans = []
        slot_state = [None, None, None]
        n_sub = 0
        n_main = 0
        for p in range(NPAIR):
            plan = []
            for j, it in enumerate(p3_items()):
                k = len(plans) * 80 + j
                slot = k % 3
                waits = []
                prev = slot_state[slot]
                if prev is not None:
                    waits.append(prev)
                if j < 3:
                    waits.append(("sQc", p + 1))
                if it[0] == "sub":
                    n_sub += 1
                    slot_state[slot] = ("sSt0", 2 * n_sub)
                    plan.append((it, slot, waits, n_sub, None))
                else:
                    n_main += 1
                    slot_state[slot] = ("sExp", n_main)
                    plan.append((it, slot, waits, None, n_main))
            plans.append(plan)
        return plans

    P3PLANS = p3_plan()

    def tail_pieces(t, p, m, piece):
        mm = NT * p + m
        if piece == 0:
            t.wait_ge(sDiff.h, mm + 1)
            if mm >= 1:
                t.wait_ge(sTe.h, 2 * mm)
            for sc in range(8):
                ins = t.transpose(psT[:, sc, :],
                                  diffb[m % 2][:, sc * 128:(sc + 1) * 128],
                                  iden_bf[:])
            sTr.inc(ins)
        elif piece == 1:
            t.wait_ge(sTe.h, 2 * mm + 1)
            for sc in range(8):
                ins = t.transpose(psT[:, sc, :],
                                  diffb[m % 2][:, (8 + sc) * 128:(9 + sc) * 128],
                                  iden_bf[:])
            sTr.inc(ins)
        elif piece == 2:
            t.wait_ge(sTe.h, 2 * mm + 1)
            if m == 0 and p >= 1:
                t.wait_ge(sSm2.h, p)
            if mm >= 1:
                t.wait_ge(sSq.h, mm)
            for sc in range(8):
                t.matmul(psU[:], Vsb[:, sc, p * 128:(p + 1) * 128],
                         diffT[m % 2][:, sc, :],
                         start=(sc == 0), stop=False)
        elif piece == 3:
            t.wait_ge(sTe.h, 2 * mm + 2)
            for sc in range(8, NT):
                ins = t.matmul(psU[:], Vsb[:, sc, p * 128:(p + 1) * 128],
                               diffT[m % 2][:, sc, :],
                               start=False, stop=(sc == NT - 1))
            sPv.inc(ins)
        else:
            t.wait_ge(sSq.h, mm + 1)
            sMm.inc(t.matmul(psSc[:, m:m + 1], sqb[:], inv128_bf[:],
                             start=True, stop=True))

    with nc.Block("p3") as blk:
        @blk.gpsimd
        def _(g):
            sWqa[0].inc(g.dma_start(wqb[0][:], wqt_h[:, :, 0:128]), 16)
            for p in range(1, NPAIR):
                if p >= 2:
                    g.wait_ge(sQm.h, p - 1)
                sWqa[p % 2].inc(g.dma_start(wqb[p % 2][:],
                                            wqt_h[:, :, p * 128:(p + 1) * 128]), 16)

        @blk.tensor
        def _(t):
            semmap = {"sQc": sQc, "sSt0": sSt0, "sExp": sExp}

            def emit_item(item):
                (it, slot, waits, nsub, nmain) = item
                for (semname, val) in waits:
                    t.wait_ge(semmap[semname].h, val)
                if it[0] == "sub":
                    m = it[1]
                    if m < 2:
                        t.wait_ge(sQr.h, p + 1)
                    for h in range(2):
                        ins = t.matmul(
                            SLOT[slot][:, h * 512:h * 512 + 256],
                            qbf[p % 2][h * 64:(h + 1) * 64,
                                       m * 128:(m + 1) * 128],
                            kbf_sub[h * 64:(h + 1) * 64, p, :],
                            start=True, stop=True, tile_position=(64 * h, 0))
                    sSub.inc(ins)
                else:
                    m, c = it[1], it[2]
                    h = c // 2
                    half = c % 2
                    for sc in range(2):
                        col = half * 1024 + sc * 512
                        ins = t.matmul(
                            SLOT[slot][:, sc * 512:(sc + 1) * 512],
                            qbf[p % 2][h * 64:(h + 1) * 64,
                                       m * 128:(m + 1) * 128],
                            kbf[h * 64:(h + 1) * 64, p, col:col + 512],
                            start=True, stop=True, tile_position=(64 * h, 0))
                    sS0.inc(ins)

            for p in range(NPAIR):
                plan = P3PLANS[p]
                t.wait_ge(sWqa[p % 2].h, 16 * (p // 2 + 1))
                if p >= 1:
                    t.wait_ge(sExp.h, 64 * p)
                    t.wait_ge(sSt0.h, 32 * p)
                for tc in range(4):
                    for ec in range(NEC):
                        ins = t.matmul(psQ[:, tc * 512:(tc + 1) * 512],
                                       wqb[p % 2][:, ec, :],
                                       xnT[:, ec, tc * 512:(tc + 1) * 512],
                                       start=(ec == 0), stop=(ec == NEC - 1))
                sQm.inc(ins)
                by_tile = [[] for _ in range(NT)]
                head = []
                for item in plan:
                    it = item[0]
                    if it[0] == "sub" and it[1] < 2:
                        head.append(item)
                    elif it[0] == "sub":
                        by_tile[it[1] - 2].append(item)
                    else:
                        by_tile[it[1]].append(item)
                for item in head:
                    emit_item(item)
                for m in range(NT):
                    items = by_tile[m]
                    mains = items[-4:]
                    subs = items[:-4]
                    for item in subs:
                        emit_item(item)
                    emit_item(mains[0])
                    if m >= 1:
                        tail_pieces(t, p, m - 1, 0)
                    emit_item(mains[1])
                    if m >= 1:
                        tail_pieces(t, p, m - 1, 1)
                    emit_item(mains[2])
                    if m >= 1:
                        tail_pieces(t, p, m - 1, 2)
                    emit_item(mains[3])
                    if m >= 1:
                        tail_pieces(t, p, m - 1, 3)
                    if m >= 2:
                        tail_pieces(t, p, m - 2, 4)
                tail_pieces(t, p, NT - 2, 4)
                for piece in range(5):
                    tail_pieces(t, p, NT - 1, piece)

        @blk.vector
        def _(v):
            W = mkW(v, sCv)
            nsub_g = 0
            for p in range(NPAIR):
                plan = P3PLANS[p]
                v.wait_ge(sQm.h, p + 1)
                W(v.tensor_copy(qraw[:], psQ), sQc)
                W(v.tensor_mul(t1b[:], qraw[:], cq_bf[:]))
                W(v.tensor_mul(usw[0:32, :], qraw[32:64, :], sq_bf[32:64, :]))
                W(v.tensor_mul(usw[32:64, :], qraw[0:32, :], sq_bf[0:32, :]))
                W(v.tensor_mul(usw[64:96, :], qraw[96:128, :], sq_bf[96:128, :]))
                W(v.tensor_mul(usw[96:128, :], qraw[64:96, :], sq_bf[64:96, :]))
                W(v.tensor_add(qbf[p % 2][:], t1b[:], usw[:]), sQr)
                subslot = {it[1]: slot for (it, slot, w_, ns, nm) in plan
                           if it[0] == "sub"}
                for m in range(NT):
                    mm = NT * p + m
                    for msub in ([0, 1] if m == 0 else []) + \
                            ([m + 2] if m + 2 < NT else []):
                        nsub_g += 1
                        v.wait_ge(sSub.h, nsub_g)
                        if p >= 1:
                            v.wait_ge(sExp.h, 64 * (p - 1) + 4 * (msub + 1))
                        sl = subslot[msub]
                        for h in range(2):
                            W(v.tensor_scalar(
                                sjunk[:], SLOT[sl][:, h * 512:h * 512 + 256],
                                -1.0, None, OP.mult, OP.min,
                                accum_out=nb_all[:, h * NT + msub:
                                                 h * NT + msub + 1]), sSt0)
                    v.wait_ge(sExp.h, 64 * p + 4 * m + 4)
                    W(v.tensor_tensor(s01[:], csums[m % 2][:, 0:4:2],
                                      csums[m % 2][:, 1:4:2], OP.add))
                    W(v.tensor_copy(sums0[:, m:m + 1], s01[:, 0:1]))
                    W(v.tensor_scalar(s1t[:], s01[:, 1:2], 2.0 ** -40, None,
                                      OP.mult))
                    W(v.reciprocal(rhot[:], s1t[:]))
                    W(v.tensor_mul(rhot[:], rhot[:], s01[:, 0:1]))
                    if mm >= 2:
                        v.wait_ge(sTr.h, 2 * (mm - 2) + 2)
                    W(v.tensor_scalar(nrho[m % 2][:], rhot[:], neglam[:],
                                      2.0 ** -40, OP.mult, OP.mult))
                    W(v.tensor_scalar(t1b[:], e1b[m % 2][:], nrho[m % 2][:],
                                      None, OP.mult))
                    W(v.tensor_add(diffb[m % 2][:], t1b[:], e0b[m % 2][:]), sDiff)
                    v.wait_ge(sTr.h, 2 * mm + 1)
                    W(v.tensor_copy(diffT[m % 2][:, 0:8, :], psT[:]), sTe)
                    v.wait_ge(sTr.h, 2 * mm + 2)
                    W(v.tensor_copy(diffT[m % 2][:, 8:16, :], psT[:]), sTe)
                v.wait_ge(sMm.h, NT * (p + 1))
                W(v.tensor_scalar(s0sq[:], sums0[:], 2.0 ** -40, None, OP.mult))
                W(v.reciprocal(ab[:], s0sq[:]))
                W(v.tensor_mul(s0sq[:], ab[:], ab[:]))
                W(v.tensor_tensor(innerb[:], psSc[:], s0sq[:], OP.mult))
                W(v.tensor_scalar(innerb[:], innerb[:], 2.0 ** -30, None, OP.mult),
                  sSm2)
                v.wait_ge(sRst.h, p + 1)
                W(v.tensor_mul(innerb[:], ab[:], rstb[:]))
                W(v.tensor_scalar(scales[:, p, :], innerb[:],
                                  (1.0 - LI) * 2.0 ** -40, None, OP.mult))

        @blk.scalar
        def _(s):
            Ws = mkW(s, sCs)
            for p in range(NPAIR):
                plan = P3PLANS[p]
                for (it, slot, waits, nsub, nmain) in plan:
                    if it[0] != "main":
                        continue
                    m, c = it[1], it[2]
                    mm = NT * p + m
                    h = c // 2
                    half = c % 2
                    s.wait_ge(sS0.h, nmain)
                    s.wait_ge(sSt0.h, 2 * (NT * p + m + 1))
                    if mm >= 2 and c == 0:
                        s.wait_ge(sDiff.h, mm - 1)
                    ebuf = e0b[m % 2] if h == 0 else e1b[m % 2]
                    Ws(s.activation(ebuf[:, half * 1024:(half + 1) * 1024],
                                    SLOT[slot][:], AF.Exp,
                                    bias=nb_all[:, h * NT + m:h * NT + m + 1],
                                    accum_out=csums[m % 2][:, c:c + 1]), sExp)
                    if c == 3 and m >= 1:
                        s.wait_ge(sPv.h, mm)
                        Ws(s.activation(UT[:, p, (m - 1) * 128:m * 128], psU[:],
                                        AF.Identity))
                        if mm >= 2:
                            s.wait_ge(sMm.h, mm - 1)
                        Ws(s.activation(sqb[:], psU[:], AF.Square,
                                        scale=2.0 ** -25), sSq)
                s.wait_ge(sPv.h, NT * (p + 1))
                Ws(s.activation(UT[:, p, (NT - 1) * 128:NT * 128], psU[:],
                                AF.Identity))
                s.wait_ge(sMm.h, NT * (p + 1) - 1)
                Ws(s.activation(sqb[:], psU[:], AF.Square, scale=2.0 ** -25), sSq)
                s.wait_ge(sSm2.h, p + 1)
                Ws(s.activation(lnb[:], innerb[:], AF.Ln, bias=cvec_sb[:, 0:1]),
                   sAct)
                Ws(s.activation(rstb[:], lnb[:], AF.Exp, scale=cvec_sb[:, 1:2]),
                   sRst)

    if _STAGE < 5:
        emit_taps(dict(xnT=xnT, wall=wall, kbf=kbf, V=Vsb, UT=UT, scales=scales,
                       sums0=sums0, qbf0=qbf[0],
                       e0b=e0b[0], e1b=e1b[0], neglam=neglam))
        esA2.close(); esU.close(); esA.close(); es.close()
        return taps

    esA2.close(); esA.close()
    esC = ExitStack()
    sbC = lambda *a, **k: esC.enter_context(nc.sbuf_tensor(*a, **k))
    acc = sbC("acc", [128, NT, E], F32)
    wob = [sbC(f"wob{i}", [128, E], BF16) for i in range(2)]

    with nc.Block("p4") as blk:
        @blk.gpsimd
        def _(g):
            sWoa[0].inc(g.dma_start(wob[0][:], wot_h[:, 0, :]), 16)
            for p in range(1, NPAIR):
                if p >= 2:
                    g.wait_ge(sWm.h, 32 * (p - 1))
                sWoa[p % 2].inc(g.dma_start(wob[p % 2][:], wot_h[:, p, :]), 16)

        @blk.tensor
        def _(t):
            for p in range(NPAIR):
                t.wait_ge(sWoa[p % 2].h, 16 * (p // 2 + 1))
                for m in range(NT):
                    for oc in range(2):
                        i = p * 32 + m * 2 + oc
                        slot = (m * 2 + oc) % 4
                        psW = arena[:, slot * 512:(slot + 1) * 512]
                        if i >= 4:
                            t.wait_ge(sWacc.h, i - 3)
                        sWm.inc(t.matmul(psW, UT[:, p, m * 128:(m + 1) * 128],
                                         wob[p % 2][:, oc * 512:(oc + 1) * 512],
                                         start=True, stop=True))

        @blk.vector
        def _(v):
            W = mkW(v, sCv)
            for p in range(NPAIR):
                for m in range(NT):
                    for oc in range(2):
                        i = p * 32 + m * 2 + oc
                        slot = (m * 2 + oc) % 4
                        psW = arena[:, slot * 512:(slot + 1) * 512]
                        v.wait_ge(sWm.h, i + 1)
                        dst = acc[:, m, oc * 512:(oc + 1) * 512]
                        if p == 0:
                            ins = v.tensor_scalar(dst, psW, scales[:, 0, m:m + 1],
                                                  None, OP.mult)
                        else:
                            ins = v.scalar_tensor_tensor(
                                dst, psW, scales[:, p, m:m + 1], dst,
                                OP.mult, OP.add)
                        W(ins, sWacc)

    if _STAGE < 6:
        emit_taps(dict(acc=acc, UT=UT, scales=scales, wall=wall, xnT=xnT))
        esU.close(); esC.close(); es.close()
        return taps

    esU.close()
    esD = ExitStack()
    sbD = lambda *a, **k: esD.enter_context(nc.sbuf_tensor(*a, **k))
    w1b = [sbD(f"w1b{i}", [128, NEC, E], BF16) for i in range(2)]
    w2b = [sbD(f"w2b{i}", [128, NEC, E], BF16) for i in range(2)]
    hT = sbD("hT", [128, NEC, T], BF16)
    b2e = [sbD(f"b2e{i}", [1, E], BF16) for i in range(2)]

    with nc.Block("p5") as blk:
        @blk.gpsimd
        def _(g):
            sW1a[0].inc(g.dma_start(w1b[0][:], w1_h[0]), 16)
            sB2a[0].inc(g.dma_start(b2e[0][:], b2_h[0:1, 0:E]), 16)
            sW2a[0].inc(g.dma_start(w2b[0][:], w2_h[0]), 16)
            for e in range(1, NEXP):
                g.wait_ge(sM1h.h, 16 * e)
                sW1a[e % 2].inc(g.dma_start(w1b[e % 2][:], w1_h[e]), 16)
                g.wait_ge(sM2y.h, 32 * e)
                sB2a[e % 2].inc(g.dma_start(b2e[e % 2][:],
                                            b2_h[0:1, e * E:(e + 1) * E]), 16)
                sW2a[e % 2].inc(g.dma_start(w2b[e % 2][:], w2_h[e]), 16)

        @blk.tensor
        def _(t):
            for e in range(NEXP):
                t.wait_ge(sW1a[e % 2].h, 16 * (e // 2 + 1))
                for hd in range(NEC):
                    for th in range(2):
                        i = e * 16 + hd * 2 + th
                        slot = (hd * 2 + th) % 2
                        psH = arena[:, slot * 1024:(slot + 1) * 1024]
                        if i >= 2:
                            t.wait_ge(sRelu.h, i - 1)
                        for half in range(2):
                            col = th * 1024 + half * 512
                            for dc in range(NEC):
                                ins = t.matmul(
                                    psH[:, half * 512:(half + 1) * 512],
                                    w1b[e % 2][:, dc, hd * 128:(hd + 1) * 128],
                                    xnT[:, dc, col:col + 512],
                                    start=(dc == 0), stop=(dc == NEC - 1))
                        sM1h.inc(ins)
                t.wait_ge(sRelu.h, 16 * (e + 1))
                t.wait_ge(sW2a[e % 2].h, 16 * (e // 2 + 1))
                t.wait_ge(sB2a[e % 2].h, 16 * (e // 2 + 1))
                for m in range(NT):
                    for oc in range(2):
                        i = e * 32 + m * 2 + oc
                        slot = (m * 2 + oc) % 2
                        psY = arena[:, 2048 + slot * 512:2048 + (slot + 1) * 512]
                        if i >= 2:
                            t.wait_ge(sAcc.h, i - 1)
                        for hc in range(NEC):
                            t.matmul(psY, hT[:, hc, m * 128:(m + 1) * 128],
                                     w2b[e % 2][:, hc, oc * 512:(oc + 1) * 512],
                                     start=(hc == 0), stop=False)
                        ins = t.matmul(psY, ones_bf[0:1, 0:128],
                                       b2e[e % 2][0:1, oc * 512:(oc + 1) * 512],
                                       start=False, stop=True)
                        sM2y.inc(ins)

        @blk.scalar
        def _(s):
            Ws = mkW(s, sCs)
            for e in range(NEXP):
                for hd in range(NEC):
                    for th in range(2):
                        i = e * 16 + hd * 2 + th
                        slot = (hd * 2 + th) % 2
                        psH = arena[:, slot * 1024:(slot + 1) * 1024]
                        s.wait_ge(sM1h.h, i + 1)
                        if e >= 1:
                            s.wait_ge(sM2y.h, 32 * e)
                        Ws(s.activation(
                            hT[:, hd, th * 1024:(th + 1) * 1024], psH, AF.Relu,
                            bias=b1c_sb[:, e * NEC + hd:e * NEC + hd + 1]), sRelu)

        @blk.vector
        def _(v):
            W = mkW(v, sCv)
            for e in range(NEXP):
                for m in range(NT):
                    for oc in range(2):
                        i = e * 32 + m * 2 + oc
                        slot = (m * 2 + oc) % 2
                        psY = arena[:, 2048 + slot * 512:2048 + (slot + 1) * 512]
                        v.wait_ge(sM2y.h, i + 1)
                        dst = acc[:, m, oc * 512:(oc + 1) * 512]
                        W(v.scalar_tensor_tensor(
                            dst, psY, wall[:, m, e:e + 1], dst, OP.mult, OP.add),
                          sAcc)

    if _STAGE < 7:
        emit_taps(dict(acc=acc, hT=hT, wall=wall))
        esD.close(); esC.close(); es.close()
        return taps

    esD.close()
    esE = ExitStack()
    xr = [esE.enter_context(nc.sbuf_tensor(f"xr{i}", [128, E], F32))
          for i in range(4)]

    with nc.Block("p6") as blk:
        @blk.sync
        def _(sy):
            for m in range(NT):
                if m >= 4:
                    sy.wait_ge(sYa[m % 4].h, 16 * (m // 4))
                sXra[m % 4].inc(sy.dma_start(xr[m % 4][:],
                                             x_h[m * 128:(m + 1) * 128, :]), 16)

        @blk.vector
        def _(v):
            W = mkW(v, sCv)
            for m in range(NT):
                v.wait_ge(sXra[m % 4].h, 16 * (m // 4 + 1))
                W(v.tensor_add(xr[m % 4][:], xr[m % 4][:], acc[:, m, :]), sXn)

        @blk.scalar
        def _(s):
            for m in range(NT):
                s.wait_ge(sXn.h, NT + m + 1)
                sYa[m % 4].inc(s.dma_start(y_h[m * 128:(m + 1) * 128, :],
                                           xr[m % 4][:]), 16)
            for i in range(4):
                s.wait_ge(sYa[i].h, sYa[i].n)

    emit_taps(dict(acc=acc))
    esE.close(); esC.close(); es.close()
    return taps


def kernel(**inputs):
    x_full = np.asarray(inputs["L0_attn_x"], dtype=np.float32)
    d = _host_prep(inputs)

    nc = bass.Bass("TRN2", target_bir_lowering=False)
    build(nc)

    in_maps = []
    for c in range(8):
        m = dict(d)
        m["x"] = np.ascontiguousarray(x_full[c])
        in_maps.append(m)

    res = run_bass_kernel_spmd(nc, in_maps, core_ids=list(range(8)))
    out = np.stack([res.results[c]["y"] for c in range(8)], axis=0)
    if _TAPS:
        kernel.last_taps = [{t: res.results[c]["tap_" + t] for t in _TAPS}
                            for c in range(8)]
    return out


# version 14
# speedup vs baseline: 1.2427x; 1.2427x over previous
import math
from contextlib import ExitStack

import numpy as np

import concourse.bass as bass
import concourse.mybir as mybir
from concourse.bass_utils import run_bass_kernel_spmd

F32 = mybir.dt.float32
BF16 = mybir.dt.bfloat16
AF = mybir.ActivationFunctionType
OP = mybir.AluOpType
AX = mybir.AxisListType.X

T = 2048
E = 1024
NT = 16
NEC = 8
NPAIR = 8
NEXP = 8
LI = 0.8 - 0.6 * math.exp(-0.3)

_STAGE = 99
_TAPS = []


class SemT:
    def __init__(self, h):
        self.h = h
        self.n = 0

    def inc(self, instr, k=1):
        instr.then_inc(self.h, k)
        self.n += k
        return instr


def _host_prep(inputs):
    Wq = np.asarray(inputs["L0_attn_Wq"], dtype=np.float32)
    Wk = np.asarray(inputs["L0_attn_Wk"], dtype=np.float32)
    Wv = np.asarray(inputs["L0_attn_Wv"], dtype=np.float32)
    Wo = np.asarray(inputs["L0_attn_Wo"], dtype=np.float32)

    def permute_qk(W):
        Wr = W.reshape(16, 32, 2, E)
        Wp = np.concatenate([Wr[:, :, 0, :], Wr[:, :, 1, :]], axis=1)
        Wt = Wp.reshape(E, E).T
        return np.ascontiguousarray(Wt.reshape(NEC, 128, E).transpose(1, 0, 2))

    d = {}
    d["wqt"] = permute_qk(Wq)
    d["wkt"] = permute_qk(Wk)
    d["wvt"] = np.ascontiguousarray(Wv.T.reshape(NEC, 128, E).transpose(1, 0, 2))
    d["wot"] = np.ascontiguousarray(Wo.T.reshape(NPAIR, 128, E).transpose(1, 0, 2))

    dd = 64
    inv_freq = 1.0 / (10000.0 ** (np.arange(0, dd, 2, dtype=np.float64) / dd))
    pos = np.arange(T, dtype=np.float64)
    freqs = np.repeat(pos[:, None] * inv_freq[None, :], 2, axis=-1)
    sv = (np.arange(0, dd, 2, dtype=np.float64) + 0.4 * dd) / (1.4 * dd)
    power = (pos - T // 2) / 512.0
    scale = sv[None, :] ** power[:, None]
    scale = np.concatenate([scale, scale], axis=-1)
    perm = np.concatenate([np.arange(0, 64, 2), np.arange(1, 64, 2)])
    swap = (np.arange(64) + 32) % 64
    sign = np.where(np.arange(64) < 32, -1.0, 1.0)
    cos_p = np.cos(freqs)[:, perm].T
    sin_p = np.sin(freqs)[:, perm].T
    sc_p = scale[:, perm].T

    def dup(t):
        return np.ascontiguousarray(np.concatenate([t, t], axis=0), dtype=np.float32)

    d["cq"] = dup(cos_p * sc_p * 0.125)
    d["sq"] = dup(((sin_p * sc_p * 0.125) * sign[:, None])[swap, :])
    d["ck"] = dup(cos_p / sc_p)
    d["sk"] = dup(((sin_p / sc_p) * sign[:, None])[swap, :])

    Wg = np.asarray(inputs["L0_ffn_W_gate"], np.float32)
    d["wg"] = np.ascontiguousarray(
        Wg.reshape(NEC, 128, NEXP).transpose(1, 0, 2).reshape(128, NEC * NEXP))
    d["bgbc"] = np.ascontiguousarray(
        np.broadcast_to(np.asarray(inputs["L0_ffn_b_gate"], np.float32)[None, :],
                        (128, NEXP)))
    W1 = np.asarray(inputs["L0_ffn_W1"], np.float32)
    d["w1"] = np.ascontiguousarray(W1.reshape(NEXP, NEC, 128, E).transpose(0, 2, 1, 3))
    W2 = np.asarray(inputs["L0_ffn_W2"], np.float32)
    d["w2"] = np.ascontiguousarray(W2.reshape(NEXP, NEC, 128, E).transpose(0, 2, 1, 3))
    b1 = np.asarray(inputs["L0_ffn_b1"], np.float32)
    d["b1c"] = np.ascontiguousarray(
        b1.reshape(NEXP, NEC, 128).transpose(2, 0, 1).reshape(128, NEXP * NEC))
    d["b2"] = np.ascontiguousarray(
        np.asarray(inputs["L0_ffn_b2"], np.float32).reshape(1, -1))

    lam = np.concatenate([
        np.asarray(inputs["L0_attn_lambda_q1"], np.float32),
        np.asarray(inputs["L0_attn_lambda_k1"], np.float32),
        np.asarray(inputs["L0_attn_lambda_q2"], np.float32),
        np.asarray(inputs["L0_attn_lambda_k2"], np.float32)])
    d["lam"] = np.ascontiguousarray(np.broadcast_to(lam[None, :], (128, 256)))

    d["cvec"] = np.ascontiguousarray(np.broadcast_to(
        np.array([1e-5, -0.5, -1.0, 1.0], np.float32)[None, :], (128, 4)))
    d["iden"] = np.eye(128, dtype=np.float32)
    d["ones"] = np.ones((1, 512), dtype=np.float32)
    d["inv128"] = np.full((128, 1), 1.0 / 128.0, dtype=np.float32)
    return d


def build(nc):
    taps = {}

    def dram(name, shape, dtype=F32, kind="ExternalInput"):
        return nc.dram_tensor(name, shape, dtype, kind=kind).ap()

    x_h = dram("x", [T, E])
    wqt_h = dram("wqt", [128, NEC, E])
    wkt_h = dram("wkt", [128, NEC, E])
    wvt_h = dram("wvt", [128, NEC, E])
    wot_h = dram("wot", [128, NPAIR, E])
    wg_h = dram("wg", [128, NEC * NEXP])
    bgbc_h = dram("bgbc", [128, NEXP])
    w1_h = dram("w1", [NEXP, 128, NEC, E])
    w2_h = dram("w2", [NEXP, 128, NEC, E])
    b1c_h = dram("b1c", [128, NEXP * NEC])
    b2_h = dram("b2", [1, NEXP * E])
    lam_h = dram("lam", [128, 256])
    cq_h = dram("cq", [128, T])
    sq_h = dram("sq", [128, T])
    ck_h = dram("ck", [128, T])
    sk_h = dram("sk", [128, T])
    iden_h = dram("iden", [128, 128])
    ones_h = dram("ones", [1, 512])
    inv128_h = dram("inv128", [128, 1])
    cvec_h = dram("cvec", [128, 4])
    y_h = dram("y", [T, E], kind="ExternalOutput")
    tap_shapes = {
        "xnT": [128, NEC, T], "wall": [128, NT, NEXP], "kbf": [128, NPAIR, T],
        "V": [128, NT, E], "UT": [128, NPAIR, T], "scales": [128, NPAIR, NT],
        "acc": [128, NT, E], "qbf0": [128, T], "e0b": [128, T], "e1b": [128, T],
        "sums0": [128, NT], "s1a": [128, NT], "s1b": [128, NT],
        "hT": [128, NEC, T], "neglam": [128, 1],
    }
    for tname in _TAPS:
        taps[tname] = dram("tap_" + tname, tap_shapes[tname], F32,
                           kind="ExternalOutput")

    es = ExitStack()
    S = lambda name: SemT(es.enter_context(nc.semaphore(name)))

    sX = S("sX"); sLN = S("sLN"); sSm = S("sSm"); sRs = S("sRs"); sXn = S("sXn")
    sTp = S("sTp"); sEv = S("sEv")
    sG = S("sG"); sGv = S("sGv")
    sKm = S("sKm"); sKc = S("sKc"); sVm = S("sVm"); sVc = S("sVc")
    sWq = S("sWq"); sQm = S("sQm"); sQc = S("sQc"); sQr = S("sQr"); sKr = S("sKr"); sSub = S("sSub")
    sS0 = S("sS0"); sS1a = S("sS1a"); sS1b = S("sS1b")
    sSt0 = S("sSt0"); sSt1 = S("sSt1"); sExp = S("sExp")
    sRho1 = S("sRho1"); sRl = S("sRl"); sRho2 = S("sRho2")
    sDiff = S("sDiff"); sTr = S("sTr"); sTe = S("sTe")
    sPv = S("sPv"); sSq = S("sSq"); sMm = S("sMm"); sMc = S("sMc")
    sScT = S("sScT"); sSm2 = S("sSm2"); sAct = S("sAct"); sRst = S("sRst")
    sWo = S("sWo"); sWm = S("sWm"); sWacc = S("sWacc")
    sW1L = S("sW1L"); sW2L = S("sW2L"); sM1h = S("sM1h"); sRelu = S("sRelu")
    sM2y = S("sM2y"); sAcc = S("sAcc"); sB2L = S("sB2L")
    sXr = S("sXr"); sY = S("sY"); sB0 = S("sB0"); sB0g = S("sB0g"); sTap = S("sTap")
    sXa = [S(f"sXa{i}") for i in range(4)]
    sWqa = [S(f"sWqa{i}") for i in range(2)]
    sWoa = [S(f"sWoa{i}") for i in range(2)]
    sW1a = [S(f"sW1a{i}") for i in range(2)]
    sW2a = [S(f"sW2a{i}") for i in range(2)]
    sB2a = [S(f"sB2a{i}") for i in range(2)]
    sXra = [S(f"sXra{i}") for i in range(4)]
    sYa = [S(f"sYa{i}") for i in range(4)]
    sCv = S("sCv"); sCs = S("sCs")

    def mkW(eng, chain_sem):
        def W(instr, sem=None):
            s = sem if sem is not None else chain_sem
            s.inc(instr)
            eng.wait_ge(s.h, s.n)
            return instr
        return W

    def emit_taps(available):
        live = [(n, taps[n], available[n]) for n in taps if n in available]
        if not live:
            return
        with nc.Block("taps") as blk:
            @blk.gpsimd
            def _(g):
                for name, hap, src in live:
                    sTap.inc(g.dma_start(hap[:], src[:]), 16)
                g.wait_ge(sTap.h, sTap.n)

    sb = lambda *a, **k: es.enter_context(nc.sbuf_tensor(*a, **k))

    xnT = sb("xnT", [128, NEC, T], BF16)
    wall = sb("wall", [128, NT, NEXP], F32)
    scales = sb("scales", [128, NPAIR, NT], F32)
    neglam = sb("neglam", [128, 1], F32)
    iden_bf = sb("iden_bf", [128, 128], BF16)
    ones_bf = sb("ones_bf", [1, 512], BF16)
    inv128_bf = sb("inv128_bf", [128, 1], BF16)
    b1c_sb = sb("b1c_sb", [128, NEXP * NEC], F32)
    cvec_sb = sb("cvec_sb", [128, 4], F32)

    arena = es.enter_context(nc.psum_tensor("arena", [128, 3584], F32))
    psT = es.enter_context(nc.psum_tensor("psT", [128, 8, 128], BF16))

    esA = ExitStack()
    sbA = lambda *a, **k: esA.enter_context(nc.sbuf_tensor(*a, **k))
    kbf = sbA("kbf", [128, NPAIR, T], BF16)
    kbf_sub = sbA("kbf_sub", [128, NPAIR, 256], BF16)
    Vsb = sbA("V", [128, NT, E], BF16)
    cq_bf = sbA("cq_bf", [128, T], BF16)
    sq_bf = sbA("sq_bf", [128, T], BF16)
    qraw = sbA("qraw", [128, T], BF16)
    t1b = sbA("t1b", [128, T], BF16)
    usw = sbA("usw", [128, T], BF16)

    esA1 = ExitStack()
    sbA1 = lambda *a, **k: esA1.enter_context(nc.sbuf_tensor(*a, **k))
    ck_bf = sbA1("ck_bf", [128, T], BF16)
    sk_bf = sbA1("sk_bf", [128, T], BF16)
    wkt_bf = sbA1("wkt_bf", [128, NEC, E], BF16)
    wvt_bf = sbA1("wvt_bf", [128, NEC, E], BF16)
    xt = [sbA1(f"xt{i}", [128, E], F32) for i in range(4)]
    xnb = [sbA1(f"xnb{i}", [128, E], BF16) for i in range(2)]
    ln_s1 = sbA1("ln_s1", [128, 1], F32)
    ln_s2 = sbA1("ln_s2", [128, 1], F32)
    ln_mean = sbA1("ln_mean", [128, 1], F32)
    ln_m2 = sbA1("ln_m2", [128, 1], F32)
    ln_var = sbA1("ln_var", [128, 1], F32)
    ln_ln = sbA1("ln_ln", [128, 1], F32)
    ln_rs = sbA1("ln_rs", [128, 1], F32)
    gsb = sbA1("gsb", [128, NEXP], F32)
    top8 = sbA1("top8", [128, NEXP], F32)
    gesb = sbA1("gesb", [128, NEXP], F32)
    wg_bf = sbA1("wg_bf", [128, NEC * NEXP], BF16)
    bgbc_sb = sbA1("bgbc_sb", [128, NEXP], F32)
    lam_sb = sbA1("lam_sb", [128, 256], F32)
    lprod = sbA1("lprod", [128, 64], F32)
    ld12 = sbA1("ld12", [128, 2], F32)
    le12 = sbA1("le12", [128, 2], F32)

    with nc.Block("b0") as blk:
        @blk.sync
        def _(sy):
            sB0.inc(sy.dma_start(bgbc_sb[:], bgbc_h[:]), 16)
            sB0.inc(sy.dma_start(lam_sb[:], lam_h[:]), 16)
            sB0.inc(sy.dma_start(b1c_sb[:], b1c_h[:]), 16)
            sB0.inc(sy.dma_start(cvec_sb[:], cvec_h[:]), 16)
            sy.wait_ge(sB0.h, sB0.n)

        @blk.gpsimd
        def _(g):
            for dst, src in ((iden_bf, iden_h), (ones_bf, ones_h),
                             (inv128_bf, inv128_h), (wg_bf, wg_h)):
                sB0g.inc(g.dma_start(dst[:], src[:]), 16)
            g.wait_ge(sB0g.h, sB0g.n)

        @blk.vector
        def _(v):
            W = mkW(v, sCv)
            v.wait_ge(sB0.h, 64)
            W(v.tensor_mul(lprod[:], lam_sb[:, 0:64], lam_sb[:, 64:128]))
            W(v.reduce_sum(ld12[:, 0:1], lprod[:], axis=AX))
            W(v.tensor_mul(lprod[:], lam_sb[:, 128:192], lam_sb[:, 192:256]))
            W(v.reduce_sum(ld12[:, 1:2], lprod[:], axis=AX), sGv)

        @blk.scalar
        def _(s):
            s.wait_ge(sGv.h, 1)
            Ws = mkW(s, sCs)
            Ws(s.activation(le12[:], ld12[:], AF.Exp), sRs)

        @blk.vector
        def _(v):
            W = mkW(v, sCv)
            v.wait_ge(sRs.h, 1)
            W(v.tensor_sub(neglam[:], le12[:, 1:2], le12[:, 0:1]), sGv)
            W(v.tensor_scalar(neglam[:], neglam[:], LI, None, OP.subtract))

    with nc.Block("b0b") as blk:
        @blk.gpsimd
        def _(g):
            for dst, src in ((cq_bf, cq_h), (sq_bf, sq_h), (ck_bf, ck_h),
                             (sk_bf, sk_h), (wkt_bf, wkt_h), (wvt_bf, wvt_h)):
                sB0g.inc(g.dma_start(dst[:], src[:]), 16)
            g.wait_ge(sB0g.h, sB0g.n)

        @blk.sync
        def _(sy):
            for m in range(4):
                sXa[m].inc(sy.dma_start(xt[m][:], x_h[m * 128:(m + 1) * 128, :]), 16)

    if _STAGE < 1:
        emit_taps({})
        esA1.close(); esA.close(); es.close()
        return taps

    with nc.Block("p1") as blk:
        @blk.sync
        def _(sy):
            for m in range(4, NT):
                sy.wait_ge(sXn.h, m - 3)
                sXa[m % 4].inc(sy.dma_start(xt[m % 4][:],
                                            x_h[m * 128:(m + 1) * 128, :]), 16)

        @blk.vector
        def _(v):
            W = mkW(v, sCv)
            for m in range(NT):
                v.wait_ge(sXa[m % 4].h, 16 * (m // 4 + 1))
                if m >= 2:
                    v.wait_ge(sTp.h, m - 1)
                W(v.reduce_sum(ln_s1[:], xt[m % 4][:], axis=AX))
                W(v.scalar_tensor_tensor(
                    xnb[m % 2][:], xt[m % 4][:], 1.0, xt[m % 4][:],
                    OP.bypass, OP.mult, accum_out=ln_s2[:]), sLN)
                W(v.tensor_scalar(ln_mean[:], ln_s1[:], 1.0 / E, None, OP.mult))
                W(v.tensor_mul(ln_m2[:], ln_mean[:], ln_mean[:]))
                W(v.scalar_tensor_tensor(
                    ln_var[:], ln_s2[:], 1.0 / E, ln_m2[:], OP.mult, OP.subtract),
                  sSm)
                v.wait_ge(sRs.h, 1 + m + 1)
                W(v.tensor_scalar(xnb[m % 2][:], xt[m % 4][:], ln_mean[:],
                                  ln_rs[:], OP.subtract, OP.mult), sXn)
                if m >= 1:
                    v.wait_ge(sTp.h, m)
                    W(v.tensor_copy(xnT[:, :, (m - 1) * 128:m * 128], psT[:]), sEv)

        @blk.scalar
        def _(s):
            Ws = mkW(s, sCs)
            for m in range(NT):
                s.wait_ge(sSm.h, m + 1)
                Ws(s.activation(ln_ln[:], ln_var[:], AF.Ln, bias=cvec_sb[:, 0:1]),
                   sAct)
                Ws(s.activation(ln_rs[:], ln_ln[:], AF.Exp, scale=cvec_sb[:, 1:2]),
                   sRs)

        @blk.tensor
        def _(t):
            for m in range(NT):
                t.wait_ge(sXn.h, m + 1)
                if m >= 1:
                    t.wait_ge(sEv.h, m)
                for k in range(NEC):
                    ins = t.transpose(psT[:, k, :],
                                      xnb[m % 2][:, k * 128:(k + 1) * 128], iden_bf[:])
                sTp.inc(ins)

        @blk.vector
        def _(v):
            v.wait_ge(sTp.h, NT)
            sEv.inc(v.tensor_copy(xnT[:, :, (NT - 1) * 128:NT * 128], psT[:]))
            v.wait_ge(sEv.h, sEv.n)

    if _STAGE < 2:
        emit_taps(dict(xnT=xnT, neglam=neglam))
        esA1.close(); esA.close(); es.close()
        return taps

    gv0 = sGv.n
    with nc.Block("p1b") as blk:
        @blk.tensor
        def _(t):
            for m in range(NT):
                slot = m % 2
                psG = arena[:, slot * 512:slot * 512 + 8]
                if m >= 2:
                    t.wait_ge(sGv.h, gv0 + 7 * (m - 1))
                for dc in range(NEC):
                    ins = t.matmul(psG, xnT[:, dc, m * 128:(m + 1) * 128],
                                   wg_bf[:, dc * 8:(dc + 1) * 8],
                                   start=(dc == 0), stop=(dc == NEC - 1))
                sG.inc(ins)

        @blk.vector
        def _(v):
            W = mkW(v, sCv)
            for m in range(NT):
                slot = m % 2
                psG = arena[:, slot * 512:slot * 512 + 8]
                v.wait_ge(sG.h, m + 1)
                W(v.tensor_add(gsb[:], psG, bgbc_sb[:]), sGv)
                W(v.tensor_scalar(gesb[:], gsb[:], 1.0, None, OP.mult, OP.max,
                                  accum_out=top8[:, 0:1]), sGv)
                W(v.tensor_scalar(gesb[:], gsb[:], top8[:, 0:1], None, OP.is_ge),
                  sGv)
                W(v.scalar_tensor_tensor(gesb[:], gesb[:], -1e30, gsb[:],
                                         OP.mult, OP.add), sGv)
                W(v.tensor_scalar(gesb[:], gesb[:], 1.0, None, OP.mult, OP.max,
                                  accum_out=top8[:, 1:2]), sGv)
                W(v.tensor_scalar(gesb[:], gsb[:], top8[:, 1:2], None, OP.is_ge),
                  sGv)
                W(v.tensor_mul(wall[:, m, :], gsb[:], gesb[:]), sGv)

    if _STAGE < 3:
        emit_taps(dict(xnT=xnT, wall=wall, neglam=neglam))
        esA1.close(); esA.close(); es.close()
        return taps

    with nc.Block("p2") as blk:
        @blk.tensor
        def _(t):
            psK = arena[:, 0:2048]
            for p in range(NPAIR):
                if p >= 1:
                    t.wait_ge(sKc.h, p)
                for tc in range(4):
                    for ec in range(NEC):
                        ins = t.matmul(psK[:, tc * 512:(tc + 1) * 512],
                                       wkt_bf[:, ec, p * 128:(p + 1) * 128],
                                       xnT[:, ec, tc * 512:(tc + 1) * 512],
                                       start=(ec == 0), stop=(ec == NEC - 1))
                sKm.inc(ins)
            for m in range(NT):
                slot = m % 2
                psV = arena[:, 2048:3072] if slot == 0 else arena[:, 0:1024]
                if m < 2:
                    t.wait_ge(sKc.h, NPAIR)
                else:
                    t.wait_ge(sVc.h, m - 1)
                for jc in range(2):
                    for ec in range(NEC):
                        ins = t.matmul(psV[:, jc * 512:(jc + 1) * 512],
                                       xnT[:, ec, m * 128:(m + 1) * 128],
                                       wvt_bf[:, ec, jc * 512:(jc + 1) * 512],
                                       start=(ec == 0), stop=(ec == NEC - 1))
                sVm.inc(ins)

        @blk.vector
        def _(v):
            W = mkW(v, sCv)
            psK = arena[:, 0:2048]
            for p in range(NPAIR):
                v.wait_ge(sKm.h, p + 1)
                W(v.tensor_copy(qraw[:], psK), sKc)
                W(v.tensor_mul(t1b[:], qraw[:], ck_bf[:]))
                W(v.tensor_mul(usw[0:32, :], qraw[32:64, :], sk_bf[32:64, :]))
                W(v.tensor_mul(usw[32:64, :], qraw[0:32, :], sk_bf[0:32, :]))
                W(v.tensor_mul(usw[64:96, :], qraw[96:128, :], sk_bf[96:128, :]))
                W(v.tensor_mul(usw[96:128, :], qraw[64:96, :], sk_bf[64:96, :]))
                W(v.tensor_add(kbf[:, p, :], t1b[:], usw[:]), sKr)
                W(v.tensor_copy(kbf_sub[:, p, :], kbf[:, p, 0:2048:8]))
            for m in range(NT):
                slot = m % 2
                psV = arena[:, 2048:3072] if slot == 0 else arena[:, 0:1024]
                v.wait_ge(sVm.h, m + 1)
                W(v.tensor_copy(Vsb[:, m, :], psV), sVc)

    if _STAGE < 4:
        emit_taps(dict(xnT=xnT, wall=wall, kbf=kbf, V=Vsb, neglam=neglam))
        esA1.close(); esA.close(); es.close()
        return taps

    esA1.close()
    esU = ExitStack()
    UT = esU.enter_context(nc.sbuf_tensor("UT", [128, NPAIR, T], BF16, side="right"))

    esA2 = ExitStack()
    sbA2 = lambda *a, **k: esA2.enter_context(nc.sbuf_tensor(*a, **k))
    qbf = [sbA2(f"qbf{i}", [128, T], BF16) for i in range(2)]
    e0b = [sbA2(f"e0b{i}", [128, T], BF16) for i in range(2)]
    e1b = [sbA2(f"e1b{i}", [128, T], BF16) for i in range(2)]
    diffb = [sbA2(f"diffb{i}", [128, T], BF16) for i in range(3)]
    diffT = [sbA2(f"diffT{i}", [128, NT, 128], BF16) for i in range(2)]
    sums0 = sbA2("sums0", [128, NT], F32)
    nb_all = sbA2("nb_all", [128, 2 * NT], F32)
    csums = [sbA2(f"csums{i}", [128, 4], F32) for i in range(2)]
    sjunk = sbA2("sjunk", [128, 256], BF16)
    s01 = sbA2("s01", [128, 2], F32)
    s1t = sbA2("s1t", [128, 1], F32)
    ab = sbA2("ab", [128, NT], F32)
    rhot = sbA2("rhot", [128, 1], F32)
    nrho = [sbA2(f"nrho{i}", [128, 1], F32) for i in range(2)]
    sqb = sbA2("sqb", [128, 128], BF16)
    s0sq = sbA2("s0sq", [128, NT], F32)
    innerb = sbA2("innerb", [128, NT], F32)
    lnb = sbA2("lnb", [128, NT], F32)
    rstb = sbA2("rstb", [128, NT], F32)
    wqb = [sbA2(f"wqb{i}", [128, NEC, 128], BF16) for i in range(2)]

    psQ = arena[:, 0:2048]
    SLOT = [arena[:, 0:1024], arena[:, 1024:2048], arena[:, 2048:3072]]
    psU = arena[:, 3072:3200]
    psMsq = arena[0:1, 3200:3328]
    psSc = arena[:, 3328:3344]

    def p3_items():
        items = [("sub", 0), ("sub", 1)]
        for m in range(NT):
            if m + 2 < NT:
                items.append(("sub", m + 2))
            for c in range(4):
                items.append(("main", m, c))
        return items

    def p3_plan():
        plans = []
        slot_state = [None, None, None]
        n_sub = 0
        n_main = 0
        for p in range(NPAIR):
            plan = []
            for j, it in enumerate(p3_items()):
                k = len(plans) * 80 + j
                slot = k % 3
                waits = []
                prev = slot_state[slot]
                if prev is not None:
                    waits.append(prev)
                if j < 3:
                    waits.append(("sQc", p + 1))
                if it[0] == "sub":
                    n_sub += 1
                    slot_state[slot] = ("sSt0", 2 * n_sub)
                    plan.append((it, slot, waits, n_sub, None))
                else:
                    n_main += 1
                    slot_state[slot] = ("sExp", n_main)
                    plan.append((it, slot, waits, None, n_main))
            plans.append(plan)
        return plans

    P3PLANS = p3_plan()

    def tail_pieces(t, p, m, piece):
        mm = NT * p + m
        if piece == 0:
            t.wait_ge(sDiff.h, mm + 1)
            if mm >= 1:
                t.wait_ge(sTe.h, 2 * mm)
            for sc in range(8):
                ins = t.transpose(psT[:, sc, :],
                                  diffb[m % 3][:, sc * 128:(sc + 1) * 128],
                                  iden_bf[:])
            sTr.inc(ins)
        elif piece == 1:
            t.wait_ge(sTe.h, 2 * mm + 1)
            for sc in range(8):
                ins = t.transpose(psT[:, sc, :],
                                  diffb[m % 3][:, (8 + sc) * 128:(9 + sc) * 128],
                                  iden_bf[:])
            sTr.inc(ins)
        elif piece == 2:
            t.wait_ge(sTe.h, 2 * mm + 1)
            if m == 0 and p >= 1:
                t.wait_ge(sSm2.h, p)
            if mm >= 1:
                t.wait_ge(sSq.h, mm)
            for sc in range(8):
                t.matmul(psU[:], Vsb[:, sc, p * 128:(p + 1) * 128],
                         diffT[m % 2][:, sc, :],
                         start=(sc == 0), stop=False)
        elif piece == 3:
            t.wait_ge(sTe.h, 2 * mm + 2)
            for sc in range(8, NT):
                ins = t.matmul(psU[:], Vsb[:, sc, p * 128:(p + 1) * 128],
                               diffT[m % 2][:, sc, :],
                               start=False, stop=(sc == NT - 1))
            sPv.inc(ins)
        else:
            t.wait_ge(sSq.h, mm + 1)
            sMm.inc(t.matmul(psSc[:, m:m + 1], sqb[:], inv128_bf[:],
                             start=True, stop=True))

    with nc.Block("p3") as blk:
        @blk.gpsimd
        def _(g):
            sWqa[0].inc(g.dma_start(wqb[0][:], wqt_h[:, :, 0:128]), 16)
            for p in range(1, NPAIR):
                if p >= 2:
                    g.wait_ge(sQm.h, p - 1)
                sWqa[p % 2].inc(g.dma_start(wqb[p % 2][:],
                                            wqt_h[:, :, p * 128:(p + 1) * 128]), 16)

        @blk.tensor
        def _(t):
            semmap = {"sQc": sQc, "sSt0": sSt0, "sExp": sExp}

            def emit_item(item):
                (it, slot, waits, nsub, nmain) = item
                for (semname, val) in waits:
                    t.wait_ge(semmap[semname].h, val)
                if it[0] == "sub":
                    m = it[1]
                    if m < 2:
                        t.wait_ge(sQr.h, p + 1)
                    for h in range(2):
                        ins = t.matmul(
                            SLOT[slot][:, h * 512:h * 512 + 256],
                            qbf[p % 2][h * 64:(h + 1) * 64,
                                       m * 128:(m + 1) * 128],
                            kbf_sub[h * 64:(h + 1) * 64, p, :],
                            start=True, stop=True, tile_position=(64 * h, 0))
                    sSub.inc(ins)
                else:
                    m, c = it[1], it[2]
                    h = c // 2
                    half = c % 2
                    for sc in range(2):
                        col = half * 1024 + sc * 512
                        ins = t.matmul(
                            SLOT[slot][:, sc * 512:(sc + 1) * 512],
                            qbf[p % 2][h * 64:(h + 1) * 64,
                                       m * 128:(m + 1) * 128],
                            kbf[h * 64:(h + 1) * 64, p, col:col + 512],
                            start=True, stop=True, tile_position=(64 * h, 0))
                    sS0.inc(ins)

            for p in range(NPAIR):
                plan = P3PLANS[p]
                t.wait_ge(sWqa[p % 2].h, 16 * (p // 2 + 1))
                if p >= 1:
                    t.wait_ge(sExp.h, 64 * p)
                    t.wait_ge(sSt0.h, 32 * p)
                for tc in range(4):
                    for ec in range(NEC):
                        ins = t.matmul(psQ[:, tc * 512:(tc + 1) * 512],
                                       wqb[p % 2][:, ec, :],
                                       xnT[:, ec, tc * 512:(tc + 1) * 512],
                                       start=(ec == 0), stop=(ec == NEC - 1))
                sQm.inc(ins)
                by_tile = [[] for _ in range(NT)]
                head = []
                for item in plan:
                    it = item[0]
                    if it[0] == "sub" and it[1] < 2:
                        head.append(item)
                    elif it[0] == "sub":
                        by_tile[it[1] - 2].append(item)
                    else:
                        by_tile[it[1]].append(item)
                for item in head:
                    emit_item(item)
                for m in range(NT):
                    items = by_tile[m]
                    mains = items[-4:]
                    subs = items[:-4]
                    for item in subs:
                        emit_item(item)
                    emit_item(mains[0])
                    if m >= 2:
                        tail_pieces(t, p, m - 2, 0)
                    emit_item(mains[1])
                    if m >= 2:
                        tail_pieces(t, p, m - 2, 1)
                    emit_item(mains[2])
                    if m >= 2:
                        tail_pieces(t, p, m - 2, 2)
                    emit_item(mains[3])
                    if m >= 2:
                        tail_pieces(t, p, m - 2, 3)
                    if m >= 3:
                        tail_pieces(t, p, m - 3, 4)
                for piece in range(4):
                    tail_pieces(t, p, NT - 2, piece)
                tail_pieces(t, p, NT - 3, 4)
                for piece in range(4):
                    tail_pieces(t, p, NT - 1, piece)
                tail_pieces(t, p, NT - 2, 4)
                tail_pieces(t, p, NT - 1, 4)

        @blk.vector
        def _(v):
            W = mkW(v, sCv)
            nsub_g = 0
            for p in range(NPAIR):
                plan = P3PLANS[p]
                v.wait_ge(sQm.h, p + 1)
                W(v.tensor_copy(qraw[:], psQ), sQc)
                W(v.tensor_mul(t1b[:], qraw[:], cq_bf[:]))
                W(v.tensor_mul(usw[0:32, :], qraw[32:64, :], sq_bf[32:64, :]))
                W(v.tensor_mul(usw[32:64, :], qraw[0:32, :], sq_bf[0:32, :]))
                W(v.tensor_mul(usw[64:96, :], qraw[96:128, :], sq_bf[96:128, :]))
                W(v.tensor_mul(usw[96:128, :], qraw[64:96, :], sq_bf[64:96, :]))
                W(v.tensor_add(qbf[p % 2][:], t1b[:], usw[:]), sQr)
                subslot = {it[1]: slot for (it, slot, w_, ns, nm) in plan
                           if it[0] == "sub"}
                for m in range(NT):
                    mm = NT * p + m
                    if m >= 2:
                        v.wait_ge(sTr.h, 2 * (mm - 2) + 1)
                        W(v.tensor_copy(diffT[(m - 2) % 2][:, 0:8, :], psT[:]),
                          sTe)
                        v.wait_ge(sTr.h, 2 * (mm - 2) + 2)
                        W(v.tensor_copy(diffT[(m - 2) % 2][:, 8:16, :], psT[:]),
                          sTe)
                    for msub in ([0, 1] if m == 0 else []) + \
                            ([m + 2] if m + 2 < NT else []):
                        nsub_g += 1
                        v.wait_ge(sSub.h, nsub_g)
                        if p >= 1:
                            v.wait_ge(sExp.h, 64 * (p - 1) + 4 * (msub + 1))
                        sl = subslot[msub]
                        for h in range(2):
                            W(v.tensor_scalar(
                                sjunk[:], SLOT[sl][:, h * 512:h * 512 + 256],
                                -1.0, None, OP.mult, OP.min,
                                accum_out=nb_all[:, h * NT + msub:
                                                 h * NT + msub + 1]), sSt0)
                    v.wait_ge(sExp.h, 64 * p + 4 * m + 4)
                    W(v.tensor_tensor(s01[:], csums[m % 2][:, 0:4:2],
                                      csums[m % 2][:, 1:4:2], OP.add))
                    W(v.tensor_copy(sums0[:, m:m + 1], s01[:, 0:1]))
                    W(v.tensor_scalar(s1t[:], s01[:, 1:2], 2.0 ** -40, None,
                                      OP.mult))
                    W(v.reciprocal(rhot[:], s1t[:]))
                    W(v.tensor_mul(rhot[:], rhot[:], s01[:, 0:1]))
                    if mm >= 3:
                        v.wait_ge(sTr.h, 2 * (mm - 3) + 2)
                    W(v.tensor_scalar(nrho[m % 2][:], rhot[:], neglam[:],
                                      2.0 ** -40, OP.mult, OP.mult))
                    W(v.tensor_scalar(t1b[:], e1b[m % 2][:], nrho[m % 2][:],
                                      None, OP.mult))
                    W(v.tensor_add(diffb[m % 3][:], t1b[:], e0b[m % 2][:]), sDiff)
                for mf in (NT - 2, NT - 1):
                    mmf = NT * p + mf
                    v.wait_ge(sTr.h, 2 * mmf + 1)
                    W(v.tensor_copy(diffT[mf % 2][:, 0:8, :], psT[:]), sTe)
                    v.wait_ge(sTr.h, 2 * mmf + 2)
                    W(v.tensor_copy(diffT[mf % 2][:, 8:16, :], psT[:]), sTe)
                v.wait_ge(sMm.h, NT * (p + 1))
                W(v.tensor_scalar(s0sq[:], sums0[:], 2.0 ** -40, None, OP.mult))
                W(v.reciprocal(ab[:], s0sq[:]))
                W(v.tensor_mul(s0sq[:], ab[:], ab[:]))
                W(v.tensor_tensor(innerb[:], psSc[:], s0sq[:], OP.mult))
                W(v.tensor_scalar(innerb[:], innerb[:], 2.0 ** -30, None, OP.mult),
                  sSm2)
                v.wait_ge(sRst.h, p + 1)
                W(v.tensor_mul(innerb[:], ab[:], rstb[:]))
                W(v.tensor_scalar(scales[:, p, :], innerb[:],
                                  (1.0 - LI) * 2.0 ** -40, None, OP.mult))

        @blk.scalar
        def _(s):
            Ws = mkW(s, sCs)
            for p in range(NPAIR):
                plan = P3PLANS[p]
                for (it, slot, waits, nsub, nmain) in plan:
                    if it[0] != "main":
                        continue
                    m, c = it[1], it[2]
                    mm = NT * p + m
                    h = c // 2
                    half = c % 2
                    s.wait_ge(sS0.h, nmain)
                    s.wait_ge(sSt0.h, 2 * (NT * p + m + 1))
                    if mm >= 2 and c == 0:
                        s.wait_ge(sDiff.h, mm - 1)
                    ebuf = e0b[m % 2] if h == 0 else e1b[m % 2]
                    Ws(s.activation(ebuf[:, half * 1024:(half + 1) * 1024],
                                    SLOT[slot][:], AF.Exp,
                                    bias=nb_all[:, h * NT + m:h * NT + m + 1],
                                    accum_out=csums[m % 2][:, c:c + 1]), sExp)
                    if c == 3 and m >= 2:
                        s.wait_ge(sPv.h, mm - 1)
                        Ws(s.activation(UT[:, p, (m - 2) * 128:(m - 1) * 128],
                                        psU[:], AF.Identity))
                        if mm >= 3:
                            s.wait_ge(sMm.h, mm - 2)
                        Ws(s.activation(sqb[:], psU[:], AF.Square,
                                        scale=2.0 ** -25), sSq)
                for mf in (NT - 2, NT - 1):
                    mmf = NT * p + mf
                    s.wait_ge(sPv.h, mmf + 1)
                    Ws(s.activation(UT[:, p, mf * 128:(mf + 1) * 128], psU[:],
                                    AF.Identity))
                    s.wait_ge(sMm.h, mmf)
                    Ws(s.activation(sqb[:], psU[:], AF.Square, scale=2.0 ** -25),
                       sSq)
                s.wait_ge(sSm2.h, p + 1)
                Ws(s.activation(lnb[:], innerb[:], AF.Ln, bias=cvec_sb[:, 0:1]),
                   sAct)
                Ws(s.activation(rstb[:], lnb[:], AF.Exp, scale=cvec_sb[:, 1:2]),
                   sRst)

    if _STAGE < 5:
        emit_taps(dict(xnT=xnT, wall=wall, kbf=kbf, V=Vsb, UT=UT, scales=scales,
                       sums0=sums0, qbf0=qbf[0],
                       e0b=e0b[0], e1b=e1b[0], neglam=neglam))
        esA2.close(); esU.close(); esA.close(); es.close()
        return taps

    esA2.close(); esA.close()
    esC = ExitStack()
    sbC = lambda *a, **k: esC.enter_context(nc.sbuf_tensor(*a, **k))
    acc = sbC("acc", [128, NT, E], F32)
    wob = [sbC(f"wob{i}", [128, E], BF16) for i in range(2)]

    with nc.Block("p4") as blk:
        @blk.gpsimd
        def _(g):
            sWoa[0].inc(g.dma_start(wob[0][:], wot_h[:, 0, :]), 16)
            for p in range(1, NPAIR):
                if p >= 2:
                    g.wait_ge(sWm.h, 32 * (p - 1))
                sWoa[p % 2].inc(g.dma_start(wob[p % 2][:], wot_h[:, p, :]), 16)

        @blk.tensor
        def _(t):
            for p in range(NPAIR):
                t.wait_ge(sWoa[p % 2].h, 16 * (p // 2 + 1))
                for m in range(NT):
                    for oc in range(2):
                        i = p * 32 + m * 2 + oc
                        slot = (m * 2 + oc) % 4
                        psW = arena[:, slot * 512:(slot + 1) * 512]
                        if i >= 4:
                            t.wait_ge(sWacc.h, i - 3)
                        sWm.inc(t.matmul(psW, UT[:, p, m * 128:(m + 1) * 128],
                                         wob[p % 2][:, oc * 512:(oc + 1) * 512],
                                         start=True, stop=True))

        @blk.vector
        def _(v):
            W = mkW(v, sCv)
            for p in range(NPAIR):
                for m in range(NT):
                    for oc in range(2):
                        i = p * 32 + m * 2 + oc
                        slot = (m * 2 + oc) % 4
                        psW = arena[:, slot * 512:(slot + 1) * 512]
                        v.wait_ge(sWm.h, i + 1)
                        dst = acc[:, m, oc * 512:(oc + 1) * 512]
                        if p == 0:
                            ins = v.tensor_scalar(dst, psW, scales[:, 0, m:m + 1],
                                                  None, OP.mult)
                        else:
                            ins = v.scalar_tensor_tensor(
                                dst, psW, scales[:, p, m:m + 1], dst,
                                OP.mult, OP.add)
                        W(ins, sWacc)

    if _STAGE < 6:
        emit_taps(dict(acc=acc, UT=UT, scales=scales, wall=wall, xnT=xnT))
        esU.close(); esC.close(); es.close()
        return taps

    esU.close()
    esD = ExitStack()
    sbD = lambda *a, **k: esD.enter_context(nc.sbuf_tensor(*a, **k))
    w1b = [sbD(f"w1b{i}", [128, NEC, E], BF16) for i in range(2)]
    w2b = [sbD(f"w2b{i}", [128, NEC, E], BF16) for i in range(2)]
    hT = sbD("hT", [128, NEC, T], BF16)
    b2e = [sbD(f"b2e{i}", [1, E], BF16) for i in range(2)]

    with nc.Block("p5") as blk:
        @blk.gpsimd
        def _(g):
            sW1a[0].inc(g.dma_start(w1b[0][:], w1_h[0]), 16)
            sB2a[0].inc(g.dma_start(b2e[0][:], b2_h[0:1, 0:E]), 16)
            sW2a[0].inc(g.dma_start(w2b[0][:], w2_h[0]), 16)
            for e in range(1, NEXP):
                g.wait_ge(sM1h.h, 16 * e)
                sW1a[e % 2].inc(g.dma_start(w1b[e % 2][:], w1_h[e]), 16)
                g.wait_ge(sM2y.h, 32 * e)
                sB2a[e % 2].inc(g.dma_start(b2e[e % 2][:],
                                            b2_h[0:1, e * E:(e + 1) * E]), 16)
                sW2a[e % 2].inc(g.dma_start(w2b[e % 2][:], w2_h[e]), 16)

        @blk.tensor
        def _(t):
            for e in range(NEXP):
                t.wait_ge(sW1a[e % 2].h, 16 * (e // 2 + 1))
                for hd in range(NEC):
                    for th in range(2):
                        i = e * 16 + hd * 2 + th
                        slot = (hd * 2 + th) % 2
                        psH = arena[:, slot * 1024:(slot + 1) * 1024]
                        if i >= 2:
                            t.wait_ge(sRelu.h, i - 1)
                        for half in range(2):
                            col = th * 1024 + half * 512
                            for dc in range(NEC):
                                ins = t.matmul(
                                    psH[:, half * 512:(half + 1) * 512],
                                    w1b[e % 2][:, dc, hd * 128:(hd + 1) * 128],
                                    xnT[:, dc, col:col + 512],
                                    start=(dc == 0), stop=(dc == NEC - 1))
                        sM1h.inc(ins)
                t.wait_ge(sRelu.h, 16 * (e + 1))
                t.wait_ge(sW2a[e % 2].h, 16 * (e // 2 + 1))
                t.wait_ge(sB2a[e % 2].h, 16 * (e // 2 + 1))
                for m in range(NT):
                    for oc in range(2):
                        i = e * 32 + m * 2 + oc
                        slot = (m * 2 + oc) % 2
                        psY = arena[:, 2048 + slot * 512:2048 + (slot + 1) * 512]
                        if i >= 2:
                            t.wait_ge(sAcc.h, i - 1)
                        for hc in range(NEC):
                            t.matmul(psY, hT[:, hc, m * 128:(m + 1) * 128],
                                     w2b[e % 2][:, hc, oc * 512:(oc + 1) * 512],
                                     start=(hc == 0), stop=False)
                        ins = t.matmul(psY, ones_bf[0:1, 0:128],
                                       b2e[e % 2][0:1, oc * 512:(oc + 1) * 512],
                                       start=False, stop=True)
                        sM2y.inc(ins)

        @blk.scalar
        def _(s):
            Ws = mkW(s, sCs)
            for e in range(NEXP):
                for hd in range(NEC):
                    for th in range(2):
                        i = e * 16 + hd * 2 + th
                        slot = (hd * 2 + th) % 2
                        psH = arena[:, slot * 1024:(slot + 1) * 1024]
                        s.wait_ge(sM1h.h, i + 1)
                        if e >= 1:
                            s.wait_ge(sM2y.h, 32 * e)
                        Ws(s.activation(
                            hT[:, hd, th * 1024:(th + 1) * 1024], psH, AF.Relu,
                            bias=b1c_sb[:, e * NEC + hd:e * NEC + hd + 1]), sRelu)

        @blk.vector
        def _(v):
            W = mkW(v, sCv)
            for e in range(NEXP):
                for m in range(NT):
                    for oc in range(2):
                        i = e * 32 + m * 2 + oc
                        slot = (m * 2 + oc) % 2
                        psY = arena[:, 2048 + slot * 512:2048 + (slot + 1) * 512]
                        v.wait_ge(sM2y.h, i + 1)
                        dst = acc[:, m, oc * 512:(oc + 1) * 512]
                        W(v.scalar_tensor_tensor(
                            dst, psY, wall[:, m, e:e + 1], dst, OP.mult, OP.add),
                          sAcc)

    if _STAGE < 7:
        emit_taps(dict(acc=acc, hT=hT, wall=wall))
        esD.close(); esC.close(); es.close()
        return taps

    esD.close()
    esE = ExitStack()
    xr = [esE.enter_context(nc.sbuf_tensor(f"xr{i}", [128, E], F32))
          for i in range(4)]

    with nc.Block("p6") as blk:
        @blk.sync
        def _(sy):
            for m in range(NT):
                if m >= 4:
                    sy.wait_ge(sYa[m % 4].h, 16 * (m // 4))
                sXra[m % 4].inc(sy.dma_start(xr[m % 4][:],
                                             x_h[m * 128:(m + 1) * 128, :]), 16)

        @blk.vector
        def _(v):
            W = mkW(v, sCv)
            for m in range(NT):
                v.wait_ge(sXra[m % 4].h, 16 * (m // 4 + 1))
                W(v.tensor_add(xr[m % 4][:], xr[m % 4][:], acc[:, m, :]), sXn)

        @blk.scalar
        def _(s):
            for m in range(NT):
                s.wait_ge(sXn.h, NT + m + 1)
                sYa[m % 4].inc(s.dma_start(y_h[m * 128:(m + 1) * 128, :],
                                           xr[m % 4][:]), 16)
            for i in range(4):
                s.wait_ge(sYa[i].h, sYa[i].n)

    emit_taps(dict(acc=acc))
    esE.close(); esC.close(); es.close()
    return taps


def kernel(**inputs):
    x_full = np.asarray(inputs["L0_attn_x"], dtype=np.float32)
    d = _host_prep(inputs)

    nc = bass.Bass("TRN2", target_bir_lowering=False)
    build(nc)

    in_maps = []
    for c in range(8):
        m = dict(d)
        m["x"] = np.ascontiguousarray(x_full[c])
        in_maps.append(m)

    res = run_bass_kernel_spmd(nc, in_maps, core_ids=list(range(8)))
    out = np.stack([res.results[c]["y"] for c in range(8)], axis=0)
    if _TAPS:
        kernel.last_taps = [{t: res.results[c]["tap_" + t] for t in _TAPS}
                            for c in range(8)]
    return out


# version 15
# speedup vs baseline: 1.3190x; 1.0614x over previous
import math
from contextlib import ExitStack

import numpy as np

import concourse.bass as bass
import concourse.mybir as mybir
from concourse.bass_utils import run_bass_kernel_spmd

F32 = mybir.dt.float32
BF16 = mybir.dt.bfloat16
AF = mybir.ActivationFunctionType
OP = mybir.AluOpType
AX = mybir.AxisListType.X

T = 2048
E = 1024
NT = 16
NEC = 8
NPAIR = 8
NEXP = 8
LI = 0.8 - 0.6 * math.exp(-0.3)

_STAGE = 99
_TAPS = []


class SemT:
    def __init__(self, h):
        self.h = h
        self.n = 0

    def inc(self, instr, k=1):
        instr.then_inc(self.h, k)
        self.n += k
        return instr


def _host_prep(inputs):
    Wq = np.asarray(inputs["L0_attn_Wq"], dtype=np.float32)
    Wk = np.asarray(inputs["L0_attn_Wk"], dtype=np.float32)
    Wv = np.asarray(inputs["L0_attn_Wv"], dtype=np.float32)
    Wo = np.asarray(inputs["L0_attn_Wo"], dtype=np.float32)

    def permute_qk(W):
        Wr = W.reshape(16, 32, 2, E)
        Wp = np.concatenate([Wr[:, :, 0, :], Wr[:, :, 1, :]], axis=1)
        Wt = Wp.reshape(E, E).T
        return np.ascontiguousarray(Wt.reshape(NEC, 128, E).transpose(1, 0, 2))

    d = {}
    d["wqt"] = permute_qk(Wq)
    d["wkt"] = permute_qk(Wk)
    d["wvt"] = np.ascontiguousarray(Wv.T.reshape(NEC, 128, E).transpose(1, 0, 2))
    d["wot"] = np.ascontiguousarray(Wo.T.reshape(NPAIR, 128, E).transpose(1, 0, 2))

    dd = 64
    inv_freq = 1.0 / (10000.0 ** (np.arange(0, dd, 2, dtype=np.float64) / dd))
    pos = np.arange(T, dtype=np.float64)
    freqs = np.repeat(pos[:, None] * inv_freq[None, :], 2, axis=-1)
    sv = (np.arange(0, dd, 2, dtype=np.float64) + 0.4 * dd) / (1.4 * dd)
    power = (pos - T // 2) / 512.0
    scale = sv[None, :] ** power[:, None]
    scale = np.concatenate([scale, scale], axis=-1)
    perm = np.concatenate([np.arange(0, 64, 2), np.arange(1, 64, 2)])
    swap = (np.arange(64) + 32) % 64
    sign = np.where(np.arange(64) < 32, -1.0, 1.0)
    cos_p = np.cos(freqs)[:, perm].T
    sin_p = np.sin(freqs)[:, perm].T
    sc_p = scale[:, perm].T

    def dup(t):
        return np.ascontiguousarray(np.concatenate([t, t], axis=0), dtype=np.float32)

    d["cq"] = dup(cos_p * sc_p * 0.125)
    d["sq"] = dup(((sin_p * sc_p * 0.125) * sign[:, None])[swap, :])
    d["ck"] = dup(cos_p / sc_p)
    d["sk"] = dup(((sin_p / sc_p) * sign[:, None])[swap, :])

    Wg = np.asarray(inputs["L0_ffn_W_gate"], np.float32)
    d["wg"] = np.ascontiguousarray(
        Wg.reshape(NEC, 128, NEXP).transpose(1, 0, 2).reshape(128, NEC * NEXP))
    d["bgbc"] = np.ascontiguousarray(
        np.broadcast_to(np.asarray(inputs["L0_ffn_b_gate"], np.float32)[None, :],
                        (128, NEXP)))
    W1 = np.asarray(inputs["L0_ffn_W1"], np.float32)
    d["w1"] = np.ascontiguousarray(W1.reshape(NEXP, NEC, 128, E).transpose(0, 2, 1, 3))
    W2 = np.asarray(inputs["L0_ffn_W2"], np.float32)
    d["w2"] = np.ascontiguousarray(W2.reshape(NEXP, NEC, 128, E).transpose(0, 2, 1, 3))
    b1 = np.asarray(inputs["L0_ffn_b1"], np.float32)
    d["b1c"] = np.ascontiguousarray(
        b1.reshape(NEXP, NEC, 128).transpose(2, 0, 1).reshape(128, NEXP * NEC))
    d["b2"] = np.ascontiguousarray(
        np.asarray(inputs["L0_ffn_b2"], np.float32).reshape(1, -1))

    lam = np.concatenate([
        np.asarray(inputs["L0_attn_lambda_q1"], np.float32),
        np.asarray(inputs["L0_attn_lambda_k1"], np.float32),
        np.asarray(inputs["L0_attn_lambda_q2"], np.float32),
        np.asarray(inputs["L0_attn_lambda_k2"], np.float32)])
    d["lam"] = np.ascontiguousarray(np.broadcast_to(lam[None, :], (128, 256)))

    d["cvec"] = np.ascontiguousarray(np.broadcast_to(
        np.array([1e-5, -0.5, -1.0, 1.0], np.float32)[None, :], (128, 4)))
    d["iden"] = np.eye(128, dtype=np.float32)
    d["ones"] = np.ones((1, 512), dtype=np.float32)
    d["inv128"] = np.full((128, 1), 1.0 / 128.0, dtype=np.float32)
    return d


def build(nc):
    taps = {}

    def dram(name, shape, dtype=F32, kind="ExternalInput"):
        return nc.dram_tensor(name, shape, dtype, kind=kind).ap()

    x_h = dram("x", [T, E])
    wqt_h = dram("wqt", [128, NEC, E])
    wkt_h = dram("wkt", [128, NEC, E])
    wvt_h = dram("wvt", [128, NEC, E])
    wot_h = dram("wot", [128, NPAIR, E])
    wg_h = dram("wg", [128, NEC * NEXP])
    bgbc_h = dram("bgbc", [128, NEXP])
    w1_h = dram("w1", [NEXP, 128, NEC, E])
    w2_h = dram("w2", [NEXP, 128, NEC, E])
    b1c_h = dram("b1c", [128, NEXP * NEC])
    b2_h = dram("b2", [1, NEXP * E])
    lam_h = dram("lam", [128, 256])
    cq_h = dram("cq", [128, T])
    sq_h = dram("sq", [128, T])
    ck_h = dram("ck", [128, T])
    sk_h = dram("sk", [128, T])
    iden_h = dram("iden", [128, 128])
    ones_h = dram("ones", [1, 512])
    inv128_h = dram("inv128", [128, 1])
    cvec_h = dram("cvec", [128, 4])
    y_h = dram("y", [T, E], kind="ExternalOutput")
    tap_shapes = {
        "xnT": [128, NEC, T], "wall": [128, NT, NEXP], "kbf": [128, NPAIR, T],
        "V": [128, NT, E], "UT": [128, NPAIR, T], "scales": [128, NPAIR, NT],
        "acc": [128, NT, E], "qbf0": [128, T], "e0b": [128, T], "e1b": [128, T],
        "sums0": [128, NT], "s1a": [128, NT], "s1b": [128, NT],
        "hT": [128, NEC, T], "neglam": [128, 1],
    }
    for tname in _TAPS:
        taps[tname] = dram("tap_" + tname, tap_shapes[tname], F32,
                           kind="ExternalOutput")

    es = ExitStack()
    S = lambda name: SemT(es.enter_context(nc.semaphore(name)))

    sX = S("sX"); sLN = S("sLN"); sSm = S("sSm"); sRs = S("sRs"); sXn = S("sXn")
    sTp = S("sTp"); sEv = S("sEv")
    sG = S("sG"); sGv = S("sGv")
    sKm = S("sKm"); sKc = S("sKc"); sVm = S("sVm"); sVc = S("sVc")
    sWq = S("sWq"); sQm = S("sQm"); sQc = S("sQc"); sQr = S("sQr"); sKr = S("sKr"); sSub = S("sSub")
    sS0 = S("sS0"); sS1a = S("sS1a"); sS1b = S("sS1b")
    sSt0 = S("sSt0"); sSt1 = S("sSt1"); sExp = S("sExp")
    sRho1 = S("sRho1"); sRl = S("sRl"); sRho2 = S("sRho2")
    sDiff = S("sDiff"); sTr = S("sTr"); sTe = S("sTe")
    sPv = S("sPv"); sSq = S("sSq"); sMm = S("sMm"); sMc = S("sMc")
    sScT = S("sScT"); sSm2 = S("sSm2"); sAct = S("sAct"); sRst = S("sRst")
    sWo = S("sWo"); sWm = S("sWm"); sWacc = S("sWacc")
    sW1L = S("sW1L"); sW2L = S("sW2L"); sM1h = S("sM1h"); sRelu = S("sRelu")
    sM2y = S("sM2y"); sAcc = S("sAcc"); sB2L = S("sB2L")
    sXr = S("sXr"); sY = S("sY"); sB0 = S("sB0"); sB0g = S("sB0g"); sTap = S("sTap")
    sXa = [S(f"sXa{i}") for i in range(4)]
    sWqa = [S(f"sWqa{i}") for i in range(2)]
    sWoa = [S(f"sWoa{i}") for i in range(2)]
    sW1a = [S(f"sW1a{i}") for i in range(2)]
    sW2a = [S(f"sW2a{i}") for i in range(2)]
    sB2a = [S(f"sB2a{i}") for i in range(2)]
    sXra = [S(f"sXra{i}") for i in range(4)]
    sYa = [S(f"sYa{i}") for i in range(4)]
    sCv = S("sCv"); sCs = S("sCs")

    def mkW(eng, chain_sem):
        def W(instr, sem=None):
            s = sem if sem is not None else chain_sem
            s.inc(instr)
            eng.wait_ge(s.h, s.n)
            return instr
        return W

    def emit_taps(available):
        live = [(n, taps[n], available[n]) for n in taps if n in available]
        if not live:
            return
        with nc.Block("taps") as blk:
            @blk.gpsimd
            def _(g):
                for name, hap, src in live:
                    sTap.inc(g.dma_start(hap[:], src[:]), 16)
                g.wait_ge(sTap.h, sTap.n)

    sb = lambda *a, **k: es.enter_context(nc.sbuf_tensor(*a, **k))

    xnT = sb("xnT", [128, NEC, T], BF16)
    wall = sb("wall", [128, NT, NEXP], F32)
    scales = sb("scales", [128, NPAIR, NT], F32)
    neglam = sb("neglam", [128, 1], F32)
    iden_bf = sb("iden_bf", [128, 128], BF16)
    ones_bf = sb("ones_bf", [1, 512], BF16)
    inv128_bf = sb("inv128_bf", [128, 1], BF16)
    b1c_sb = sb("b1c_sb", [128, NEXP * NEC], F32)
    cvec_sb = sb("cvec_sb", [128, 4], F32)

    arena = es.enter_context(nc.psum_tensor("arena", [128, 3584], F32))
    psT = es.enter_context(nc.psum_tensor("psT", [128, 8, 128], BF16))

    esA = ExitStack()
    sbA = lambda *a, **k: esA.enter_context(nc.sbuf_tensor(*a, **k))
    kbf = sbA("kbf", [128, NPAIR, T], BF16)
    kbf_sub = sbA("kbf_sub", [128, NPAIR, 256], BF16)
    Vsb = sbA("V", [128, NT, E], BF16)
    cq_bf = sbA("cq_bf", [128, T], BF16)
    sq_bf = sbA("sq_bf", [128, T], BF16)
    qraw = sbA("qraw", [128, T], BF16)
    t1b = sbA("t1b", [128, T], BF16)
    usw = sbA("usw", [128, T], BF16)

    esA1 = ExitStack()
    sbA1 = lambda *a, **k: esA1.enter_context(nc.sbuf_tensor(*a, **k))
    ck_bf = sbA1("ck_bf", [128, T], BF16)
    sk_bf = sbA1("sk_bf", [128, T], BF16)
    wkt_bf = sbA1("wkt_bf", [128, NEC, E], BF16)
    wvt_bf = sbA1("wvt_bf", [128, NEC, E], BF16)
    xt = [sbA1(f"xt{i}", [128, E], F32) for i in range(4)]
    xnb = [sbA1(f"xnb{i}", [128, E], BF16) for i in range(2)]
    ln_s1 = sbA1("ln_s1", [128, 1], F32)
    ln_s2 = sbA1("ln_s2", [128, 1], F32)
    ln_mean = sbA1("ln_mean", [128, 1], F32)
    ln_m2 = sbA1("ln_m2", [128, 1], F32)
    ln_var = sbA1("ln_var", [128, 1], F32)
    ln_ln = sbA1("ln_ln", [128, 1], F32)
    ln_rs = sbA1("ln_rs", [128, 1], F32)
    gsb = sbA1("gsb", [128, NEXP], F32)
    top8 = sbA1("top8", [128, NEXP], F32)
    gesb = sbA1("gesb", [128, NEXP], F32)
    wg_bf = sbA1("wg_bf", [128, NEC * NEXP], BF16)
    bgbc_sb = sbA1("bgbc_sb", [128, NEXP], F32)
    lam_sb = sbA1("lam_sb", [128, 256], F32)
    lprod = sbA1("lprod", [128, 64], F32)
    ld12 = sbA1("ld12", [128, 2], F32)
    le12 = sbA1("le12", [128, 2], F32)

    with nc.Block("b0") as blk:
        @blk.sync
        def _(sy):
            sB0.inc(sy.dma_start(bgbc_sb[:], bgbc_h[:]), 16)
            sB0.inc(sy.dma_start(lam_sb[:], lam_h[:]), 16)
            sB0.inc(sy.dma_start(b1c_sb[:], b1c_h[:]), 16)
            sB0.inc(sy.dma_start(cvec_sb[:], cvec_h[:]), 16)
            sy.wait_ge(sB0.h, sB0.n)

        @blk.gpsimd
        def _(g):
            for dst, src in ((iden_bf, iden_h), (ones_bf, ones_h),
                             (inv128_bf, inv128_h), (wg_bf, wg_h)):
                sB0g.inc(g.dma_start(dst[:], src[:]), 16)
            g.wait_ge(sB0g.h, sB0g.n)

        @blk.vector
        def _(v):
            W = mkW(v, sCv)
            v.wait_ge(sB0.h, 64)
            W(v.tensor_mul(lprod[:], lam_sb[:, 0:64], lam_sb[:, 64:128]))
            W(v.reduce_sum(ld12[:, 0:1], lprod[:], axis=AX))
            W(v.tensor_mul(lprod[:], lam_sb[:, 128:192], lam_sb[:, 192:256]))
            W(v.reduce_sum(ld12[:, 1:2], lprod[:], axis=AX), sGv)

        @blk.scalar
        def _(s):
            s.wait_ge(sGv.h, 1)
            Ws = mkW(s, sCs)
            Ws(s.activation(le12[:], ld12[:], AF.Exp), sRs)

        @blk.vector
        def _(v):
            W = mkW(v, sCv)
            v.wait_ge(sRs.h, 1)
            W(v.tensor_sub(neglam[:], le12[:, 1:2], le12[:, 0:1]), sGv)
            W(v.tensor_scalar(neglam[:], neglam[:], LI, None, OP.subtract))

    with nc.Block("b0b") as blk:
        @blk.gpsimd
        def _(g):
            for dst, src in ((cq_bf, cq_h), (sq_bf, sq_h), (ck_bf, ck_h),
                             (sk_bf, sk_h), (wkt_bf, wkt_h), (wvt_bf, wvt_h)):
                sB0g.inc(g.dma_start(dst[:], src[:]), 16)
            g.wait_ge(sB0g.h, sB0g.n)

        @blk.sync
        def _(sy):
            for m in range(4):
                sXa[m].inc(sy.dma_start(xt[m][:], x_h[m * 128:(m + 1) * 128, :]), 16)

    if _STAGE < 1:
        emit_taps({})
        esA1.close(); esA.close(); es.close()
        return taps

    with nc.Block("p1") as blk:
        @blk.sync
        def _(sy):
            for m in range(4, NT):
                sy.wait_ge(sXn.h, m - 3)
                sXa[m % 4].inc(sy.dma_start(xt[m % 4][:],
                                            x_h[m * 128:(m + 1) * 128, :]), 16)

        @blk.vector
        def _(v):
            W = mkW(v, sCv)
            for m in range(NT):
                v.wait_ge(sXa[m % 4].h, 16 * (m // 4 + 1))
                if m >= 2:
                    v.wait_ge(sTp.h, m - 1)
                W(v.reduce_sum(ln_s1[:], xt[m % 4][:], axis=AX))
                W(v.scalar_tensor_tensor(
                    xnb[m % 2][:], xt[m % 4][:], 1.0, xt[m % 4][:],
                    OP.bypass, OP.mult, accum_out=ln_s2[:]), sLN)
                W(v.tensor_scalar(ln_mean[:], ln_s1[:], 1.0 / E, None, OP.mult))
                W(v.tensor_mul(ln_m2[:], ln_mean[:], ln_mean[:]))
                W(v.scalar_tensor_tensor(
                    ln_var[:], ln_s2[:], 1.0 / E, ln_m2[:], OP.mult, OP.subtract),
                  sSm)
                v.wait_ge(sRs.h, 1 + m + 1)
                W(v.tensor_scalar(xnb[m % 2][:], xt[m % 4][:], ln_mean[:],
                                  ln_rs[:], OP.subtract, OP.mult), sXn)
                if m >= 1:
                    v.wait_ge(sTp.h, m)
                    W(v.tensor_copy(xnT[:, :, (m - 1) * 128:m * 128], psT[:]), sEv)

        @blk.scalar
        def _(s):
            Ws = mkW(s, sCs)
            for m in range(NT):
                s.wait_ge(sSm.h, m + 1)
                Ws(s.activation(ln_ln[:], ln_var[:], AF.Ln, bias=cvec_sb[:, 0:1]),
                   sAct)
                Ws(s.activation(ln_rs[:], ln_ln[:], AF.Exp, scale=cvec_sb[:, 1:2]),
                   sRs)

        @blk.tensor
        def _(t):
            for m in range(NT):
                t.wait_ge(sXn.h, m + 1)
                if m >= 1:
                    t.wait_ge(sEv.h, m)
                for k in range(NEC):
                    ins = t.transpose(psT[:, k, :],
                                      xnb[m % 2][:, k * 128:(k + 1) * 128], iden_bf[:])
                sTp.inc(ins)

        @blk.vector
        def _(v):
            v.wait_ge(sTp.h, NT)
            sEv.inc(v.tensor_copy(xnT[:, :, (NT - 1) * 128:NT * 128], psT[:]))
            v.wait_ge(sEv.h, sEv.n)

    if _STAGE < 2:
        emit_taps(dict(xnT=xnT, neglam=neglam))
        esA1.close(); esA.close(); es.close()
        return taps

    gv0 = sGv.n
    with nc.Block("p1b") as blk:
        @blk.tensor
        def _(t):
            for m in range(NT):
                slot = m % 2
                psG = arena[:, slot * 512:slot * 512 + 8]
                if m >= 2:
                    t.wait_ge(sGv.h, gv0 + 7 * (m - 1))
                for dc in range(NEC):
                    ins = t.matmul(psG, xnT[:, dc, m * 128:(m + 1) * 128],
                                   wg_bf[:, dc * 8:(dc + 1) * 8],
                                   start=(dc == 0), stop=(dc == NEC - 1))
                sG.inc(ins)

        @blk.vector
        def _(v):
            W = mkW(v, sCv)
            for m in range(NT):
                slot = m % 2
                psG = arena[:, slot * 512:slot * 512 + 8]
                v.wait_ge(sG.h, m + 1)
                W(v.tensor_add(gsb[:], psG, bgbc_sb[:]), sGv)
                W(v.tensor_scalar(gesb[:], gsb[:], 1.0, None, OP.mult, OP.max,
                                  accum_out=top8[:, 0:1]), sGv)
                W(v.tensor_scalar(gesb[:], gsb[:], top8[:, 0:1], None, OP.is_ge),
                  sGv)
                W(v.scalar_tensor_tensor(gesb[:], gesb[:], -1e30, gsb[:],
                                         OP.mult, OP.add), sGv)
                W(v.tensor_scalar(gesb[:], gesb[:], 1.0, None, OP.mult, OP.max,
                                  accum_out=top8[:, 1:2]), sGv)
                W(v.tensor_scalar(gesb[:], gsb[:], top8[:, 1:2], None, OP.is_ge),
                  sGv)
                W(v.tensor_mul(wall[:, m, :], gsb[:], gesb[:]), sGv)

    if _STAGE < 3:
        emit_taps(dict(xnT=xnT, wall=wall, neglam=neglam))
        esA1.close(); esA.close(); es.close()
        return taps

    with nc.Block("p2") as blk:
        @blk.tensor
        def _(t):
            psK = arena[:, 0:2048]
            for p in range(NPAIR):
                if p >= 1:
                    t.wait_ge(sKc.h, p)
                for tc in range(4):
                    for ec in range(NEC):
                        ins = t.matmul(psK[:, tc * 512:(tc + 1) * 512],
                                       wkt_bf[:, ec, p * 128:(p + 1) * 128],
                                       xnT[:, ec, tc * 512:(tc + 1) * 512],
                                       start=(ec == 0), stop=(ec == NEC - 1))
                sKm.inc(ins)
            for m in range(NT):
                slot = m % 2
                psV = arena[:, 2048:3072] if slot == 0 else arena[:, 0:1024]
                if m < 2:
                    t.wait_ge(sKc.h, NPAIR)
                else:
                    t.wait_ge(sVc.h, m - 1)
                for jc in range(2):
                    for ec in range(NEC):
                        ins = t.matmul(psV[:, jc * 512:(jc + 1) * 512],
                                       xnT[:, ec, m * 128:(m + 1) * 128],
                                       wvt_bf[:, ec, jc * 512:(jc + 1) * 512],
                                       start=(ec == 0), stop=(ec == NEC - 1))
                sVm.inc(ins)

        @blk.vector
        def _(v):
            W = mkW(v, sCv)
            psK = arena[:, 0:2048]
            for p in range(NPAIR):
                v.wait_ge(sKm.h, p + 1)
                W(v.tensor_copy(qraw[:], psK), sKc)
                W(v.tensor_mul(t1b[:], qraw[:], ck_bf[:]))
                W(v.tensor_mul(usw[0:32, :], qraw[32:64, :], sk_bf[32:64, :]))
                W(v.tensor_mul(usw[32:64, :], qraw[0:32, :], sk_bf[0:32, :]))
                W(v.tensor_mul(usw[64:96, :], qraw[96:128, :], sk_bf[96:128, :]))
                W(v.tensor_mul(usw[96:128, :], qraw[64:96, :], sk_bf[64:96, :]))
                W(v.tensor_add(kbf[:, p, :], t1b[:], usw[:]), sKr)
                W(v.tensor_copy(kbf_sub[:, p, :], kbf[:, p, 0:2048:8]))
            for m in range(NT):
                slot = m % 2
                psV = arena[:, 2048:3072] if slot == 0 else arena[:, 0:1024]
                v.wait_ge(sVm.h, m + 1)
                W(v.tensor_copy(Vsb[:, m, :], psV), sVc)

    if _STAGE < 4:
        emit_taps(dict(xnT=xnT, wall=wall, kbf=kbf, V=Vsb, neglam=neglam))
        esA1.close(); esA.close(); es.close()
        return taps

    esA1.close()
    esU = ExitStack()
    UT = esU.enter_context(nc.sbuf_tensor("UT", [128, NPAIR, T], BF16, side="right"))

    esA2 = ExitStack()
    sbA2 = lambda *a, **k: esA2.enter_context(nc.sbuf_tensor(*a, **k))
    qbf = [sbA2(f"qbf{i}", [128, T], BF16) for i in range(2)]
    e0b = [sbA2(f"e0b{i}", [128, T], BF16) for i in range(2)]
    e1b = [sbA2(f"e1b{i}", [128, T], BF16) for i in range(2)]
    diffb = [sbA2(f"diffb{i}", [128, T], BF16) for i in range(3)]
    diffT = [sbA2(f"diffT{i}", [128, NT, 128], BF16) for i in range(2)]
    sums0 = sbA2("sums0", [128, NT], F32)
    nb_all = sbA2("nb_all", [128, 2 * NT], F32)
    csums = [sbA2(f"csums{i}", [128, 4], F32) for i in range(2)]
    sjunk = sbA2("sjunk", [128, 256], BF16)
    s01 = sbA2("s01", [128, 2], F32)
    s1t = sbA2("s1t", [128, 1], F32)
    ab = sbA2("ab", [128, NT], F32)
    rhot = sbA2("rhot", [128, 1], F32)
    nrho = [sbA2(f"nrho{i}", [128, 1], F32) for i in range(2)]
    sqb = sbA2("sqb", [128, 128], BF16)
    s0sq = sbA2("s0sq", [128, NT], F32)
    innerb = sbA2("innerb", [128, NT], F32)
    lnb = sbA2("lnb", [128, NT], F32)
    rstb = sbA2("rstb", [128, NT], F32)
    wqb = [sbA2(f"wqb{i}", [128, NEC, 128], BF16) for i in range(2)]

    psQ = arena[:, 0:2048]
    SLOT = [arena[:, 0:1024], arena[:, 1024:2048], arena[:, 2048:3072]]
    psU = arena[:, 3072:3200]
    psMsq = arena[0:1, 3200:3328]
    psSc = arena[:, 3328:3344]

    def p3_items():
        items = [("sub", 0), ("sub", 1)]
        for m in range(NT):
            if m + 2 < NT:
                items.append(("sub", m + 2))
            for c in range(4):
                items.append(("main", m, c))
        return items

    def p3_plan():
        plans = []
        slot_state = [None, None, None]
        n_sub = 0
        n_main = 0
        for p in range(NPAIR):
            plan = []
            for j, it in enumerate(p3_items()):
                k = len(plans) * 80 + j
                slot = k % 3
                waits = []
                prev = slot_state[slot]
                if prev is not None:
                    waits.append(prev)
                if j < 3:
                    waits.append(("sQc", p + 1))
                if it[0] == "sub":
                    n_sub += 1
                    slot_state[slot] = ("sSt0", 2 * n_sub)
                    plan.append((it, slot, waits, n_sub, None))
                else:
                    n_main += 1
                    slot_state[slot] = ("sExp", n_main)
                    plan.append((it, slot, waits, None, n_main))
            plans.append(plan)
        return plans

    P3PLANS = p3_plan()

    def tail_pieces(t, p, m, piece):
        mm = NT * p + m
        if piece == 0:
            t.wait_ge(sDiff.h, mm + 1)
            if mm >= 1:
                t.wait_ge(sTe.h, 2 * mm)
            for sc in range(8):
                ins = t.transpose(psT[:, sc, :],
                                  diffb[m % 3][:, sc * 128:(sc + 1) * 128],
                                  iden_bf[:])
            sTr.inc(ins)
        elif piece == 1:
            t.wait_ge(sTe.h, 2 * mm + 1)
            for sc in range(8):
                ins = t.transpose(psT[:, sc, :],
                                  diffb[m % 3][:, (8 + sc) * 128:(9 + sc) * 128],
                                  iden_bf[:])
            sTr.inc(ins)
        elif piece == 2:
            t.wait_ge(sTe.h, 2 * mm + 1)
            if m == 0 and p >= 1:
                t.wait_ge(sSm2.h, p)
            if mm >= 1:
                t.wait_ge(sSq.h, mm)
            for sc in range(8):
                t.matmul(psU[:], Vsb[:, sc, p * 128:(p + 1) * 128],
                         diffT[m % 2][:, sc, :],
                         start=(sc == 0), stop=False)
        elif piece == 3:
            t.wait_ge(sTe.h, 2 * mm + 2)
            for sc in range(8, NT):
                ins = t.matmul(psU[:], Vsb[:, sc, p * 128:(p + 1) * 128],
                               diffT[m % 2][:, sc, :],
                               start=False, stop=(sc == NT - 1))
            sPv.inc(ins)
        else:
            t.wait_ge(sSq.h, mm + 1)
            sMm.inc(t.matmul(psSc[:, m:m + 1], sqb[:], inv128_bf[:],
                             start=True, stop=True))

    with nc.Block("p3") as blk:
        @blk.gpsimd
        def _(g):
            sWqa[0].inc(g.dma_start(wqb[0][:], wqt_h[:, :, 0:128]), 16)
            for p in range(1, NPAIR):
                if p >= 2:
                    g.wait_ge(sQm.h, p - 1)
                sWqa[p % 2].inc(g.dma_start(wqb[p % 2][:],
                                            wqt_h[:, :, p * 128:(p + 1) * 128]), 16)

        @blk.tensor
        def _(t):
            semmap = {"sQc": sQc, "sSt0": sSt0, "sExp": sExp}

            def emit_item(item):
                (it, slot, waits, nsub, nmain) = item
                for (semname, val) in waits:
                    t.wait_ge(semmap[semname].h, val)
                if it[0] == "sub":
                    m = it[1]
                    if m < 2:
                        t.wait_ge(sQr.h, p + 1)
                    for h in range(2):
                        ins = t.matmul(
                            SLOT[slot][:, h * 512:h * 512 + 256],
                            qbf[p % 2][h * 64:(h + 1) * 64,
                                       m * 128:(m + 1) * 128],
                            kbf_sub[h * 64:(h + 1) * 64, p, :],
                            start=True, stop=True, tile_position=(64 * h, 0))
                    sSub.inc(ins)
                else:
                    m, c = it[1], it[2]
                    h = c // 2
                    half = c % 2
                    for sc in range(2):
                        col = half * 1024 + sc * 512
                        ins = t.matmul(
                            SLOT[slot][:, sc * 512:(sc + 1) * 512],
                            qbf[p % 2][h * 64:(h + 1) * 64,
                                       m * 128:(m + 1) * 128],
                            kbf[h * 64:(h + 1) * 64, p, col:col + 512],
                            start=True, stop=True, tile_position=(64 * h, 0))
                    sS0.inc(ins)

            for p in range(NPAIR):
                plan = P3PLANS[p]
                t.wait_ge(sWqa[p % 2].h, 16 * (p // 2 + 1))
                if p >= 1:
                    t.wait_ge(sExp.h, 64 * p)
                    t.wait_ge(sSt0.h, 32 * p)
                for tc in range(4):
                    for ec in range(NEC):
                        ins = t.matmul(psQ[:, tc * 512:(tc + 1) * 512],
                                       wqb[p % 2][:, ec, :],
                                       xnT[:, ec, tc * 512:(tc + 1) * 512],
                                       start=(ec == 0), stop=(ec == NEC - 1))
                sQm.inc(ins)
                by_tile = [[] for _ in range(NT)]
                head = []
                for item in plan:
                    it = item[0]
                    if it[0] == "sub" and it[1] < 2:
                        head.append(item)
                    elif it[0] == "sub":
                        by_tile[it[1] - 2].append(item)
                    else:
                        by_tile[it[1]].append(item)
                for item in head:
                    emit_item(item)
                for m in range(NT):
                    items = by_tile[m]
                    mains = items[-4:]
                    subs = items[:-4]
                    for item in subs:
                        emit_item(item)
                    emit_item(mains[0])
                    if m >= 2:
                        tail_pieces(t, p, m - 2, 0)
                    emit_item(mains[1])
                    if m >= 2:
                        tail_pieces(t, p, m - 2, 1)
                    emit_item(mains[2])
                    if m >= 2:
                        tail_pieces(t, p, m - 2, 2)
                    emit_item(mains[3])
                    if m >= 2:
                        tail_pieces(t, p, m - 2, 3)
                    if m >= 3:
                        tail_pieces(t, p, m - 3, 4)
                for piece in range(4):
                    tail_pieces(t, p, NT - 2, piece)
                tail_pieces(t, p, NT - 3, 4)
                for piece in range(4):
                    tail_pieces(t, p, NT - 1, piece)
                tail_pieces(t, p, NT - 2, 4)
                tail_pieces(t, p, NT - 1, 4)

        @blk.vector
        def _(v):
            W = mkW(v, sCv)
            nsub_g = 0
            for p in range(NPAIR):
                plan = P3PLANS[p]
                v.wait_ge(sQm.h, p + 1)
                W(v.tensor_copy(qraw[:], psQ), sQc)
                W(v.tensor_mul(t1b[:], qraw[:], cq_bf[:]))
                W(v.tensor_mul(usw[0:32, :], qraw[32:64, :], sq_bf[32:64, :]))
                W(v.tensor_mul(usw[32:64, :], qraw[0:32, :], sq_bf[0:32, :]))
                W(v.tensor_mul(usw[64:96, :], qraw[96:128, :], sq_bf[96:128, :]))
                W(v.tensor_mul(usw[96:128, :], qraw[64:96, :], sq_bf[64:96, :]))
                W(v.tensor_add(qbf[p % 2][:], t1b[:], usw[:]), sQr)
                subslot = {it[1]: slot for (it, slot, w_, ns, nm) in plan
                           if it[0] == "sub"}
                for m in range(NT):
                    mm = NT * p + m
                    if m >= 2:
                        v.wait_ge(sTr.h, 2 * (mm - 2) + 1)
                        W(v.tensor_copy(diffT[(m - 2) % 2][:, 0:8, :], psT[:]),
                          sTe)
                        v.wait_ge(sTr.h, 2 * (mm - 2) + 2)
                        W(v.tensor_copy(diffT[(m - 2) % 2][:, 8:16, :], psT[:]),
                          sTe)
                    for msub in ([0, 1] if m == 0 else []) + \
                            ([m + 2] if m + 2 < NT else []):
                        nsub_g += 1
                        v.wait_ge(sSub.h, nsub_g)
                        if p >= 1:
                            v.wait_ge(sExp.h, 64 * (p - 1) + 4 * (msub + 1))
                        sl = subslot[msub]
                        for h in range(2):
                            W(v.tensor_scalar(
                                sjunk[:], SLOT[sl][:, h * 512:h * 512 + 256],
                                -1.0, None, OP.mult, OP.min,
                                accum_out=nb_all[:, h * NT + msub:
                                                 h * NT + msub + 1]), sSt0)
                    v.wait_ge(sExp.h, 64 * p + 4 * m + 4)
                    W(v.tensor_tensor(s01[:], csums[m % 2][:, 0:4:2],
                                      csums[m % 2][:, 1:4:2], OP.add))
                    W(v.tensor_copy(sums0[:, m:m + 1], s01[:, 0:1]))
                    W(v.tensor_scalar(s1t[:], s01[:, 1:2], 2.0 ** -40, None,
                                      OP.mult))
                    W(v.reciprocal(rhot[:], s1t[:]))
                    W(v.tensor_mul(rhot[:], rhot[:], s01[:, 0:1]))
                    if mm >= 3:
                        v.wait_ge(sTr.h, 2 * (mm - 3) + 2)
                    W(v.tensor_scalar(nrho[m % 2][:], rhot[:], neglam[:],
                                      2.0 ** -40, OP.mult, OP.mult))
                    W(v.tensor_scalar(t1b[:], e1b[m % 2][:], nrho[m % 2][:],
                                      None, OP.mult))
                    W(v.tensor_add(diffb[m % 3][:], t1b[:], e0b[m % 2][:]), sDiff)
                for mf in (NT - 2, NT - 1):
                    mmf = NT * p + mf
                    v.wait_ge(sTr.h, 2 * mmf + 1)
                    W(v.tensor_copy(diffT[mf % 2][:, 0:8, :], psT[:]), sTe)
                    v.wait_ge(sTr.h, 2 * mmf + 2)
                    W(v.tensor_copy(diffT[mf % 2][:, 8:16, :], psT[:]), sTe)
                v.wait_ge(sMm.h, NT * (p + 1))
                W(v.tensor_scalar(s0sq[:], sums0[:], 2.0 ** -40, None, OP.mult))
                W(v.reciprocal(ab[:], s0sq[:]))
                W(v.tensor_mul(s0sq[:], ab[:], ab[:]))
                W(v.tensor_tensor(innerb[:], psSc[:], s0sq[:], OP.mult))
                W(v.tensor_scalar(innerb[:], innerb[:], 2.0 ** -30, None, OP.mult),
                  sSm2)
                v.wait_ge(sRst.h, p + 1)
                W(v.tensor_mul(innerb[:], ab[:], rstb[:]))
                W(v.tensor_scalar(scales[:, p, :], innerb[:],
                                  (1.0 - LI) * 2.0 ** -40, None, OP.mult))

        @blk.scalar
        def _(s):
            Ws = mkW(s, sCs)
            for p in range(NPAIR):
                plan = P3PLANS[p]
                for (it, slot, waits, nsub, nmain) in plan:
                    if it[0] != "main":
                        continue
                    m, c = it[1], it[2]
                    mm = NT * p + m
                    h = c // 2
                    half = c % 2
                    s.wait_ge(sS0.h, nmain)
                    s.wait_ge(sSt0.h, 2 * (NT * p + m + 1))
                    if mm >= 2 and c == 0:
                        s.wait_ge(sDiff.h, mm - 1)
                    ebuf = e0b[m % 2] if h == 0 else e1b[m % 2]
                    Ws(s.activation(ebuf[:, half * 1024:(half + 1) * 1024],
                                    SLOT[slot][:], AF.Exp,
                                    bias=nb_all[:, h * NT + m:h * NT + m + 1],
                                    accum_out=csums[m % 2][:, c:c + 1]), sExp)
                    if c == 3 and m >= 2:
                        s.wait_ge(sPv.h, mm - 1)
                        Ws(s.activation(UT[:, p, (m - 2) * 128:(m - 1) * 128],
                                        psU[:], AF.Identity))
                        if mm >= 3:
                            s.wait_ge(sMm.h, mm - 2)
                        Ws(s.activation(sqb[:], psU[:], AF.Square,
                                        scale=2.0 ** -25), sSq)
                for mf in (NT - 2, NT - 1):
                    mmf = NT * p + mf
                    s.wait_ge(sPv.h, mmf + 1)
                    Ws(s.activation(UT[:, p, mf * 128:(mf + 1) * 128], psU[:],
                                    AF.Identity))
                    s.wait_ge(sMm.h, mmf)
                    Ws(s.activation(sqb[:], psU[:], AF.Square, scale=2.0 ** -25),
                       sSq)
                s.wait_ge(sSm2.h, p + 1)
                Ws(s.activation(lnb[:], innerb[:], AF.Ln, bias=cvec_sb[:, 0:1]),
                   sAct)
                Ws(s.activation(rstb[:], lnb[:], AF.Exp, scale=cvec_sb[:, 1:2]),
                   sRst)

    if _STAGE < 5:
        emit_taps(dict(xnT=xnT, wall=wall, kbf=kbf, V=Vsb, UT=UT, scales=scales,
                       sums0=sums0, qbf0=qbf[0],
                       e0b=e0b[0], e1b=e1b[0], neglam=neglam))
        esA2.close(); esU.close(); esA.close(); es.close()
        return taps

    esA2.close(); esA.close()
    esC = ExitStack()
    sbC = lambda *a, **k: esC.enter_context(nc.sbuf_tensor(*a, **k))
    acc = sbC("acc", [128, NT, E], F32)
    wob = sbC("wob", [128, E], BF16)
    esD = ExitStack()
    sbD = lambda *a, **k: esD.enter_context(nc.sbuf_tensor(*a, **k))
    w1b = sbD("w1b", [128, NEC, E], BF16)
    w2b = sbD("w2b", [128, NEC, E], BF16)
    hT = sbD("hT", [128, NEC, T], BF16)
    b2e = sbD("b2e", [1, E], BF16)

    with nc.Block("p45") as blk:
        @blk.gpsimd
        def _(g):
            sWoa[0].inc(g.dma_start(wob[:], wot_h[:, 0, :]), 16)
            sW1a[0].inc(g.dma_start(w1b[:], w1_h[0]), 16)
            sB2a[0].inc(g.dma_start(b2e[:], b2_h[0:1, 0:E]), 16)
            sW2a[0].inc(g.dma_start(w2b[:], w2_h[0]), 16)
            for e in range(1, NEXP):
                g.wait_ge(sWm.h, 32 * e)
                sWoa[0].inc(g.dma_start(wob[:], wot_h[:, e, :]), 16)
                g.wait_ge(sM1h.h, 16 * e)
                sW1a[0].inc(g.dma_start(w1b[:], w1_h[e]), 16)
                g.wait_ge(sM2y.h, 32 * e)
                sB2a[0].inc(g.dma_start(b2e[:], b2_h[0:1, e * E:(e + 1) * E]),
                            16)
                sW2a[0].inc(g.dma_start(w2b[:], w2_h[e]), 16)

        @blk.tensor
        def _(t):
            for e in range(NEXP):
                t.wait_ge(sW1a[0].h, 16 * (e + 1))
                t.wait_ge(sWoa[0].h, 16 * (e + 1))
                for hd in range(NEC):
                    for th in range(2):
                        grp = hd * 2 + th
                        i = e * 16 + grp
                        slot = grp % 2
                        psH = arena[:, slot * 1024:(slot + 1) * 1024]
                        if i >= 2:
                            t.wait_ge(sRelu.h, i - 1)
                        for half in range(2):
                            col = th * 1024 + half * 512
                            for dc in range(NEC):
                                ins = t.matmul(
                                    psH[:, half * 512:(half + 1) * 512],
                                    w1b[:, dc, hd * 128:(hd + 1) * 128],
                                    xnT[:, dc, col:col + 512],
                                    start=(dc == 0), stop=(dc == NEC - 1))
                        sM1h.inc(ins)
                        for j in (2 * grp, 2 * grp + 1):
                            wj = e * 32 + j
                            m, oc = j // 2, j % 2
                            sl = j % 3
                            psW = arena[:, 2048 + sl * 512:
                                        2048 + (sl + 1) * 512]
                            if j == 0 and e >= 1:
                                t.wait_ge(sAcc.h, 32 * e)
                            if wj >= 3:
                                t.wait_ge(sWacc.h, wj - 2)
                            sWm.inc(t.matmul(
                                psW, UT[:, e, m * 128:(m + 1) * 128],
                                wob[:, oc * 512:(oc + 1) * 512],
                                start=True, stop=True))
                t.wait_ge(sRelu.h, 16 * (e + 1))
                t.wait_ge(sW2a[0].h, 16 * (e + 1))
                t.wait_ge(sB2a[0].h, 16 * (e + 1))
                t.wait_ge(sWacc.h, 32 * (e + 1))
                for m in range(NT):
                    for oc in range(2):
                        i = e * 32 + m * 2 + oc
                        slot = (m * 2 + oc) % 2
                        psY = arena[:, 2048 + slot * 512:
                                    2048 + (slot + 1) * 512]
                        if i >= 2:
                            t.wait_ge(sAcc.h, i - 1)
                        for hc in range(NEC):
                            t.matmul(psY, hT[:, hc, m * 128:(m + 1) * 128],
                                     w2b[:, hc, oc * 512:(oc + 1) * 512],
                                     start=(hc == 0), stop=False)
                        ins = t.matmul(psY, ones_bf[0:1, 0:128],
                                       b2e[0:1, oc * 512:(oc + 1) * 512],
                                       start=False, stop=True)
                        sM2y.inc(ins)

        @blk.scalar
        def _(s):
            Ws = mkW(s, sCs)
            for e in range(NEXP):
                for hd in range(NEC):
                    for th in range(2):
                        i = e * 16 + hd * 2 + th
                        slot = (hd * 2 + th) % 2
                        psH = arena[:, slot * 1024:(slot + 1) * 1024]
                        s.wait_ge(sM1h.h, i + 1)
                        if e >= 1:
                            s.wait_ge(sM2y.h, 32 * e)
                        Ws(s.activation(
                            hT[:, hd, th * 1024:(th + 1) * 1024], psH, AF.Relu,
                            bias=b1c_sb[:, e * NEC + hd:e * NEC + hd + 1]),
                           sRelu)

        @blk.vector
        def _(v):
            W = mkW(v, sCv)
            for e in range(NEXP):
                for j in range(32):
                    m, oc = j // 2, j % 2
                    sl = j % 3
                    psW = arena[:, 2048 + sl * 512:2048 + (sl + 1) * 512]
                    v.wait_ge(sWm.h, e * 32 + j + 1)
                    dst = acc[:, m, oc * 512:(oc + 1) * 512]
                    if e == 0:
                        ins = v.tensor_scalar(dst, psW, scales[:, 0, m:m + 1],
                                              None, OP.mult)
                    else:
                        ins = v.scalar_tensor_tensor(
                            dst, psW, scales[:, e, m:m + 1], dst,
                            OP.mult, OP.add)
                    W(ins, sWacc)
                for m in range(NT):
                    for oc in range(2):
                        i = e * 32 + m * 2 + oc
                        slot = (m * 2 + oc) % 2
                        psY = arena[:, 2048 + slot * 512:
                                    2048 + (slot + 1) * 512]
                        v.wait_ge(sM2y.h, i + 1)
                        dst = acc[:, m, oc * 512:(oc + 1) * 512]
                        W(v.scalar_tensor_tensor(
                            dst, psY, wall[:, m, e:e + 1], dst, OP.mult,
                            OP.add), sAcc)
    esU.close()

    if _STAGE < 7:
        emit_taps(dict(acc=acc, hT=hT, wall=wall))
        esD.close(); esC.close(); es.close()
        return taps

    esD.close()
    esE = ExitStack()
    xr = [esE.enter_context(nc.sbuf_tensor(f"xr{i}", [128, E], F32))
          for i in range(4)]

    with nc.Block("p6") as blk:
        @blk.sync
        def _(sy):
            for m in range(NT):
                if m >= 4:
                    sy.wait_ge(sYa[m % 4].h, 16 * (m // 4))
                sXra[m % 4].inc(sy.dma_start(xr[m % 4][:],
                                             x_h[m * 128:(m + 1) * 128, :]), 16)

        @blk.vector
        def _(v):
            W = mkW(v, sCv)
            for m in range(NT):
                v.wait_ge(sXra[m % 4].h, 16 * (m // 4 + 1))
                W(v.tensor_add(xr[m % 4][:], xr[m % 4][:], acc[:, m, :]), sXn)

        @blk.scalar
        def _(s):
            for m in range(NT):
                s.wait_ge(sXn.h, NT + m + 1)
                sYa[m % 4].inc(s.dma_start(y_h[m * 128:(m + 1) * 128, :],
                                           xr[m % 4][:]), 16)
            for i in range(4):
                s.wait_ge(sYa[i].h, sYa[i].n)

    emit_taps(dict(acc=acc))
    esE.close(); esC.close(); es.close()
    return taps


def kernel(**inputs):
    x_full = np.asarray(inputs["L0_attn_x"], dtype=np.float32)
    d = _host_prep(inputs)

    nc = bass.Bass("TRN2", target_bir_lowering=False)
    build(nc)

    in_maps = []
    for c in range(8):
        m = dict(d)
        m["x"] = np.ascontiguousarray(x_full[c])
        in_maps.append(m)

    res = run_bass_kernel_spmd(nc, in_maps, core_ids=list(range(8)))
    out = np.stack([res.results[c]["y"] for c in range(8)], axis=0)
    if _TAPS:
        kernel.last_taps = [{t: res.results[c]["tap_" + t] for t in _TAPS}
                            for c in range(8)]
    return out


# version 16
# speedup vs baseline: 1.3203x; 1.0009x over previous
import math
from contextlib import ExitStack

import numpy as np

import concourse.bass as bass
import concourse.mybir as mybir
from concourse.bass_utils import run_bass_kernel_spmd

F32 = mybir.dt.float32
BF16 = mybir.dt.bfloat16
AF = mybir.ActivationFunctionType
OP = mybir.AluOpType
AX = mybir.AxisListType.X

T = 2048
E = 1024
NT = 16
NEC = 8
NPAIR = 8
NEXP = 8
LI = 0.8 - 0.6 * math.exp(-0.3)

_STAGE = 99
_TAPS = []


class SemT:
    def __init__(self, h):
        self.h = h
        self.n = 0

    def inc(self, instr, k=1):
        instr.then_inc(self.h, k)
        self.n += k
        return instr


def _host_prep(inputs):
    Wq = np.asarray(inputs["L0_attn_Wq"], dtype=np.float32)
    Wk = np.asarray(inputs["L0_attn_Wk"], dtype=np.float32)
    Wv = np.asarray(inputs["L0_attn_Wv"], dtype=np.float32)
    Wo = np.asarray(inputs["L0_attn_Wo"], dtype=np.float32)

    def permute_qk(W):
        Wr = W.reshape(16, 32, 2, E)
        Wp = np.concatenate([Wr[:, :, 0, :], Wr[:, :, 1, :]], axis=1)
        Wt = Wp.reshape(E, E).T
        return np.ascontiguousarray(Wt.reshape(NEC, 128, E).transpose(1, 0, 2))

    d = {}
    d["wqt"] = permute_qk(Wq)
    d["wkt"] = permute_qk(Wk)
    d["wvt"] = np.ascontiguousarray(Wv.T.reshape(NEC, 128, E).transpose(1, 0, 2))
    d["wot"] = np.ascontiguousarray(Wo.T.reshape(NPAIR, 128, E).transpose(1, 0, 2))

    dd = 64
    inv_freq = 1.0 / (10000.0 ** (np.arange(0, dd, 2, dtype=np.float64) / dd))
    pos = np.arange(T, dtype=np.float64)
    freqs = np.repeat(pos[:, None] * inv_freq[None, :], 2, axis=-1)
    sv = (np.arange(0, dd, 2, dtype=np.float64) + 0.4 * dd) / (1.4 * dd)
    power = (pos - T // 2) / 512.0
    scale = sv[None, :] ** power[:, None]
    scale = np.concatenate([scale, scale], axis=-1)
    perm = np.concatenate([np.arange(0, 64, 2), np.arange(1, 64, 2)])
    swap = (np.arange(64) + 32) % 64
    sign = np.where(np.arange(64) < 32, -1.0, 1.0)
    cos_p = np.cos(freqs)[:, perm].T
    sin_p = np.sin(freqs)[:, perm].T
    sc_p = scale[:, perm].T

    def dup(t):
        return np.ascontiguousarray(np.concatenate([t, t], axis=0), dtype=np.float32)

    d["cq"] = dup(cos_p * sc_p * 0.125)
    d["sq"] = dup(((sin_p * sc_p * 0.125) * sign[:, None])[swap, :])
    d["ck"] = dup(cos_p / sc_p)
    d["sk"] = dup(((sin_p / sc_p) * sign[:, None])[swap, :])

    Wg = np.asarray(inputs["L0_ffn_W_gate"], np.float32)
    d["wg"] = np.ascontiguousarray(
        Wg.reshape(NEC, 128, NEXP).transpose(1, 0, 2).reshape(128, NEC * NEXP))
    d["bgbc"] = np.ascontiguousarray(
        np.broadcast_to(np.asarray(inputs["L0_ffn_b_gate"], np.float32)[None, :],
                        (128, NEXP)))
    W1 = np.asarray(inputs["L0_ffn_W1"], np.float32)
    d["w1"] = np.ascontiguousarray(W1.reshape(NEXP, NEC, 128, E).transpose(0, 2, 1, 3))
    W2 = np.asarray(inputs["L0_ffn_W2"], np.float32)
    d["w2"] = np.ascontiguousarray(W2.reshape(NEXP, NEC, 128, E).transpose(0, 2, 1, 3))
    b1 = np.asarray(inputs["L0_ffn_b1"], np.float32)
    d["b1c"] = np.ascontiguousarray(
        b1.reshape(NEXP, NEC, 128).transpose(2, 0, 1).reshape(128, NEXP * NEC))
    d["b2"] = np.ascontiguousarray(
        np.asarray(inputs["L0_ffn_b2"], np.float32).reshape(1, -1))

    lam = np.concatenate([
        np.asarray(inputs["L0_attn_lambda_q1"], np.float32),
        np.asarray(inputs["L0_attn_lambda_k1"], np.float32),
        np.asarray(inputs["L0_attn_lambda_q2"], np.float32),
        np.asarray(inputs["L0_attn_lambda_k2"], np.float32)])
    d["lam"] = np.ascontiguousarray(np.broadcast_to(lam[None, :], (128, 256)))

    d["cvec"] = np.ascontiguousarray(np.broadcast_to(
        np.array([1e-5, -0.5, -1.0, 1.0], np.float32)[None, :], (128, 4)))
    d["iden"] = np.eye(128, dtype=np.float32)
    d["ones"] = np.ones((1, 512), dtype=np.float32)
    d["inv128"] = np.full((128, 1), 1.0 / 128.0, dtype=np.float32)
    return d


def build(nc):
    taps = {}

    def dram(name, shape, dtype=F32, kind="ExternalInput"):
        return nc.dram_tensor(name, shape, dtype, kind=kind).ap()

    x_h = dram("x", [T, E])
    wqt_h = dram("wqt", [128, NEC, E])
    wkt_h = dram("wkt", [128, NEC, E])
    wvt_h = dram("wvt", [128, NEC, E])
    wot_h = dram("wot", [128, NPAIR, E])
    wg_h = dram("wg", [128, NEC * NEXP])
    bgbc_h = dram("bgbc", [128, NEXP])
    w1_h = dram("w1", [NEXP, 128, NEC, E])
    w2_h = dram("w2", [NEXP, 128, NEC, E])
    b1c_h = dram("b1c", [128, NEXP * NEC])
    b2_h = dram("b2", [1, NEXP * E])
    lam_h = dram("lam", [128, 256])
    cq_h = dram("cq", [128, T])
    sq_h = dram("sq", [128, T])
    ck_h = dram("ck", [128, T])
    sk_h = dram("sk", [128, T])
    iden_h = dram("iden", [128, 128])
    ones_h = dram("ones", [1, 512])
    inv128_h = dram("inv128", [128, 1])
    cvec_h = dram("cvec", [128, 4])
    y_h = dram("y", [T, E], kind="ExternalOutput")
    tap_shapes = {
        "xnT": [128, NEC, T], "wall": [128, NT, NEXP], "kbf": [128, NPAIR, T],
        "V": [128, NT, E], "UT": [128, NPAIR, T], "scales": [128, NPAIR, NT],
        "acc": [128, NT, E], "qbf0": [128, T], "e0b": [128, T], "e1b": [128, T],
        "sums0": [128, NT], "s1a": [128, NT], "s1b": [128, NT],
        "hT": [128, NEC, T], "neglam": [128, 1],
    }
    for tname in _TAPS:
        taps[tname] = dram("tap_" + tname, tap_shapes[tname], F32,
                           kind="ExternalOutput")

    es = ExitStack()
    S = lambda name: SemT(es.enter_context(nc.semaphore(name)))

    sX = S("sX"); sLN = S("sLN"); sSm = S("sSm"); sRs = S("sRs"); sXn = S("sXn")
    sTp = S("sTp"); sEv = S("sEv")
    sG = S("sG"); sGv = S("sGv")
    sKm = S("sKm"); sKc = S("sKc"); sVm = S("sVm"); sVc = S("sVc")
    sWq = S("sWq"); sQm = S("sQm"); sQc = S("sQc"); sQr = S("sQr"); sKr = S("sKr"); sSub = S("sSub")
    sS0 = S("sS0"); sS1a = S("sS1a"); sS1b = S("sS1b")
    sSt0 = S("sSt0"); sSt1 = S("sSt1"); sExp = S("sExp")
    sRho1 = S("sRho1"); sRl = S("sRl"); sRho2 = S("sRho2")
    sDiff = S("sDiff"); sTr = S("sTr"); sTe = S("sTe")
    sPv = S("sPv"); sSq = S("sSq"); sMm = S("sMm"); sMc = S("sMc")
    sScT = S("sScT"); sSm2 = S("sSm2"); sAct = S("sAct"); sRst = S("sRst")
    sWo = S("sWo"); sWm = S("sWm"); sWacc = S("sWacc")
    sW1L = S("sW1L"); sW2L = S("sW2L"); sM1h = S("sM1h"); sRelu = S("sRelu")
    sM2y = S("sM2y"); sAcc = S("sAcc"); sB2L = S("sB2L")
    sXr = S("sXr"); sY = S("sY"); sB0 = S("sB0"); sB0g = S("sB0g"); sTap = S("sTap")
    sXa = [S(f"sXa{i}") for i in range(4)]
    sWqa = [S(f"sWqa{i}") for i in range(2)]
    sWoa = [S(f"sWoa{i}") for i in range(2)]
    sW1a = [S(f"sW1a{i}") for i in range(2)]
    sW2a = [S(f"sW2a{i}") for i in range(2)]
    sB2a = [S(f"sB2a{i}") for i in range(2)]
    sXra = [S(f"sXra{i}") for i in range(4)]
    sYa = [S(f"sYa{i}") for i in range(4)]
    sCv = S("sCv"); sCs = S("sCs")

    def mkW(eng, chain_sem):
        def W(instr, sem=None):
            s = sem if sem is not None else chain_sem
            s.inc(instr)
            eng.wait_ge(s.h, s.n)
            return instr
        return W

    def emit_taps(available):
        live = [(n, taps[n], available[n]) for n in taps if n in available]
        if not live:
            return
        with nc.Block("taps") as blk:
            @blk.gpsimd
            def _(g):
                for name, hap, src in live:
                    sTap.inc(g.dma_start(hap[:], src[:]), 16)
                g.wait_ge(sTap.h, sTap.n)

    sb = lambda *a, **k: es.enter_context(nc.sbuf_tensor(*a, **k))

    xnT = sb("xnT", [128, NEC, T], BF16)
    wall = sb("wall", [128, NT, NEXP], F32)
    scales = sb("scales", [128, NPAIR, NT], F32)
    neglam = sb("neglam", [128, 1], F32)
    iden_bf = sb("iden_bf", [128, 128], BF16)
    ones_bf = sb("ones_bf", [1, 512], BF16)
    inv128_bf = sb("inv128_bf", [128, 1], BF16)
    b1c_sb = sb("b1c_sb", [128, NEXP * NEC], F32)
    cvec_sb = sb("cvec_sb", [128, 4], F32)

    arena = es.enter_context(nc.psum_tensor("arena", [128, 3584], F32))
    psT = es.enter_context(nc.psum_tensor("psT", [128, 8, 128], BF16))

    esA = ExitStack()
    sbA = lambda *a, **k: esA.enter_context(nc.sbuf_tensor(*a, **k))
    kbf = sbA("kbf", [128, NPAIR, T], BF16)
    kbf_sub = sbA("kbf_sub", [128, NPAIR, 256], BF16)
    Vsb = sbA("V", [128, NT, E], BF16)
    cq_bf = sbA("cq_bf", [128, T], BF16)
    sq_bf = sbA("sq_bf", [128, T], BF16)
    qraw = sbA("qraw", [128, T], BF16)
    t1b = sbA("t1b", [128, T], BF16)
    usw = sbA("usw", [128, T], BF16)

    esA1 = ExitStack()
    sbA1 = lambda *a, **k: esA1.enter_context(nc.sbuf_tensor(*a, **k))
    ck_bf = sbA1("ck_bf", [128, T], BF16)
    sk_bf = sbA1("sk_bf", [128, T], BF16)
    wkt_bf = sbA1("wkt_bf", [128, NEC, E], BF16)
    wvt_bf = sbA1("wvt_bf", [128, NEC, E], BF16)
    xt = [sbA1(f"xt{i}", [128, E], F32) for i in range(4)]
    xnb = [sbA1(f"xnb{i}", [128, E], BF16) for i in range(2)]
    ln_s1 = sbA1("ln_s1", [128, 1], F32)
    ln_s2 = sbA1("ln_s2", [128, 1], F32)
    ln_mean = sbA1("ln_mean", [128, 1], F32)
    ln_m2 = sbA1("ln_m2", [128, 1], F32)
    ln_var = sbA1("ln_var", [128, 1], F32)
    ln_ln = sbA1("ln_ln", [128, 1], F32)
    ln_rs = sbA1("ln_rs", [128, 1], F32)
    gsb = sbA1("gsb", [128, NEXP], F32)
    top8 = sbA1("top8", [128, NEXP], F32)
    gesb = sbA1("gesb", [128, NEXP], F32)
    wg_bf = sbA1("wg_bf", [128, NEC * NEXP], BF16)
    bgbc_sb = sbA1("bgbc_sb", [128, NEXP], F32)
    lam_sb = sbA1("lam_sb", [128, 256], F32)
    lprod = sbA1("lprod", [128, 64], F32)
    ld12 = sbA1("ld12", [128, 2], F32)
    le12 = sbA1("le12", [128, 2], F32)

    with nc.Block("b0") as blk:
        @blk.sync
        def _(sy):
            sB0.inc(sy.dma_start(bgbc_sb[:], bgbc_h[:]), 16)
            sB0.inc(sy.dma_start(lam_sb[:], lam_h[:]), 16)
            sB0.inc(sy.dma_start(b1c_sb[:], b1c_h[:]), 16)
            sB0.inc(sy.dma_start(cvec_sb[:], cvec_h[:]), 16)
            sy.wait_ge(sB0.h, sB0.n)

        @blk.gpsimd
        def _(g):
            for dst, src in ((iden_bf, iden_h), (ones_bf, ones_h),
                             (inv128_bf, inv128_h), (wg_bf, wg_h)):
                sB0g.inc(g.dma_start(dst[:], src[:]), 16)
            g.wait_ge(sB0g.h, sB0g.n)

        @blk.vector
        def _(v):
            W = mkW(v, sCv)
            v.wait_ge(sB0.h, 64)
            W(v.tensor_mul(lprod[:], lam_sb[:, 0:64], lam_sb[:, 64:128]))
            W(v.reduce_sum(ld12[:, 0:1], lprod[:], axis=AX))
            W(v.tensor_mul(lprod[:], lam_sb[:, 128:192], lam_sb[:, 192:256]))
            W(v.reduce_sum(ld12[:, 1:2], lprod[:], axis=AX), sGv)

        @blk.scalar
        def _(s):
            s.wait_ge(sGv.h, 1)
            Ws = mkW(s, sCs)
            Ws(s.activation(le12[:], ld12[:], AF.Exp), sRs)

        @blk.vector
        def _(v):
            W = mkW(v, sCv)
            v.wait_ge(sRs.h, 1)
            W(v.tensor_sub(neglam[:], le12[:, 1:2], le12[:, 0:1]), sGv)
            W(v.tensor_scalar(neglam[:], neglam[:], LI, None, OP.subtract))

    with nc.Block("b0b") as blk:
        @blk.gpsimd
        def _(g):
            for dst, src in ((cq_bf, cq_h), (sq_bf, sq_h), (ck_bf, ck_h),
                             (sk_bf, sk_h), (wkt_bf, wkt_h), (wvt_bf, wvt_h)):
                sB0g.inc(g.dma_start(dst[:], src[:]), 16)
            g.wait_ge(sB0g.h, sB0g.n)

        @blk.sync
        def _(sy):
            for m in range(4):
                sXa[m].inc(sy.dma_start(xt[m][:], x_h[m * 128:(m + 1) * 128, :]), 16)

    if _STAGE < 1:
        emit_taps({})
        esA1.close(); esA.close(); es.close()
        return taps

    gv0 = sGv.n

    def gate_comb2(v, W, m):
        slot = m % 2
        psG = arena[:, slot * 512:slot * 512 + 8]
        v.wait_ge(sG.h, m + 1)
        W(v.tensor_add(gsb[:], psG, bgbc_sb[:]), sGv)
        W(v.tensor_scalar(gesb[:], gsb[:], 1.0, None, OP.mult, OP.max,
                          accum_out=top8[:, 0:1]), sGv)
        W(v.tensor_scalar(gesb[:], gsb[:], top8[:, 0:1], None, OP.is_ge), sGv)
        W(v.scalar_tensor_tensor(gesb[:], gesb[:], -1e30, gsb[:],
                                 OP.mult, OP.add), sGv)
        W(v.tensor_scalar(gesb[:], gesb[:], 1.0, None, OP.mult, OP.max,
                          accum_out=top8[:, 1:2]), sGv)
        W(v.tensor_scalar(gesb[:], gsb[:], top8[:, 1:2], None, OP.is_ge), sGv)
        W(v.tensor_mul(wall[:, m, :], gsb[:], gesb[:]), sGv)

    with nc.Block("p1") as blk:
        @blk.sync
        def _(sy):
            for m in range(4, NT):
                sy.wait_ge(sXn.h, m - 3)
                sXa[m % 4].inc(sy.dma_start(xt[m % 4][:],
                                            x_h[m * 128:(m + 1) * 128, :]), 16)

        @blk.vector
        def _(v):
            W = mkW(v, sCv)
            def gate_comb(m):
                slot = m % 2
                psG = arena[:, slot * 512:slot * 512 + 8]
                v.wait_ge(sG.h, m + 1)
                W(v.tensor_add(gsb[:], psG, bgbc_sb[:]), sGv)
                W(v.tensor_scalar(gesb[:], gsb[:], 1.0, None, OP.mult, OP.max,
                                  accum_out=top8[:, 0:1]), sGv)
                W(v.tensor_scalar(gesb[:], gsb[:], top8[:, 0:1], None, OP.is_ge),
                  sGv)
                W(v.scalar_tensor_tensor(gesb[:], gesb[:], -1e30, gsb[:],
                                         OP.mult, OP.add), sGv)
                W(v.tensor_scalar(gesb[:], gesb[:], 1.0, None, OP.mult, OP.max,
                                  accum_out=top8[:, 1:2]), sGv)
                W(v.tensor_scalar(gesb[:], gsb[:], top8[:, 1:2], None, OP.is_ge),
                  sGv)
                W(v.tensor_mul(wall[:, m, :], gsb[:], gesb[:]), sGv)
            for m in range(NT):
                v.wait_ge(sXa[m % 4].h, 16 * (m // 4 + 1))
                if m >= 2:
                    v.wait_ge(sTp.h, m - 1)
                W(v.reduce_sum(ln_s1[:], xt[m % 4][:], axis=AX))
                W(v.scalar_tensor_tensor(
                    xnb[m % 2][:], xt[m % 4][:], 1.0, xt[m % 4][:],
                    OP.bypass, OP.mult, accum_out=ln_s2[:]), sLN)
                W(v.tensor_scalar(ln_mean[:], ln_s1[:], 1.0 / E, None, OP.mult))
                W(v.tensor_mul(ln_m2[:], ln_mean[:], ln_mean[:]))
                W(v.scalar_tensor_tensor(
                    ln_var[:], ln_s2[:], 1.0 / E, ln_m2[:], OP.mult, OP.subtract),
                  sSm)
                v.wait_ge(sRs.h, 1 + m + 1)
                W(v.tensor_scalar(xnb[m % 2][:], xt[m % 4][:], ln_mean[:],
                                  ln_rs[:], OP.subtract, OP.mult), sXn)
                if m >= 1:
                    v.wait_ge(sTp.h, m)
                    W(v.tensor_copy(xnT[:, :, (m - 1) * 128:m * 128], psT[:]), sEv)
                if m >= 1:
                    gate_comb(m - 1)

        @blk.scalar
        def _(s):
            Ws = mkW(s, sCs)
            for m in range(NT):
                s.wait_ge(sSm.h, m + 1)
                Ws(s.activation(ln_ln[:], ln_var[:], AF.Ln, bias=cvec_sb[:, 0:1]),
                   sAct)
                Ws(s.activation(ln_rs[:], ln_ln[:], AF.Exp, scale=cvec_sb[:, 1:2]),
                   sRs)

        @blk.tensor
        def _(t):
            def gate_mm(m):
                slot = m % 2
                psG = arena[:, slot * 512:slot * 512 + 8]
                t.wait_ge(sEv.h, m + 1)
                if m >= 2:
                    t.wait_ge(sGv.h, gv0 + 7 * (m - 1))
                for dc in range(NEC):
                    ins = t.matmul(psG, xnT[:, dc, m * 128:(m + 1) * 128],
                                   wg_bf[:, dc * 8:(dc + 1) * 8],
                                   start=(dc == 0), stop=(dc == NEC - 1))
                sG.inc(ins)
            for m in range(NT):
                t.wait_ge(sXn.h, m + 1)
                if m >= 1:
                    t.wait_ge(sEv.h, m)
                for k in range(NEC):
                    ins = t.transpose(psT[:, k, :],
                                      xnb[m % 2][:, k * 128:(k + 1) * 128], iden_bf[:])
                sTp.inc(ins)
                if m >= 1:
                    gate_mm(m - 1)
            gate_mm(NT - 1)

        @blk.vector
        def _(v):
            W = mkW(v, sCv)
            v.wait_ge(sTp.h, NT)
            sEv.inc(v.tensor_copy(xnT[:, :, (NT - 1) * 128:NT * 128], psT[:]))
            v.wait_ge(sEv.h, sEv.n)
            gate_comb2(v, W, NT - 1)

    if _STAGE < 2:
        emit_taps(dict(xnT=xnT, neglam=neglam))
        esA1.close(); esA.close(); es.close()
        return taps

    if _STAGE < 3:
        emit_taps(dict(xnT=xnT, wall=wall, neglam=neglam))
        esA1.close(); esA.close(); es.close()
        return taps

    with nc.Block("p2") as blk:
        @blk.tensor
        def _(t):
            psK = arena[:, 0:2048]
            for p in range(NPAIR):
                if p >= 1:
                    t.wait_ge(sKc.h, p)
                for tc in range(4):
                    for ec in range(NEC):
                        ins = t.matmul(psK[:, tc * 512:(tc + 1) * 512],
                                       wkt_bf[:, ec, p * 128:(p + 1) * 128],
                                       xnT[:, ec, tc * 512:(tc + 1) * 512],
                                       start=(ec == 0), stop=(ec == NEC - 1))
                sKm.inc(ins)
            for m in range(NT):
                slot = m % 2
                psV = arena[:, 2048:3072] if slot == 0 else arena[:, 0:1024]
                if m < 2:
                    t.wait_ge(sKc.h, NPAIR)
                else:
                    t.wait_ge(sVc.h, m - 1)
                for jc in range(2):
                    for ec in range(NEC):
                        ins = t.matmul(psV[:, jc * 512:(jc + 1) * 512],
                                       xnT[:, ec, m * 128:(m + 1) * 128],
                                       wvt_bf[:, ec, jc * 512:(jc + 1) * 512],
                                       start=(ec == 0), stop=(ec == NEC - 1))
                sVm.inc(ins)

        @blk.vector
        def _(v):
            W = mkW(v, sCv)
            psK = arena[:, 0:2048]
            for p in range(NPAIR):
                v.wait_ge(sKm.h, p + 1)
                W(v.tensor_copy(qraw[:], psK), sKc)
                W(v.tensor_mul(t1b[:], qraw[:], ck_bf[:]))
                W(v.tensor_mul(usw[0:32, :], qraw[32:64, :], sk_bf[32:64, :]))
                W(v.tensor_mul(usw[32:64, :], qraw[0:32, :], sk_bf[0:32, :]))
                W(v.tensor_mul(usw[64:96, :], qraw[96:128, :], sk_bf[96:128, :]))
                W(v.tensor_mul(usw[96:128, :], qraw[64:96, :], sk_bf[64:96, :]))
                W(v.tensor_add(kbf[:, p, :], t1b[:], usw[:]), sKr)
                W(v.tensor_copy(kbf_sub[:, p, :], kbf[:, p, 0:2048:8]))
            for m in range(NT):
                slot = m % 2
                psV = arena[:, 2048:3072] if slot == 0 else arena[:, 0:1024]
                v.wait_ge(sVm.h, m + 1)
                W(v.tensor_copy(Vsb[:, m, :], psV), sVc)

    if _STAGE < 4:
        emit_taps(dict(xnT=xnT, wall=wall, kbf=kbf, V=Vsb, neglam=neglam))
        esA1.close(); esA.close(); es.close()
        return taps

    esA1.close()
    esU = ExitStack()
    UT = esU.enter_context(nc.sbuf_tensor("UT", [128, NPAIR, T], BF16, side="right"))

    esA2 = ExitStack()
    sbA2 = lambda *a, **k: esA2.enter_context(nc.sbuf_tensor(*a, **k))
    qbf = [sbA2(f"qbf{i}", [128, T], BF16) for i in range(2)]
    e0b = [sbA2(f"e0b{i}", [128, T], BF16) for i in range(2)]
    e1b = [sbA2(f"e1b{i}", [128, T], BF16) for i in range(2)]
    diffb = [sbA2(f"diffb{i}", [128, T], BF16) for i in range(3)]
    diffT = [sbA2(f"diffT{i}", [128, NT, 128], BF16) for i in range(2)]
    sums0 = sbA2("sums0", [128, NT], F32)
    nb_all = sbA2("nb_all", [128, 2 * NT], F32)
    csums = [sbA2(f"csums{i}", [128, 4], F32) for i in range(2)]
    sjunk = sbA2("sjunk", [128, 256], BF16)
    s01 = sbA2("s01", [128, 2], F32)
    s1t = sbA2("s1t", [128, 1], F32)
    ab = sbA2("ab", [128, NT], F32)
    rhot = sbA2("rhot", [128, 1], F32)
    nrho = [sbA2(f"nrho{i}", [128, 1], F32) for i in range(2)]
    sqb = sbA2("sqb", [128, 128], BF16)
    s0sq = sbA2("s0sq", [128, NT], F32)
    innerb = sbA2("innerb", [128, NT], F32)
    lnb = sbA2("lnb", [128, NT], F32)
    rstb = sbA2("rstb", [128, NT], F32)
    wqb = [sbA2(f"wqb{i}", [128, NEC, 128], BF16) for i in range(2)]

    psQ = arena[:, 0:2048]
    SLOT = [arena[:, 0:1024], arena[:, 1024:2048], arena[:, 2048:3072]]
    psU = arena[:, 3072:3200]
    psMsq = arena[0:1, 3200:3328]
    psSc = arena[:, 3328:3344]

    def p3_items():
        items = [("sub", 0), ("sub", 1)]
        for m in range(NT):
            if m + 2 < NT:
                items.append(("sub", m + 2))
            for c in range(4):
                items.append(("main", m, c))
        return items

    def p3_plan():
        plans = []
        slot_state = [None, None, None]
        n_sub = 0
        n_main = 0
        for p in range(NPAIR):
            plan = []
            for j, it in enumerate(p3_items()):
                k = len(plans) * 80 + j
                slot = k % 3
                waits = []
                prev = slot_state[slot]
                if prev is not None:
                    waits.append(prev)
                if j < 3:
                    waits.append(("sQc", p + 1))
                if it[0] == "sub":
                    n_sub += 1
                    slot_state[slot] = ("sSt0", 2 * n_sub)
                    plan.append((it, slot, waits, n_sub, None))
                else:
                    n_main += 1
                    slot_state[slot] = ("sExp", n_main)
                    plan.append((it, slot, waits, None, n_main))
            plans.append(plan)
        return plans

    P3PLANS = p3_plan()

    def tail_pieces(t, p, m, piece):
        mm = NT * p + m
        if piece == 0:
            t.wait_ge(sDiff.h, mm + 1)
            if mm >= 1:
                t.wait_ge(sTe.h, 2 * mm)
            for sc in range(8):
                ins = t.transpose(psT[:, sc, :],
                                  diffb[m % 3][:, sc * 128:(sc + 1) * 128],
                                  iden_bf[:])
            sTr.inc(ins)
        elif piece == 1:
            t.wait_ge(sTe.h, 2 * mm + 1)
            for sc in range(8):
                ins = t.transpose(psT[:, sc, :],
                                  diffb[m % 3][:, (8 + sc) * 128:(9 + sc) * 128],
                                  iden_bf[:])
            sTr.inc(ins)
        elif piece == 2:
            t.wait_ge(sTe.h, 2 * mm + 1)
            if m == 0 and p >= 1:
                t.wait_ge(sSm2.h, p)
            if mm >= 1:
                t.wait_ge(sSq.h, mm)
            for sc in range(8):
                t.matmul(psU[:], Vsb[:, sc, p * 128:(p + 1) * 128],
                         diffT[m % 2][:, sc, :],
                         start=(sc == 0), stop=False)
        elif piece == 3:
            t.wait_ge(sTe.h, 2 * mm + 2)
            for sc in range(8, NT):
                ins = t.matmul(psU[:], Vsb[:, sc, p * 128:(p + 1) * 128],
                               diffT[m % 2][:, sc, :],
                               start=False, stop=(sc == NT - 1))
            sPv.inc(ins)
        else:
            t.wait_ge(sSq.h, mm + 1)
            sMm.inc(t.matmul(psSc[:, m:m + 1], sqb[:], inv128_bf[:],
                             start=True, stop=True))

    with nc.Block("p3") as blk:
        @blk.gpsimd
        def _(g):
            sWqa[0].inc(g.dma_start(wqb[0][:], wqt_h[:, :, 0:128]), 16)
            for p in range(1, NPAIR):
                if p >= 2:
                    g.wait_ge(sQm.h, p - 1)
                sWqa[p % 2].inc(g.dma_start(wqb[p % 2][:],
                                            wqt_h[:, :, p * 128:(p + 1) * 128]), 16)

        @blk.tensor
        def _(t):
            semmap = {"sQc": sQc, "sSt0": sSt0, "sExp": sExp}

            def emit_item(item):
                (it, slot, waits, nsub, nmain) = item
                for (semname, val) in waits:
                    t.wait_ge(semmap[semname].h, val)
                if it[0] == "sub":
                    m = it[1]
                    if m < 2:
                        t.wait_ge(sQr.h, p + 1)
                    for h in range(2):
                        ins = t.matmul(
                            SLOT[slot][:, h * 512:h * 512 + 256],
                            qbf[p % 2][h * 64:(h + 1) * 64,
                                       m * 128:(m + 1) * 128],
                            kbf_sub[h * 64:(h + 1) * 64, p, :],
                            start=True, stop=True, tile_position=(64 * h, 0))
                    sSub.inc(ins)
                else:
                    m, c = it[1], it[2]
                    h = c // 2
                    half = c % 2
                    for sc in range(2):
                        col = half * 1024 + sc * 512
                        ins = t.matmul(
                            SLOT[slot][:, sc * 512:(sc + 1) * 512],
                            qbf[p % 2][h * 64:(h + 1) * 64,
                                       m * 128:(m + 1) * 128],
                            kbf[h * 64:(h + 1) * 64, p, col:col + 512],
                            start=True, stop=True, tile_position=(64 * h, 0))
                    sS0.inc(ins)

            for p in range(NPAIR):
                plan = P3PLANS[p]
                t.wait_ge(sWqa[p % 2].h, 16 * (p // 2 + 1))
                if p >= 1:
                    t.wait_ge(sExp.h, 64 * p)
                    t.wait_ge(sSt0.h, 32 * p)
                for tc in range(4):
                    for ec in range(NEC):
                        ins = t.matmul(psQ[:, tc * 512:(tc + 1) * 512],
                                       wqb[p % 2][:, ec, :],
                                       xnT[:, ec, tc * 512:(tc + 1) * 512],
                                       start=(ec == 0), stop=(ec == NEC - 1))
                sQm.inc(ins)
                by_tile = [[] for _ in range(NT)]
                head = []
                for item in plan:
                    it = item[0]
                    if it[0] == "sub" and it[1] < 2:
                        head.append(item)
                    elif it[0] == "sub":
                        by_tile[it[1] - 2].append(item)
                    else:
                        by_tile[it[1]].append(item)
                for item in head:
                    emit_item(item)
                for m in range(NT):
                    items = by_tile[m]
                    mains = items[-4:]
                    subs = items[:-4]
                    for item in subs:
                        emit_item(item)
                    emit_item(mains[0])
                    if m >= 2:
                        tail_pieces(t, p, m - 2, 0)
                    emit_item(mains[1])
                    if m >= 2:
                        tail_pieces(t, p, m - 2, 1)
                    emit_item(mains[2])
                    if m >= 2:
                        tail_pieces(t, p, m - 2, 2)
                    emit_item(mains[3])
                    if m >= 2:
                        tail_pieces(t, p, m - 2, 3)
                    if m >= 3:
                        tail_pieces(t, p, m - 3, 4)
                for piece in range(4):
                    tail_pieces(t, p, NT - 2, piece)
                tail_pieces(t, p, NT - 3, 4)
                for piece in range(4):
                    tail_pieces(t, p, NT - 1, piece)
                tail_pieces(t, p, NT - 2, 4)
                tail_pieces(t, p, NT - 1, 4)

        @blk.vector
        def _(v):
            W = mkW(v, sCv)
            nsub_g = 0
            for p in range(NPAIR):
                plan = P3PLANS[p]
                v.wait_ge(sQm.h, p + 1)
                W(v.tensor_copy(qraw[:], psQ), sQc)
                W(v.tensor_mul(t1b[:], qraw[:], cq_bf[:]))
                W(v.tensor_mul(usw[0:32, :], qraw[32:64, :], sq_bf[32:64, :]))
                W(v.tensor_mul(usw[32:64, :], qraw[0:32, :], sq_bf[0:32, :]))
                W(v.tensor_mul(usw[64:96, :], qraw[96:128, :], sq_bf[96:128, :]))
                W(v.tensor_mul(usw[96:128, :], qraw[64:96, :], sq_bf[64:96, :]))
                W(v.tensor_add(qbf[p % 2][:], t1b[:], usw[:]), sQr)
                subslot = {it[1]: slot for (it, slot, w_, ns, nm) in plan
                           if it[0] == "sub"}
                for m in range(NT):
                    mm = NT * p + m
                    if m >= 2:
                        v.wait_ge(sTr.h, 2 * (mm - 2) + 1)
                        W(v.tensor_copy(diffT[(m - 2) % 2][:, 0:8, :], psT[:]),
                          sTe)
                        v.wait_ge(sTr.h, 2 * (mm - 2) + 2)
                        W(v.tensor_copy(diffT[(m - 2) % 2][:, 8:16, :], psT[:]),
                          sTe)
                    for msub in ([0, 1] if m == 0 else []) + \
                            ([m + 2] if m + 2 < NT else []):
                        nsub_g += 1
                        v.wait_ge(sSub.h, nsub_g)
                        if p >= 1:
                            v.wait_ge(sExp.h, 64 * (p - 1) + 4 * (msub + 1))
                        sl = subslot[msub]
                        for h in range(2):
                            W(v.tensor_scalar(
                                sjunk[:], SLOT[sl][:, h * 512:h * 512 + 256],
                                -1.0, None, OP.mult, OP.min,
                                accum_out=nb_all[:, h * NT + msub:
                                                 h * NT + msub + 1]), sSt0)
                    v.wait_ge(sExp.h, 64 * p + 4 * m + 4)
                    W(v.tensor_tensor(s01[:], csums[m % 2][:, 0:4:2],
                                      csums[m % 2][:, 1:4:2], OP.add))
                    W(v.tensor_copy(sums0[:, m:m + 1], s01[:, 0:1]))
                    W(v.tensor_scalar(s1t[:], s01[:, 1:2], 2.0 ** -40, None,
                                      OP.mult))
                    W(v.reciprocal(rhot[:], s1t[:]))
                    W(v.tensor_mul(rhot[:], rhot[:], s01[:, 0:1]))
                    if mm >= 3:
                        v.wait_ge(sTr.h, 2 * (mm - 3) + 2)
                    W(v.tensor_scalar(nrho[m % 2][:], rhot[:], neglam[:],
                                      2.0 ** -40, OP.mult, OP.mult))
                    W(v.tensor_scalar(t1b[:], e1b[m % 2][:], nrho[m % 2][:],
                                      None, OP.mult))
                    W(v.tensor_add(diffb[m % 3][:], t1b[:], e0b[m % 2][:]), sDiff)
                for mf in (NT - 2, NT - 1):
                    mmf = NT * p + mf
                    v.wait_ge(sTr.h, 2 * mmf + 1)
                    W(v.tensor_copy(diffT[mf % 2][:, 0:8, :], psT[:]), sTe)
                    v.wait_ge(sTr.h, 2 * mmf + 2)
                    W(v.tensor_copy(diffT[mf % 2][:, 8:16, :], psT[:]), sTe)
                v.wait_ge(sMm.h, NT * (p + 1))
                W(v.tensor_scalar(s0sq[:], sums0[:], 2.0 ** -40, None, OP.mult))
                W(v.reciprocal(ab[:], s0sq[:]))
                W(v.tensor_mul(s0sq[:], ab[:], ab[:]))
                W(v.tensor_tensor(innerb[:], psSc[:], s0sq[:], OP.mult))
                W(v.tensor_scalar(innerb[:], innerb[:], 2.0 ** -30, None, OP.mult),
                  sSm2)
                v.wait_ge(sRst.h, p + 1)
                W(v.tensor_mul(innerb[:], ab[:], rstb[:]))
                W(v.tensor_scalar(scales[:, p, :], innerb[:],
                                  (1.0 - LI) * 2.0 ** -40, None, OP.mult))

        @blk.scalar
        def _(s):
            Ws = mkW(s, sCs)
            for p in range(NPAIR):
                plan = P3PLANS[p]
                for (it, slot, waits, nsub, nmain) in plan:
                    if it[0] != "main":
                        continue
                    m, c = it[1], it[2]
                    mm = NT * p + m
                    h = c // 2
                    half = c % 2
                    s.wait_ge(sS0.h, nmain)
                    s.wait_ge(sSt0.h, 2 * (NT * p + m + 1))
                    if mm >= 2 and c == 0:
                        s.wait_ge(sDiff.h, mm - 1)
                    ebuf = e0b[m % 2] if h == 0 else e1b[m % 2]
                    Ws(s.activation(ebuf[:, half * 1024:(half + 1) * 1024],
                                    SLOT[slot][:], AF.Exp,
                                    bias=nb_all[:, h * NT + m:h * NT + m + 1],
                                    accum_out=csums[m % 2][:, c:c + 1]), sExp)
                    if c == 3 and m >= 2:
                        s.wait_ge(sPv.h, mm - 1)
                        Ws(s.activation(UT[:, p, (m - 2) * 128:(m - 1) * 128],
                                        psU[:], AF.Identity))
                        if mm >= 3:
                            s.wait_ge(sMm.h, mm - 2)
                        Ws(s.activation(sqb[:], psU[:], AF.Square,
                                        scale=2.0 ** -25), sSq)
                for mf in (NT - 2, NT - 1):
                    mmf = NT * p + mf
                    s.wait_ge(sPv.h, mmf + 1)
                    Ws(s.activation(UT[:, p, mf * 128:(mf + 1) * 128], psU[:],
                                    AF.Identity))
                    s.wait_ge(sMm.h, mmf)
                    Ws(s.activation(sqb[:], psU[:], AF.Square, scale=2.0 ** -25),
                       sSq)
                s.wait_ge(sSm2.h, p + 1)
                Ws(s.activation(lnb[:], innerb[:], AF.Ln, bias=cvec_sb[:, 0:1]),
                   sAct)
                Ws(s.activation(rstb[:], lnb[:], AF.Exp, scale=cvec_sb[:, 1:2]),
                   sRst)

    if _STAGE < 5:
        emit_taps(dict(xnT=xnT, wall=wall, kbf=kbf, V=Vsb, UT=UT, scales=scales,
                       sums0=sums0, qbf0=qbf[0],
                       e0b=e0b[0], e1b=e1b[0], neglam=neglam))
        esA2.close(); esU.close(); esA.close(); es.close()
        return taps

    esA2.close(); esA.close()
    esC = ExitStack()
    sbC = lambda *a, **k: esC.enter_context(nc.sbuf_tensor(*a, **k))
    acc = sbC("acc", [128, NT, E], F32)
    wob = sbC("wob", [128, E], BF16)
    esD = ExitStack()
    sbD = lambda *a, **k: esD.enter_context(nc.sbuf_tensor(*a, **k))
    w1b = sbD("w1b", [128, NEC, E], BF16)
    w2b = sbD("w2b", [128, NEC, E], BF16)
    hT = sbD("hT", [128, NEC, T], BF16)
    b2e = sbD("b2e", [1, E], BF16)
    xr = [sbD(f"xr{i}", [128, E], F32) for i in range(2)]

    with nc.Block("p45") as blk:
        @blk.sync
        def _(sy):
            for m in range(NT):
                if m >= 2:
                    sy.wait_ge(sYa[m % 2].h, 16 * (m // 2))
                sXra[m % 2].inc(sy.dma_start(xr[m % 2][:],
                                             x_h[m * 128:(m + 1) * 128, :]), 16)

        @blk.gpsimd
        def _(g):
            sWoa[0].inc(g.dma_start(wob[:], wot_h[:, 0, :]), 16)
            sW1a[0].inc(g.dma_start(w1b[:], w1_h[0]), 16)
            sB2a[0].inc(g.dma_start(b2e[:], b2_h[0:1, 0:E]), 16)
            sW2a[0].inc(g.dma_start(w2b[:], w2_h[0]), 16)
            for e in range(1, NEXP):
                g.wait_ge(sWm.h, 32 * e)
                sWoa[0].inc(g.dma_start(wob[:], wot_h[:, e, :]), 16)
                g.wait_ge(sM1h.h, 16 * e)
                sW1a[0].inc(g.dma_start(w1b[:], w1_h[e]), 16)
                g.wait_ge(sM2y.h, 32 * e)
                sB2a[0].inc(g.dma_start(b2e[:], b2_h[0:1, e * E:(e + 1) * E]),
                            16)
                sW2a[0].inc(g.dma_start(w2b[:], w2_h[e]), 16)

        @blk.tensor
        def _(t):
            for e in range(NEXP):
                t.wait_ge(sW1a[0].h, 16 * (e + 1))
                t.wait_ge(sWoa[0].h, 16 * (e + 1))
                for hd in range(NEC):
                    for th in range(2):
                        grp = hd * 2 + th
                        i = e * 16 + grp
                        slot = grp % 2
                        psH = arena[:, slot * 1024:(slot + 1) * 1024]
                        if i >= 2:
                            t.wait_ge(sRelu.h, i - 1)
                        for half in range(2):
                            col = th * 1024 + half * 512
                            for dc in range(NEC):
                                ins = t.matmul(
                                    psH[:, half * 512:(half + 1) * 512],
                                    w1b[:, dc, hd * 128:(hd + 1) * 128],
                                    xnT[:, dc, col:col + 512],
                                    start=(dc == 0), stop=(dc == NEC - 1))
                        sM1h.inc(ins)
                        for j in (2 * grp, 2 * grp + 1):
                            wj = e * 32 + j
                            m, oc = j // 2, j % 2
                            sl = j % 3
                            psW = arena[:, 2048 + sl * 512:
                                        2048 + (sl + 1) * 512]
                            if j == 0 and e >= 1:
                                t.wait_ge(sAcc.h, 32 * e)
                            if wj >= 3:
                                t.wait_ge(sWacc.h, wj - 2)
                            sWm.inc(t.matmul(
                                psW, UT[:, e, m * 128:(m + 1) * 128],
                                wob[:, oc * 512:(oc + 1) * 512],
                                start=True, stop=True))
                t.wait_ge(sRelu.h, 16 * (e + 1))
                t.wait_ge(sW2a[0].h, 16 * (e + 1))
                t.wait_ge(sB2a[0].h, 16 * (e + 1))
                t.wait_ge(sWacc.h, 32 * (e + 1))
                for m in range(NT):
                    for oc in range(2):
                        i = e * 32 + m * 2 + oc
                        slot = (m * 2 + oc) % 2
                        psY = arena[:, 2048 + slot * 512:
                                    2048 + (slot + 1) * 512]
                        if i >= 2:
                            t.wait_ge(sAcc.h, i - 1)
                        for hc in range(NEC):
                            t.matmul(psY, hT[:, hc, m * 128:(m + 1) * 128],
                                     w2b[:, hc, oc * 512:(oc + 1) * 512],
                                     start=(hc == 0), stop=False)
                        ins = t.matmul(psY, ones_bf[0:1, 0:128],
                                       b2e[0:1, oc * 512:(oc + 1) * 512],
                                       start=False, stop=True)
                        sM2y.inc(ins)

        @blk.scalar
        def _(s):
            Ws = mkW(s, sCs)
            for e in range(NEXP):
                for hd in range(NEC):
                    for th in range(2):
                        i = e * 16 + hd * 2 + th
                        slot = (hd * 2 + th) % 2
                        psH = arena[:, slot * 1024:(slot + 1) * 1024]
                        s.wait_ge(sM1h.h, i + 1)
                        if e >= 1:
                            s.wait_ge(sM2y.h, 32 * e)
                        Ws(s.activation(
                            hT[:, hd, th * 1024:(th + 1) * 1024], psH, AF.Relu,
                            bias=b1c_sb[:, e * NEC + hd:e * NEC + hd + 1]),
                           sRelu)
            for m in range(NT):
                s.wait_ge(sXn.h, NT + m + 1)
                sYa[m % 2].inc(s.dma_start(y_h[m * 128:(m + 1) * 128, :],
                                           xr[m % 2][:]), 16)
            for i in range(2):
                s.wait_ge(sYa[i].h, sYa[i].n)

        @blk.vector
        def _(v):
            W = mkW(v, sCv)
            for e in range(NEXP):
                for j in range(32):
                    m, oc = j // 2, j % 2
                    sl = j % 3
                    psW = arena[:, 2048 + sl * 512:2048 + (sl + 1) * 512]
                    v.wait_ge(sWm.h, e * 32 + j + 1)
                    dst = acc[:, m, oc * 512:(oc + 1) * 512]
                    if e == 0:
                        ins = v.tensor_scalar(dst, psW, scales[:, 0, m:m + 1],
                                              None, OP.mult)
                    else:
                        ins = v.scalar_tensor_tensor(
                            dst, psW, scales[:, e, m:m + 1], dst,
                            OP.mult, OP.add)
                    W(ins, sWacc)
                for m in range(NT):
                    for oc in range(2):
                        i = e * 32 + m * 2 + oc
                        slot = (m * 2 + oc) % 2
                        psY = arena[:, 2048 + slot * 512:
                                    2048 + (slot + 1) * 512]
                        v.wait_ge(sM2y.h, i + 1)
                        dst = acc[:, m, oc * 512:(oc + 1) * 512]
                        W(v.scalar_tensor_tensor(
                            dst, psY, wall[:, m, e:e + 1], dst, OP.mult,
                            OP.add), sAcc)
                        if e == NEXP - 1 and oc == 1:
                            v.wait_ge(sXra[m % 2].h, 16 * (m // 2 + 1))
                            W(v.tensor_add(xr[m % 2][:], xr[m % 2][:],
                                           acc[:, m, :]), sXn)
    esU.close()

    if _STAGE < 7:
        emit_taps(dict(acc=acc, hT=hT, wall=wall))
        esD.close(); esC.close(); es.close()
        return taps

    emit_taps(dict(acc=acc))
    esD.close(); esC.close(); es.close()
    return taps


def kernel(**inputs):
    x_full = np.asarray(inputs["L0_attn_x"], dtype=np.float32)
    d = _host_prep(inputs)

    nc = bass.Bass("TRN2", target_bir_lowering=False)
    build(nc)

    in_maps = []
    for c in range(8):
        m = dict(d)
        m["x"] = np.ascontiguousarray(x_full[c])
        in_maps.append(m)

    res = run_bass_kernel_spmd(nc, in_maps, core_ids=list(range(8)))
    out = np.stack([res.results[c]["y"] for c in range(8)], axis=0)
    if _TAPS:
        kernel.last_taps = [{t: res.results[c]["tap_" + t] for t in _TAPS}
                            for c in range(8)]
    return out


# version 17
# speedup vs baseline: 1.3241x; 1.0029x over previous
import math
from contextlib import ExitStack

import numpy as np

import concourse.bass as bass
import concourse.mybir as mybir
from concourse.bass_utils import run_bass_kernel_spmd

F32 = mybir.dt.float32
BF16 = mybir.dt.bfloat16
AF = mybir.ActivationFunctionType
OP = mybir.AluOpType
AX = mybir.AxisListType.X

T = 2048
E = 1024
NT = 16
NEC = 8
NPAIR = 8
NEXP = 8
LI = 0.8 - 0.6 * math.exp(-0.3)

_STAGE = 99
_TAPS = []


class SemT:
    def __init__(self, h):
        self.h = h
        self.n = 0

    def inc(self, instr, k=1):
        instr.then_inc(self.h, k)
        self.n += k
        return instr


def _host_prep(inputs):
    Wq = np.asarray(inputs["L0_attn_Wq"], dtype=np.float32)
    Wk = np.asarray(inputs["L0_attn_Wk"], dtype=np.float32)
    Wv = np.asarray(inputs["L0_attn_Wv"], dtype=np.float32)
    Wo = np.asarray(inputs["L0_attn_Wo"], dtype=np.float32)

    def permute_qk(W):
        Wr = W.reshape(16, 32, 2, E)
        Wp = np.concatenate([Wr[:, :, 0, :], Wr[:, :, 1, :]], axis=1)
        Wt = Wp.reshape(E, E).T
        return np.ascontiguousarray(Wt.reshape(NEC, 128, E).transpose(1, 0, 2))

    d = {}
    d["wqt"] = permute_qk(Wq)
    d["wkt"] = permute_qk(Wk)
    d["wvt"] = np.ascontiguousarray(Wv.T.reshape(NEC, 128, E).transpose(1, 0, 2))
    d["wot"] = np.ascontiguousarray(Wo.T.reshape(NPAIR, 128, E).transpose(1, 0, 2))

    dd = 64
    inv_freq = 1.0 / (10000.0 ** (np.arange(0, dd, 2, dtype=np.float64) / dd))
    pos = np.arange(T, dtype=np.float64)
    freqs = np.repeat(pos[:, None] * inv_freq[None, :], 2, axis=-1)
    sv = (np.arange(0, dd, 2, dtype=np.float64) + 0.4 * dd) / (1.4 * dd)
    power = (pos - T // 2) / 512.0
    scale = sv[None, :] ** power[:, None]
    scale = np.concatenate([scale, scale], axis=-1)
    perm = np.concatenate([np.arange(0, 64, 2), np.arange(1, 64, 2)])
    swap = (np.arange(64) + 32) % 64
    sign = np.where(np.arange(64) < 32, -1.0, 1.0)
    cos_p = np.cos(freqs)[:, perm].T
    sin_p = np.sin(freqs)[:, perm].T
    sc_p = scale[:, perm].T

    def dup(t):
        return np.ascontiguousarray(np.concatenate([t, t], axis=0), dtype=np.float32)

    d["cq"] = dup(cos_p * sc_p * 0.125)
    d["sq"] = dup(((sin_p * sc_p * 0.125) * sign[:, None])[swap, :])
    d["ck"] = dup(cos_p / sc_p)
    d["sk"] = dup(((sin_p / sc_p) * sign[:, None])[swap, :])

    Wg = np.asarray(inputs["L0_ffn_W_gate"], np.float32)
    d["wg"] = np.ascontiguousarray(
        Wg.reshape(NEC, 128, NEXP).transpose(1, 0, 2).reshape(128, NEC * NEXP))
    d["bgbc"] = np.ascontiguousarray(
        np.broadcast_to(np.asarray(inputs["L0_ffn_b_gate"], np.float32)[None, :],
                        (128, NEXP)))
    W1 = np.asarray(inputs["L0_ffn_W1"], np.float32)
    d["w1"] = np.ascontiguousarray(W1.reshape(NEXP, NEC, 128, E).transpose(0, 2, 1, 3))
    W2 = np.asarray(inputs["L0_ffn_W2"], np.float32)
    d["w2"] = np.ascontiguousarray(W2.reshape(NEXP, NEC, 128, E).transpose(0, 2, 1, 3))
    b1 = np.asarray(inputs["L0_ffn_b1"], np.float32)
    d["b1c"] = np.ascontiguousarray(
        b1.reshape(NEXP, NEC, 128).transpose(2, 0, 1).reshape(128, NEXP * NEC))
    d["b2"] = np.ascontiguousarray(
        np.asarray(inputs["L0_ffn_b2"], np.float32).reshape(1, -1))

    lam = np.concatenate([
        np.asarray(inputs["L0_attn_lambda_q1"], np.float32),
        np.asarray(inputs["L0_attn_lambda_k1"], np.float32),
        np.asarray(inputs["L0_attn_lambda_q2"], np.float32),
        np.asarray(inputs["L0_attn_lambda_k2"], np.float32)])
    d["lam"] = np.ascontiguousarray(np.broadcast_to(lam[None, :], (128, 256)))

    d["cvec"] = np.ascontiguousarray(np.broadcast_to(
        np.array([1e-5, -0.5, -1.0, 1.0], np.float32)[None, :], (128, 4)))
    d["iden"] = np.eye(128, dtype=np.float32)
    d["ones"] = np.ones((1, 512), dtype=np.float32)
    d["inv128"] = np.full((128, 1), 1.0 / 128.0, dtype=np.float32)
    return d


def build(nc):
    taps = {}

    def dram(name, shape, dtype=F32, kind="ExternalInput"):
        return nc.dram_tensor(name, shape, dtype, kind=kind).ap()

    x_h = dram("x", [T, E])
    wqt_h = dram("wqt", [128, NEC, E])
    wkt_h = dram("wkt", [128, NEC, E])
    wvt_h = dram("wvt", [128, NEC, E])
    wot_h = dram("wot", [128, NPAIR, E])
    wg_h = dram("wg", [128, NEC * NEXP])
    bgbc_h = dram("bgbc", [128, NEXP])
    w1_h = dram("w1", [NEXP, 128, NEC, E])
    w2_h = dram("w2", [NEXP, 128, NEC, E])
    b1c_h = dram("b1c", [128, NEXP * NEC])
    b2_h = dram("b2", [1, NEXP * E])
    lam_h = dram("lam", [128, 256])
    cq_h = dram("cq", [128, T])
    sq_h = dram("sq", [128, T])
    ck_h = dram("ck", [128, T])
    sk_h = dram("sk", [128, T])
    iden_h = dram("iden", [128, 128])
    ones_h = dram("ones", [1, 512])
    inv128_h = dram("inv128", [128, 1])
    cvec_h = dram("cvec", [128, 4])
    y_h = dram("y", [T, E], kind="ExternalOutput")
    tap_shapes = {
        "xnT": [128, NEC, T], "wall": [128, NT, NEXP], "kbf": [128, NPAIR, T],
        "V": [128, NT, E], "UT": [128, NPAIR, T], "scales": [128, NPAIR, NT],
        "acc": [128, NT, E], "qbf0": [128, T], "e0b": [128, T], "e1b": [128, T],
        "sums0": [128, NT], "s1a": [128, NT], "s1b": [128, NT],
        "hT": [128, NEC, T], "neglam": [128, 1],
    }
    for tname in _TAPS:
        taps[tname] = dram("tap_" + tname, tap_shapes[tname], F32,
                           kind="ExternalOutput")

    es = ExitStack()
    S = lambda name: SemT(es.enter_context(nc.semaphore(name)))

    sX = S("sX"); sLN = S("sLN"); sSm = S("sSm"); sRs = S("sRs"); sXn = S("sXn")
    sTp = S("sTp"); sEv = S("sEv")
    sG = S("sG"); sGv = S("sGv")
    sKm = S("sKm"); sKc = S("sKc"); sVm = S("sVm"); sVc = S("sVc")
    sWq = S("sWq"); sQm = S("sQm"); sQc = S("sQc"); sQr = S("sQr"); sKr = S("sKr"); sSub = S("sSub")
    sS0 = S("sS0"); sS1a = S("sS1a"); sS1b = S("sS1b")
    sSt0 = S("sSt0"); sSt1 = S("sSt1"); sExp = S("sExp")
    sRho1 = S("sRho1"); sRl = S("sRl"); sRho2 = S("sRho2")
    sDiff = S("sDiff"); sTr = S("sTr"); sTe = S("sTe")
    sPv = S("sPv"); sSq = S("sSq"); sMm = S("sMm"); sMc = S("sMc")
    sScT = S("sScT"); sSm2 = S("sSm2"); sAct = S("sAct"); sRst = S("sRst")
    sWo = S("sWo"); sWm = S("sWm"); sWacc = S("sWacc")
    sW1L = S("sW1L"); sW2L = S("sW2L"); sM1h = S("sM1h"); sRelu = S("sRelu")
    sM2y = S("sM2y"); sAcc = S("sAcc"); sB2L = S("sB2L")
    sXr = S("sXr"); sY = S("sY"); sB0 = S("sB0"); sB0g = S("sB0g"); sTap = S("sTap")
    sXa = [S(f"sXa{i}") for i in range(4)]
    sWqa = [S(f"sWqa{i}") for i in range(2)]
    sWoa = [S(f"sWoa{i}") for i in range(2)]
    sW1a = [S(f"sW1a{i}") for i in range(2)]
    sW2a = [S(f"sW2a{i}") for i in range(2)]
    sB2a = [S(f"sB2a{i}") for i in range(2)]
    sXra = [S(f"sXra{i}") for i in range(4)]
    sYa = [S(f"sYa{i}") for i in range(4)]
    sCv = S("sCv"); sCs = S("sCs")

    def mkW(eng, chain_sem):
        def W(instr, sem=None):
            s = sem if sem is not None else chain_sem
            s.inc(instr)
            eng.wait_ge(s.h, s.n)
            return instr
        return W

    def emit_taps(available):
        live = [(n, taps[n], available[n]) for n in taps if n in available]
        if not live:
            return
        with nc.Block("taps") as blk:
            @blk.gpsimd
            def _(g):
                for name, hap, src in live:
                    sTap.inc(g.dma_start(hap[:], src[:]), 16)
                g.wait_ge(sTap.h, sTap.n)

    sb = lambda *a, **k: es.enter_context(nc.sbuf_tensor(*a, **k))

    xnT = sb("xnT", [128, NEC, T], BF16)
    wall = sb("wall", [128, NT, NEXP], F32)
    scales = sb("scales", [128, NPAIR, NT], F32)
    neglam = sb("neglam", [128, 1], F32)
    iden_bf = sb("iden_bf", [128, 128], BF16)
    ones_bf = sb("ones_bf", [1, 512], BF16)
    inv128_bf = sb("inv128_bf", [128, 1], BF16)
    b1c_sb = sb("b1c_sb", [128, NEXP * NEC], F32)
    cvec_sb = sb("cvec_sb", [128, 4], F32)

    arena = es.enter_context(nc.psum_tensor("arena", [128, 3584], F32))
    psT = es.enter_context(nc.psum_tensor("psT", [128, 8, 128], BF16))

    esA = ExitStack()
    sbA = lambda *a, **k: esA.enter_context(nc.sbuf_tensor(*a, **k))
    kbf = sbA("kbf", [128, NPAIR, T], BF16)
    kbf_sub = sbA("kbf_sub", [128, NPAIR, 256], BF16)
    Vsb = sbA("V", [128, NT, E], BF16)
    cq_bf = sbA("cq_bf", [128, T], BF16)
    sq_bf = sbA("sq_bf", [128, T], BF16)
    qraw = sbA("qraw", [128, T], BF16)
    t1b = sbA("t1b", [128, T], BF16)
    usw = sbA("usw", [128, T], BF16)

    esA1 = ExitStack()
    sbA1 = lambda *a, **k: esA1.enter_context(nc.sbuf_tensor(*a, **k))
    ck_bf = sbA1("ck_bf", [128, T], BF16)
    sk_bf = sbA1("sk_bf", [128, T], BF16)
    wkt_bf = sbA1("wkt_bf", [128, NEC, E], BF16)
    wvt_bf = sbA1("wvt_bf", [128, NEC, E], BF16)
    xt = [sbA1(f"xt{i}", [128, E], F32) for i in range(4)]
    xnb = [sbA1(f"xnb{i}", [128, E], BF16) for i in range(2)]
    ln_s1 = sbA1("ln_s1", [128, 1], F32)
    ln_s2 = sbA1("ln_s2", [128, 1], F32)
    ln_mean = sbA1("ln_mean", [128, 1], F32)
    ln_m2 = sbA1("ln_m2", [128, 1], F32)
    ln_var = sbA1("ln_var", [128, 1], F32)
    ln_ln = sbA1("ln_ln", [128, 1], F32)
    ln_rs = sbA1("ln_rs", [128, 1], F32)
    gsb = sbA1("gsb", [128, NEXP], F32)
    top8 = sbA1("top8", [128, NEXP], F32)
    gesb = sbA1("gesb", [128, NEXP], F32)
    wg_bf = sbA1("wg_bf", [128, NEC * NEXP], BF16)
    bgbc_sb = sbA1("bgbc_sb", [128, NEXP], F32)
    lam_sb = sbA1("lam_sb", [128, 256], F32)
    lprod = sbA1("lprod", [128, 64], F32)
    ld12 = sbA1("ld12", [128, 2], F32)
    le12 = sbA1("le12", [128, 2], F32)

    with nc.Block("b0") as blk:
        @blk.sync
        def _(sy):
            sB0.inc(sy.dma_start(bgbc_sb[:], bgbc_h[:]), 16)
            sB0.inc(sy.dma_start(lam_sb[:], lam_h[:]), 16)
            sB0.inc(sy.dma_start(b1c_sb[:], b1c_h[:]), 16)
            sB0.inc(sy.dma_start(cvec_sb[:], cvec_h[:]), 16)
            sy.wait_ge(sB0.h, sB0.n)

        @blk.gpsimd
        def _(g):
            for dst, src in ((iden_bf, iden_h), (ones_bf, ones_h),
                             (inv128_bf, inv128_h), (wg_bf, wg_h)):
                sB0g.inc(g.dma_start(dst[:], src[:]), 16)
            g.wait_ge(sB0g.h, sB0g.n)

        @blk.vector
        def _(v):
            W = mkW(v, sCv)
            v.wait_ge(sB0.h, 64)
            W(v.tensor_mul(lprod[:], lam_sb[:, 0:64], lam_sb[:, 64:128]))
            W(v.reduce_sum(ld12[:, 0:1], lprod[:], axis=AX))
            W(v.tensor_mul(lprod[:], lam_sb[:, 128:192], lam_sb[:, 192:256]))
            W(v.reduce_sum(ld12[:, 1:2], lprod[:], axis=AX), sGv)

        @blk.scalar
        def _(s):
            s.wait_ge(sGv.h, 1)
            Ws = mkW(s, sCs)
            Ws(s.activation(le12[:], ld12[:], AF.Exp), sRs)

        @blk.vector
        def _(v):
            W = mkW(v, sCv)
            v.wait_ge(sRs.h, 1)
            W(v.tensor_sub(neglam[:], le12[:, 1:2], le12[:, 0:1]), sGv)
            W(v.tensor_scalar(neglam[:], neglam[:], LI, None, OP.subtract))

    with nc.Block("b0b") as blk:
        @blk.gpsimd
        def _(g):
            for dst, src in ((cq_bf, cq_h), (sq_bf, sq_h), (ck_bf, ck_h),
                             (sk_bf, sk_h), (wkt_bf, wkt_h), (wvt_bf, wvt_h)):
                sB0g.inc(g.dma_start(dst[:], src[:]), 16)
            g.wait_ge(sB0g.h, sB0g.n)

        @blk.sync
        def _(sy):
            for m in range(4):
                sXa[m].inc(sy.dma_start(xt[m][:], x_h[m * 128:(m + 1) * 128, :]), 16)

    if _STAGE < 1:
        emit_taps({})
        esA1.close(); esA.close(); es.close()
        return taps

    gv0 = sGv.n

    def gate_comb2(v, W, m):
        psG = arena[:, 2048 + (m % 2) * 512:2048 + (m % 2) * 512 + 8]
        v.wait_ge(sG.h, m + 1)
        W(v.tensor_add(gsb[:], psG, bgbc_sb[:]), sGv)
        W(v.tensor_scalar(gesb[:], gsb[:], 1.0, None, OP.mult, OP.max,
                          accum_out=top8[:, 0:1]), sGv)
        W(v.tensor_scalar(gesb[:], gsb[:], top8[:, 0:1], None, OP.is_ge), sGv)
        W(v.scalar_tensor_tensor(gesb[:], gesb[:], -1e30, gsb[:],
                                 OP.mult, OP.add), sGv)
        W(v.tensor_scalar(gesb[:], gesb[:], 1.0, None, OP.mult, OP.max,
                          accum_out=top8[:, 1:2]), sGv)
        W(v.tensor_scalar(gesb[:], gsb[:], top8[:, 1:2], None, OP.is_ge), sGv)
        W(v.tensor_mul(wall[:, m, :], gsb[:], gesb[:]), sGv)

    with nc.Block("p1") as blk:
        @blk.sync
        def _(sy):
            for m in range(4, NT):
                sy.wait_ge(sXn.h, m - 3)
                sXa[m % 4].inc(sy.dma_start(xt[m % 4][:],
                                            x_h[m * 128:(m + 1) * 128, :]), 16)

        @blk.vector
        def _(v):
            W = mkW(v, sCv)
            for m in range(NT):
                v.wait_ge(sXa[m % 4].h, 16 * (m // 4 + 1))
                if m >= 2:
                    v.wait_ge(sTp.h, m - 1)
                W(v.reduce_sum(ln_s1[:], xt[m % 4][:], axis=AX))
                W(v.scalar_tensor_tensor(
                    xnb[m % 2][:], xt[m % 4][:], 1.0, xt[m % 4][:],
                    OP.bypass, OP.mult, accum_out=ln_s2[:]), sLN)
                W(v.tensor_scalar(ln_mean[:], ln_s1[:], 1.0 / E, None, OP.mult))
                W(v.tensor_mul(ln_m2[:], ln_mean[:], ln_mean[:]))
                W(v.scalar_tensor_tensor(
                    ln_var[:], ln_s2[:], 1.0 / E, ln_m2[:], OP.mult, OP.subtract),
                  sSm)
                v.wait_ge(sRs.h, 1 + m + 1)
                W(v.tensor_scalar(xnb[m % 2][:], xt[m % 4][:], ln_mean[:],
                                  ln_rs[:], OP.subtract, OP.mult), sXn)
                if m >= 1:
                    v.wait_ge(sTp.h, m)
                    W(v.tensor_copy(xnT[:, :, (m - 1) * 128:m * 128], psT[:]), sEv)

        @blk.scalar
        def _(s):
            Ws = mkW(s, sCs)
            for m in range(NT):
                s.wait_ge(sSm.h, m + 1)
                Ws(s.activation(ln_ln[:], ln_var[:], AF.Ln, bias=cvec_sb[:, 0:1]),
                   sAct)
                Ws(s.activation(ln_rs[:], ln_ln[:], AF.Exp, scale=cvec_sb[:, 1:2]),
                   sRs)

        @blk.tensor
        def _(t):
            for m in range(NT):
                t.wait_ge(sXn.h, m + 1)
                if m >= 1:
                    t.wait_ge(sEv.h, m)
                for k in range(NEC):
                    ins = t.transpose(psT[:, k, :],
                                      xnb[m % 2][:, k * 128:(k + 1) * 128], iden_bf[:])
                sTp.inc(ins)

        @blk.vector
        def _(v):
            W = mkW(v, sCv)
            v.wait_ge(sTp.h, NT)
            sEv.inc(v.tensor_copy(xnT[:, :, (NT - 1) * 128:NT * 128], psT[:]))
            v.wait_ge(sEv.h, sEv.n)

    if _STAGE < 2:
        emit_taps(dict(xnT=xnT, neglam=neglam))
        esA1.close(); esA.close(); es.close()
        return taps

    if _STAGE < 3:
        emit_taps(dict(xnT=xnT, wall=wall, neglam=neglam))
        esA1.close(); esA.close(); es.close()
        return taps

    with nc.Block("p2") as blk:
        @blk.tensor
        def _(t):
            psK = arena[:, 0:2048]
            for p in range(NPAIR):
                if p >= 1:
                    t.wait_ge(sKc.h, p)
                for tc in range(4):
                    for ec in range(NEC):
                        ins = t.matmul(psK[:, tc * 512:(tc + 1) * 512],
                                       wkt_bf[:, ec, p * 128:(p + 1) * 128],
                                       xnT[:, ec, tc * 512:(tc + 1) * 512],
                                       start=(ec == 0), stop=(ec == NEC - 1))
                sKm.inc(ins)
                for m in (2 * p, 2 * p + 1):
                    psG = arena[:, 2048 + (m % 2) * 512:2048 + (m % 2) * 512 + 8]
                    if m >= 2:
                        t.wait_ge(sGv.h, gv0 + 7 * (m - 1))
                    for dc in range(NEC):
                        ins = t.matmul(psG, xnT[:, dc, m * 128:(m + 1) * 128],
                                       wg_bf[:, dc * 8:(dc + 1) * 8],
                                       start=(dc == 0), stop=(dc == NEC - 1))
                    sG.inc(ins)
            for m in range(NT):
                slot = m % 2
                psV = arena[:, 2048:3072] if slot == 0 else arena[:, 0:1024]
                if m < 2:
                    t.wait_ge(sKc.h, NPAIR)
                    t.wait_ge(sGv.h, gv0 + 7 * NT)
                else:
                    t.wait_ge(sVc.h, m - 1)
                for jc in range(2):
                    for ec in range(NEC):
                        ins = t.matmul(psV[:, jc * 512:(jc + 1) * 512],
                                       xnT[:, ec, m * 128:(m + 1) * 128],
                                       wvt_bf[:, ec, jc * 512:(jc + 1) * 512],
                                       start=(ec == 0), stop=(ec == NEC - 1))
                sVm.inc(ins)

        @blk.vector
        def _(v):
            W = mkW(v, sCv)
            psK = arena[:, 0:2048]
            for p in range(NPAIR):
                v.wait_ge(sKm.h, p + 1)
                W(v.tensor_copy(qraw[:], psK), sKc)
                W(v.tensor_mul(t1b[:], qraw[:], ck_bf[:]))
                W(v.tensor_mul(usw[0:32, :], qraw[32:64, :], sk_bf[32:64, :]))
                W(v.tensor_mul(usw[32:64, :], qraw[0:32, :], sk_bf[0:32, :]))
                W(v.tensor_mul(usw[64:96, :], qraw[96:128, :], sk_bf[96:128, :]))
                W(v.tensor_mul(usw[96:128, :], qraw[64:96, :], sk_bf[64:96, :]))
                W(v.tensor_add(kbf[:, p, :], t1b[:], usw[:]), sKr)
                W(v.tensor_copy(kbf_sub[:, p, :], kbf[:, p, 0:2048:8]))
                for gm in (2 * p, 2 * p + 1):
                    gate_comb2(v, W, gm)
            for m in range(NT):
                slot = m % 2
                psV = arena[:, 2048:3072] if slot == 0 else arena[:, 0:1024]
                v.wait_ge(sVm.h, m + 1)
                W(v.tensor_copy(Vsb[:, m, :], psV), sVc)

    if _STAGE < 4:
        emit_taps(dict(xnT=xnT, wall=wall, kbf=kbf, V=Vsb, neglam=neglam))
        esA1.close(); esA.close(); es.close()
        return taps

    esA1.close()
    esU = ExitStack()
    UT = esU.enter_context(nc.sbuf_tensor("UT", [128, NPAIR, T], BF16, side="right"))

    esA2 = ExitStack()
    sbA2 = lambda *a, **k: esA2.enter_context(nc.sbuf_tensor(*a, **k))
    qbf = [sbA2(f"qbf{i}", [128, T], BF16) for i in range(2)]
    e0b = [sbA2(f"e0b{i}", [128, T], BF16) for i in range(2)]
    e1b = [sbA2(f"e1b{i}", [128, T], BF16) for i in range(2)]
    diffb = [sbA2(f"diffb{i}", [128, T], BF16) for i in range(3)]
    diffT = [sbA2(f"diffT{i}", [128, NT, 128], BF16) for i in range(2)]
    sums0 = sbA2("sums0", [128, NT], F32)
    nb_all = sbA2("nb_all", [128, 2 * NT], F32)
    csums = [sbA2(f"csums{i}", [128, 4], F32) for i in range(2)]
    sjunk = sbA2("sjunk", [128, 256], BF16)
    s01 = sbA2("s01", [128, 2], F32)
    s1t = sbA2("s1t", [128, 1], F32)
    ab = sbA2("ab", [128, NT], F32)
    rhot = sbA2("rhot", [128, 1], F32)
    nrho = [sbA2(f"nrho{i}", [128, 1], F32) for i in range(2)]
    sqb = sbA2("sqb", [128, 128], BF16)
    s0sq = sbA2("s0sq", [128, NT], F32)
    innerb = sbA2("innerb", [128, NT], F32)
    lnb = sbA2("lnb", [128, NT], F32)
    rstb = sbA2("rstb", [128, NT], F32)
    wqb = [sbA2(f"wqb{i}", [128, NEC, 128], BF16) for i in range(2)]

    psQ = arena[:, 0:2048]
    SLOT = [arena[:, 0:1024], arena[:, 1024:2048], arena[:, 2048:3072]]
    psU = arena[:, 3072:3200]
    psMsq = arena[0:1, 3200:3328]
    psSc = arena[:, 3328:3344]

    def p3_items():
        items = [("sub", 0), ("sub", 1)]
        for m in range(NT):
            if m + 2 < NT:
                items.append(("sub", m + 2))
            for c in range(4):
                items.append(("main", m, c))
        return items

    def p3_plan():
        plans = []
        slot_state = [None, None, None]
        n_sub = 0
        n_main = 0
        for p in range(NPAIR):
            plan = []
            for j, it in enumerate(p3_items()):
                k = len(plans) * 80 + j
                slot = k % 3
                waits = []
                prev = slot_state[slot]
                if prev is not None:
                    waits.append(prev)
                if j < 3:
                    waits.append(("sQc", p + 1))
                if it[0] == "sub":
                    n_sub += 1
                    slot_state[slot] = ("sSt0", 2 * n_sub)
                    plan.append((it, slot, waits, n_sub, None))
                else:
                    n_main += 1
                    slot_state[slot] = ("sExp", n_main)
                    plan.append((it, slot, waits, None, n_main))
            plans.append(plan)
        return plans

    P3PLANS = p3_plan()

    def tail_pieces(t, p, m, piece):
        mm = NT * p + m
        if piece == 0:
            t.wait_ge(sDiff.h, mm + 1)
            if mm >= 1:
                t.wait_ge(sTe.h, 2 * mm)
            for sc in range(8):
                ins = t.transpose(psT[:, sc, :],
                                  diffb[m % 3][:, sc * 128:(sc + 1) * 128],
                                  iden_bf[:])
            sTr.inc(ins)
        elif piece == 1:
            t.wait_ge(sTe.h, 2 * mm + 1)
            for sc in range(8):
                ins = t.transpose(psT[:, sc, :],
                                  diffb[m % 3][:, (8 + sc) * 128:(9 + sc) * 128],
                                  iden_bf[:])
            sTr.inc(ins)
        elif piece == 2:
            t.wait_ge(sTe.h, 2 * mm + 1)
            if m == 0 and p >= 1:
                t.wait_ge(sSm2.h, p)
            if mm >= 1:
                t.wait_ge(sSq.h, mm)
            for sc in range(8):
                t.matmul(psU[:], Vsb[:, sc, p * 128:(p + 1) * 128],
                         diffT[m % 2][:, sc, :],
                         start=(sc == 0), stop=False)
        elif piece == 3:
            t.wait_ge(sTe.h, 2 * mm + 2)
            for sc in range(8, NT):
                ins = t.matmul(psU[:], Vsb[:, sc, p * 128:(p + 1) * 128],
                               diffT[m % 2][:, sc, :],
                               start=False, stop=(sc == NT - 1))
            sPv.inc(ins)
        else:
            t.wait_ge(sSq.h, mm + 1)
            sMm.inc(t.matmul(psSc[:, m:m + 1], sqb[:], inv128_bf[:],
                             start=True, stop=True))

    with nc.Block("p3") as blk:
        @blk.gpsimd
        def _(g):
            sWqa[0].inc(g.dma_start(wqb[0][:], wqt_h[:, :, 0:128]), 16)
            for p in range(1, NPAIR):
                if p >= 2:
                    g.wait_ge(sQm.h, p - 1)
                sWqa[p % 2].inc(g.dma_start(wqb[p % 2][:],
                                            wqt_h[:, :, p * 128:(p + 1) * 128]), 16)

        @blk.tensor
        def _(t):
            semmap = {"sQc": sQc, "sSt0": sSt0, "sExp": sExp}

            def emit_item(item):
                (it, slot, waits, nsub, nmain) = item
                for (semname, val) in waits:
                    t.wait_ge(semmap[semname].h, val)
                if it[0] == "sub":
                    m = it[1]
                    if m < 2:
                        t.wait_ge(sQr.h, p + 1)
                    for h in range(2):
                        ins = t.matmul(
                            SLOT[slot][:, h * 512:h * 512 + 256],
                            qbf[p % 2][h * 64:(h + 1) * 64,
                                       m * 128:(m + 1) * 128],
                            kbf_sub[h * 64:(h + 1) * 64, p, :],
                            start=True, stop=True, tile_position=(64 * h, 0))
                    sSub.inc(ins)
                else:
                    m, c = it[1], it[2]
                    h = c // 2
                    half = c % 2
                    for sc in range(2):
                        col = half * 1024 + sc * 512
                        ins = t.matmul(
                            SLOT[slot][:, sc * 512:(sc + 1) * 512],
                            qbf[p % 2][h * 64:(h + 1) * 64,
                                       m * 128:(m + 1) * 128],
                            kbf[h * 64:(h + 1) * 64, p, col:col + 512],
                            start=True, stop=True, tile_position=(64 * h, 0))
                    sS0.inc(ins)

            for p in range(NPAIR):
                plan = P3PLANS[p]
                t.wait_ge(sWqa[p % 2].h, 16 * (p // 2 + 1))
                if p >= 1:
                    t.wait_ge(sExp.h, 64 * p)
                    t.wait_ge(sSt0.h, 32 * p)
                for tc in range(4):
                    for ec in range(NEC):
                        ins = t.matmul(psQ[:, tc * 512:(tc + 1) * 512],
                                       wqb[p % 2][:, ec, :],
                                       xnT[:, ec, tc * 512:(tc + 1) * 512],
                                       start=(ec == 0), stop=(ec == NEC - 1))
                sQm.inc(ins)
                by_tile = [[] for _ in range(NT)]
                head = []
                for item in plan:
                    it = item[0]
                    if it[0] == "sub" and it[1] < 2:
                        head.append(item)
                    elif it[0] == "sub":
                        by_tile[it[1] - 2].append(item)
                    else:
                        by_tile[it[1]].append(item)
                for item in head:
                    emit_item(item)
                for m in range(NT):
                    items = by_tile[m]
                    mains = items[-4:]
                    subs = items[:-4]
                    for item in subs:
                        emit_item(item)
                    emit_item(mains[0])
                    if m >= 2:
                        tail_pieces(t, p, m - 2, 0)
                    emit_item(mains[1])
                    if m >= 2:
                        tail_pieces(t, p, m - 2, 1)
                    emit_item(mains[2])
                    if m >= 2:
                        tail_pieces(t, p, m - 2, 2)
                    emit_item(mains[3])
                    if m >= 2:
                        tail_pieces(t, p, m - 2, 3)
                    if m >= 3:
                        tail_pieces(t, p, m - 3, 4)
                for piece in range(4):
                    tail_pieces(t, p, NT - 2, piece)
                tail_pieces(t, p, NT - 3, 4)
                for piece in range(4):
                    tail_pieces(t, p, NT - 1, piece)
                tail_pieces(t, p, NT - 2, 4)
                tail_pieces(t, p, NT - 1, 4)

        @blk.vector
        def _(v):
            W = mkW(v, sCv)
            nsub_g = 0
            for p in range(NPAIR):
                plan = P3PLANS[p]
                v.wait_ge(sQm.h, p + 1)
                W(v.tensor_copy(qraw[:], psQ), sQc)
                W(v.tensor_mul(t1b[:], qraw[:], cq_bf[:]))
                W(v.tensor_mul(usw[0:32, :], qraw[32:64, :], sq_bf[32:64, :]))
                W(v.tensor_mul(usw[32:64, :], qraw[0:32, :], sq_bf[0:32, :]))
                W(v.tensor_mul(usw[64:96, :], qraw[96:128, :], sq_bf[96:128, :]))
                W(v.tensor_mul(usw[96:128, :], qraw[64:96, :], sq_bf[64:96, :]))
                W(v.tensor_add(qbf[p % 2][:], t1b[:], usw[:]), sQr)
                subslot = {it[1]: slot for (it, slot, w_, ns, nm) in plan
                           if it[0] == "sub"}
                for m in range(NT):
                    mm = NT * p + m
                    if m >= 2:
                        v.wait_ge(sTr.h, 2 * (mm - 2) + 1)
                        W(v.tensor_copy(diffT[(m - 2) % 2][:, 0:8, :], psT[:]),
                          sTe)
                        v.wait_ge(sTr.h, 2 * (mm - 2) + 2)
                        W(v.tensor_copy(diffT[(m - 2) % 2][:, 8:16, :], psT[:]),
                          sTe)
                    for msub in ([0, 1] if m == 0 else []) + \
                            ([m + 2] if m + 2 < NT else []):
                        nsub_g += 1
                        v.wait_ge(sSub.h, nsub_g)
                        if p >= 1:
                            v.wait_ge(sExp.h, 64 * (p - 1) + 4 * (msub + 1))
                        sl = subslot[msub]
                        for h in range(2):
                            W(v.tensor_scalar(
                                sjunk[:], SLOT[sl][:, h * 512:h * 512 + 256],
                                -1.0, None, OP.mult, OP.min,
                                accum_out=nb_all[:, h * NT + msub:
                                                 h * NT + msub + 1]), sSt0)
                    v.wait_ge(sExp.h, 64 * p + 4 * m + 4)
                    W(v.tensor_tensor(s01[:], csums[m % 2][:, 0:4:2],
                                      csums[m % 2][:, 1:4:2], OP.add))
                    W(v.tensor_copy(sums0[:, m:m + 1], s01[:, 0:1]))
                    W(v.tensor_scalar(s1t[:], s01[:, 1:2], 2.0 ** -40, None,
                                      OP.mult))
                    W(v.reciprocal(rhot[:], s1t[:]))
                    W(v.tensor_mul(rhot[:], rhot[:], s01[:, 0:1]))
                    if mm >= 3:
                        v.wait_ge(sTr.h, 2 * (mm - 3) + 2)
                    W(v.tensor_scalar(nrho[m % 2][:], rhot[:], neglam[:],
                                      2.0 ** -40, OP.mult, OP.mult))
                    W(v.tensor_scalar(t1b[:], e1b[m % 2][:], nrho[m % 2][:],
                                      None, OP.mult))
                    W(v.tensor_add(diffb[m % 3][:], t1b[:], e0b[m % 2][:]), sDiff)
                for mf in (NT - 2, NT - 1):
                    mmf = NT * p + mf
                    v.wait_ge(sTr.h, 2 * mmf + 1)
                    W(v.tensor_copy(diffT[mf % 2][:, 0:8, :], psT[:]), sTe)
                    v.wait_ge(sTr.h, 2 * mmf + 2)
                    W(v.tensor_copy(diffT[mf % 2][:, 8:16, :], psT[:]), sTe)
                v.wait_ge(sMm.h, NT * (p + 1))
                W(v.tensor_scalar(s0sq[:], sums0[:], 2.0 ** -40, None, OP.mult))
                W(v.reciprocal(ab[:], s0sq[:]))
                W(v.tensor_mul(s0sq[:], ab[:], ab[:]))
                W(v.tensor_tensor(innerb[:], psSc[:], s0sq[:], OP.mult))
                W(v.tensor_scalar(innerb[:], innerb[:], 2.0 ** -30, None, OP.mult),
                  sSm2)
                v.wait_ge(sRst.h, p + 1)
                W(v.tensor_mul(innerb[:], ab[:], rstb[:]))
                W(v.tensor_scalar(scales[:, p, :], innerb[:],
                                  (1.0 - LI) * 2.0 ** -40, None, OP.mult))

        @blk.scalar
        def _(s):
            Ws = mkW(s, sCs)
            for p in range(NPAIR):
                plan = P3PLANS[p]
                for (it, slot, waits, nsub, nmain) in plan:
                    if it[0] != "main":
                        continue
                    m, c = it[1], it[2]
                    mm = NT * p + m
                    h = c // 2
                    half = c % 2
                    s.wait_ge(sS0.h, nmain)
                    s.wait_ge(sSt0.h, 2 * (NT * p + m + 1))
                    if mm >= 2 and c == 0:
                        s.wait_ge(sDiff.h, mm - 1)
                    ebuf = e0b[m % 2] if h == 0 else e1b[m % 2]
                    Ws(s.activation(ebuf[:, half * 1024:(half + 1) * 1024],
                                    SLOT[slot][:], AF.Exp,
                                    bias=nb_all[:, h * NT + m:h * NT + m + 1],
                                    accum_out=csums[m % 2][:, c:c + 1]), sExp)
                    if c == 3 and m >= 2:
                        s.wait_ge(sPv.h, mm - 1)
                        Ws(s.activation(UT[:, p, (m - 2) * 128:(m - 1) * 128],
                                        psU[:], AF.Identity))
                        if mm >= 3:
                            s.wait_ge(sMm.h, mm - 2)
                        Ws(s.activation(sqb[:], psU[:], AF.Square,
                                        scale=2.0 ** -25), sSq)
                for mf in (NT - 2, NT - 1):
                    mmf = NT * p + mf
                    s.wait_ge(sPv.h, mmf + 1)
                    Ws(s.activation(UT[:, p, mf * 128:(mf + 1) * 128], psU[:],
                                    AF.Identity))
                    s.wait_ge(sMm.h, mmf)
                    Ws(s.activation(sqb[:], psU[:], AF.Square, scale=2.0 ** -25),
                       sSq)
                s.wait_ge(sSm2.h, p + 1)
                Ws(s.activation(lnb[:], innerb[:], AF.Ln, bias=cvec_sb[:, 0:1]),
                   sAct)
                Ws(s.activation(rstb[:], lnb[:], AF.Exp, scale=cvec_sb[:, 1:2]),
                   sRst)

    if _STAGE < 5:
        emit_taps(dict(xnT=xnT, wall=wall, kbf=kbf, V=Vsb, UT=UT, scales=scales,
                       sums0=sums0, qbf0=qbf[0],
                       e0b=e0b[0], e1b=e1b[0], neglam=neglam))
        esA2.close(); esU.close(); esA.close(); es.close()
        return taps

    esA2.close(); esA.close()
    esC = ExitStack()
    sbC = lambda *a, **k: esC.enter_context(nc.sbuf_tensor(*a, **k))
    acc = sbC("acc", [128, NT, E], F32)
    wob = sbC("wob", [128, E], BF16)
    esD = ExitStack()
    sbD = lambda *a, **k: esD.enter_context(nc.sbuf_tensor(*a, **k))
    w1b = sbD("w1b", [128, NEC, E], BF16)
    w2b = sbD("w2b", [128, NEC, E], BF16)
    hT = sbD("hT", [128, NEC, T], BF16)
    b2e = sbD("b2e", [1, E], BF16)
    xr = [sbD(f"xr{i}", [128, E], F32) for i in range(2)]

    with nc.Block("p45") as blk:
        @blk.sync
        def _(sy):
            for m in range(NT):
                if m >= 2:
                    sy.wait_ge(sYa[m % 2].h, 16 * (m // 2))
                sXra[m % 2].inc(sy.dma_start(xr[m % 2][:],
                                             x_h[m * 128:(m + 1) * 128, :]), 16)

        @blk.gpsimd
        def _(g):
            sWoa[0].inc(g.dma_start(wob[:], wot_h[:, 0, :]), 16)
            sW1a[0].inc(g.dma_start(w1b[:], w1_h[0]), 16)
            sB2a[0].inc(g.dma_start(b2e[:], b2_h[0:1, 0:E]), 16)
            sW2a[0].inc(g.dma_start(w2b[:], w2_h[0]), 16)
            for e in range(1, NEXP):
                g.wait_ge(sWm.h, 32 * e)
                sWoa[0].inc(g.dma_start(wob[:], wot_h[:, e, :]), 16)
                g.wait_ge(sM1h.h, 16 * e)
                sW1a[0].inc(g.dma_start(w1b[:], w1_h[e]), 16)
                g.wait_ge(sM2y.h, 32 * e)
                sB2a[0].inc(g.dma_start(b2e[:], b2_h[0:1, e * E:(e + 1) * E]),
                            16)
                sW2a[0].inc(g.dma_start(w2b[:], w2_h[e]), 16)

        @blk.tensor
        def _(t):
            for e in range(NEXP):
                t.wait_ge(sW1a[0].h, 16 * (e + 1))
                t.wait_ge(sWoa[0].h, 16 * (e + 1))
                for hd in range(NEC):
                    for th in range(2):
                        grp = hd * 2 + th
                        i = e * 16 + grp
                        slot = grp % 2
                        psH = arena[:, slot * 1024:(slot + 1) * 1024]
                        if i >= 2:
                            t.wait_ge(sRelu.h, i - 1)
                        for half in range(2):
                            col = th * 1024 + half * 512
                            for dc in range(NEC):
                                ins = t.matmul(
                                    psH[:, half * 512:(half + 1) * 512],
                                    w1b[:, dc, hd * 128:(hd + 1) * 128],
                                    xnT[:, dc, col:col + 512],
                                    start=(dc == 0), stop=(dc == NEC - 1))
                        sM1h.inc(ins)
                        for j in (2 * grp, 2 * grp + 1):
                            wj = e * 32 + j
                            m, oc = j // 2, j % 2
                            sl = j % 3
                            psW = arena[:, 2048 + sl * 512:
                                        2048 + (sl + 1) * 512]
                            if j == 0 and e >= 1:
                                t.wait_ge(sAcc.h, 32 * e)
                            if wj >= 3:
                                t.wait_ge(sWacc.h, wj - 2)
                            sWm.inc(t.matmul(
                                psW, UT[:, e, m * 128:(m + 1) * 128],
                                wob[:, oc * 512:(oc + 1) * 512],
                                start=True, stop=True))
                t.wait_ge(sRelu.h, 16 * (e + 1))
                t.wait_ge(sW2a[0].h, 16 * (e + 1))
                t.wait_ge(sB2a[0].h, 16 * (e + 1))
                t.wait_ge(sWacc.h, 32 * (e + 1))
                for m in range(NT):
                    for oc in range(2):
                        i = e * 32 + m * 2 + oc
                        slot = (m * 2 + oc) % 2
                        psY = arena[:, 2048 + slot * 512:
                                    2048 + (slot + 1) * 512]
                        if i >= 2:
                            t.wait_ge(sAcc.h, i - 1)
                        for hc in range(NEC):
                            t.matmul(psY, hT[:, hc, m * 128:(m + 1) * 128],
                                     w2b[:, hc, oc * 512:(oc + 1) * 512],
                                     start=(hc == 0), stop=False)
                        ins = t.matmul(psY, ones_bf[0:1, 0:128],
                                       b2e[0:1, oc * 512:(oc + 1) * 512],
                                       start=False, stop=True)
                        sM2y.inc(ins)

        @blk.scalar
        def _(s):
            Ws = mkW(s, sCs)
            for e in range(NEXP):
                for hd in range(NEC):
                    for th in range(2):
                        i = e * 16 + hd * 2 + th
                        slot = (hd * 2 + th) % 2
                        psH = arena[:, slot * 1024:(slot + 1) * 1024]
                        s.wait_ge(sM1h.h, i + 1)
                        if e >= 1:
                            s.wait_ge(sM2y.h, 32 * e)
                        Ws(s.activation(
                            hT[:, hd, th * 1024:(th + 1) * 1024], psH, AF.Relu,
                            bias=b1c_sb[:, e * NEC + hd:e * NEC + hd + 1]),
                           sRelu)
            for m in range(NT):
                s.wait_ge(sXn.h, NT + m + 1)
                sYa[m % 2].inc(s.dma_start(y_h[m * 128:(m + 1) * 128, :],
                                           xr[m % 2][:]), 16)
            for i in range(2):
                s.wait_ge(sYa[i].h, sYa[i].n)

        @blk.vector
        def _(v):
            W = mkW(v, sCv)
            for e in range(NEXP):
                for j in range(32):
                    m, oc = j // 2, j % 2
                    sl = j % 3
                    psW = arena[:, 2048 + sl * 512:2048 + (sl + 1) * 512]
                    v.wait_ge(sWm.h, e * 32 + j + 1)
                    dst = acc[:, m, oc * 512:(oc + 1) * 512]
                    if e == 0:
                        ins = v.tensor_scalar(dst, psW, scales[:, 0, m:m + 1],
                                              None, OP.mult)
                    else:
                        ins = v.scalar_tensor_tensor(
                            dst, psW, scales[:, e, m:m + 1], dst,
                            OP.mult, OP.add)
                    W(ins, sWacc)
                for m in range(NT):
                    for oc in range(2):
                        i = e * 32 + m * 2 + oc
                        slot = (m * 2 + oc) % 2
                        psY = arena[:, 2048 + slot * 512:
                                    2048 + (slot + 1) * 512]
                        v.wait_ge(sM2y.h, i + 1)
                        dst = acc[:, m, oc * 512:(oc + 1) * 512]
                        W(v.scalar_tensor_tensor(
                            dst, psY, wall[:, m, e:e + 1], dst, OP.mult,
                            OP.add), sAcc)
                        if e == NEXP - 1 and oc == 1:
                            v.wait_ge(sXra[m % 2].h, 16 * (m // 2 + 1))
                            W(v.tensor_add(xr[m % 2][:], xr[m % 2][:],
                                           acc[:, m, :]), sXn)
    esU.close()

    if _STAGE < 7:
        emit_taps(dict(acc=acc, hT=hT, wall=wall))
        esD.close(); esC.close(); es.close()
        return taps

    emit_taps(dict(acc=acc))
    esD.close(); esC.close(); es.close()
    return taps


def kernel(**inputs):
    x_full = np.asarray(inputs["L0_attn_x"], dtype=np.float32)
    d = _host_prep(inputs)

    nc = bass.Bass("TRN2", target_bir_lowering=False)
    build(nc)

    in_maps = []
    for c in range(8):
        m = dict(d)
        m["x"] = np.ascontiguousarray(x_full[c])
        in_maps.append(m)

    res = run_bass_kernel_spmd(nc, in_maps, core_ids=list(range(8)))
    out = np.stack([res.results[c]["y"] for c in range(8)], axis=0)
    if _TAPS:
        kernel.last_taps = [{t: res.results[c]["tap_" + t] for t in _TAPS}
                            for c in range(8)]
    return out


# version 18
# speedup vs baseline: 1.3429x; 1.0143x over previous
import math
from contextlib import ExitStack

import numpy as np

import concourse.bass as bass
import concourse.mybir as mybir
from concourse.bass_utils import run_bass_kernel_spmd

F32 = mybir.dt.float32
BF16 = mybir.dt.bfloat16
AF = mybir.ActivationFunctionType
OP = mybir.AluOpType
AX = mybir.AxisListType.X

T = 2048
E = 1024
NT = 16
NEC = 8
NPAIR = 8
NEXP = 8
LI = 0.8 - 0.6 * math.exp(-0.3)

_STAGE = 99
_TAPS = []


class SemT:
    def __init__(self, h):
        self.h = h
        self.n = 0

    def inc(self, instr, k=1):
        instr.then_inc(self.h, k)
        self.n += k
        return instr


def _host_prep(inputs):
    Wq = np.asarray(inputs["L0_attn_Wq"], dtype=np.float32)
    Wk = np.asarray(inputs["L0_attn_Wk"], dtype=np.float32)
    Wv = np.asarray(inputs["L0_attn_Wv"], dtype=np.float32)
    Wo = np.asarray(inputs["L0_attn_Wo"], dtype=np.float32)

    def permute_qk(W):
        Wr = W.reshape(16, 32, 2, E)
        Wp = np.concatenate([Wr[:, :, 0, :], Wr[:, :, 1, :]], axis=1)
        Wt = Wp.reshape(E, E).T
        return np.ascontiguousarray(Wt.reshape(NEC, 128, E).transpose(1, 0, 2))

    d = {}
    d["wqt"] = permute_qk(Wq)
    d["wkt"] = permute_qk(Wk)
    d["wvt"] = np.ascontiguousarray(Wv.T.reshape(NEC, 128, E).transpose(1, 0, 2))
    d["wot"] = np.ascontiguousarray(Wo.T.reshape(NPAIR, 128, E).transpose(1, 0, 2))

    dd = 64
    inv_freq = 1.0 / (10000.0 ** (np.arange(0, dd, 2, dtype=np.float64) / dd))
    pos = np.arange(T, dtype=np.float64)
    freqs = np.repeat(pos[:, None] * inv_freq[None, :], 2, axis=-1)
    sv = (np.arange(0, dd, 2, dtype=np.float64) + 0.4 * dd) / (1.4 * dd)
    power = (pos - T // 2) / 512.0
    scale = sv[None, :] ** power[:, None]
    scale = np.concatenate([scale, scale], axis=-1)
    perm = np.concatenate([np.arange(0, 64, 2), np.arange(1, 64, 2)])
    swap = (np.arange(64) + 32) % 64
    sign = np.where(np.arange(64) < 32, -1.0, 1.0)
    cos_p = np.cos(freqs)[:, perm].T
    sin_p = np.sin(freqs)[:, perm].T
    sc_p = scale[:, perm].T

    def dup(t):
        return np.ascontiguousarray(np.concatenate([t, t], axis=0), dtype=np.float32)

    d["cq"] = dup(cos_p * sc_p * 0.125)
    d["sq"] = dup(((sin_p * sc_p * 0.125) * sign[:, None])[swap, :])
    d["ck"] = dup(cos_p / sc_p)
    d["sk"] = dup(((sin_p / sc_p) * sign[:, None])[swap, :])

    Wg = np.asarray(inputs["L0_ffn_W_gate"], np.float32)
    d["wg"] = np.ascontiguousarray(
        Wg.reshape(NEC, 128, NEXP).transpose(1, 0, 2).reshape(128, NEC * NEXP))
    d["bgbc"] = np.ascontiguousarray(
        np.broadcast_to(np.asarray(inputs["L0_ffn_b_gate"], np.float32)[None, :],
                        (128, NEXP)))
    W1 = np.asarray(inputs["L0_ffn_W1"], np.float32)
    d["w1"] = np.ascontiguousarray(W1.reshape(NEXP, NEC, 128, E).transpose(0, 2, 1, 3))
    W2 = np.asarray(inputs["L0_ffn_W2"], np.float32)
    d["w2"] = np.ascontiguousarray(W2.reshape(NEXP, NEC, 128, E).transpose(0, 2, 1, 3))
    b1 = np.asarray(inputs["L0_ffn_b1"], np.float32)
    d["b1c"] = np.ascontiguousarray(
        b1.reshape(NEXP, NEC, 128).transpose(2, 0, 1).reshape(128, NEXP * NEC))
    d["b2"] = np.ascontiguousarray(
        np.asarray(inputs["L0_ffn_b2"], np.float32).reshape(1, -1))

    lam = np.concatenate([
        np.asarray(inputs["L0_attn_lambda_q1"], np.float32),
        np.asarray(inputs["L0_attn_lambda_k1"], np.float32),
        np.asarray(inputs["L0_attn_lambda_q2"], np.float32),
        np.asarray(inputs["L0_attn_lambda_k2"], np.float32)])
    d["lam"] = np.ascontiguousarray(np.broadcast_to(lam[None, :], (128, 256)))

    d["cvec"] = np.ascontiguousarray(np.broadcast_to(
        np.array([1e-5, -0.5, -1.0, 1.0], np.float32)[None, :], (128, 4)))
    d["iden"] = np.eye(128, dtype=np.float32)
    d["ones"] = np.ones((1, 512), dtype=np.float32)
    d["inv128"] = np.full((128, 1), 1.0 / 128.0, dtype=np.float32)
    return d


def build(nc):
    taps = {}

    def dram(name, shape, dtype=F32, kind="ExternalInput"):
        return nc.dram_tensor(name, shape, dtype, kind=kind).ap()

    x_h = dram("x", [T, E])
    wqt_h = dram("wqt", [128, NEC, E])
    wkt_h = dram("wkt", [128, NEC, E])
    wvt_h = dram("wvt", [128, NEC, E])
    wot_h = dram("wot", [128, NPAIR, E])
    wg_h = dram("wg", [128, NEC * NEXP])
    bgbc_h = dram("bgbc", [128, NEXP])
    w1_h = dram("w1", [NEXP, 128, NEC, E])
    w2_h = dram("w2", [NEXP, 128, NEC, E])
    b1c_h = dram("b1c", [128, NEXP * NEC])
    b2_h = dram("b2", [1, NEXP * E])
    lam_h = dram("lam", [128, 256])
    cq_h = dram("cq", [128, T])
    sq_h = dram("sq", [128, T])
    ck_h = dram("ck", [128, T])
    sk_h = dram("sk", [128, T])
    iden_h = dram("iden", [128, 128])
    ones_h = dram("ones", [1, 512])
    inv128_h = dram("inv128", [128, 1])
    cvec_h = dram("cvec", [128, 4])
    y_h = dram("y", [T, E], kind="ExternalOutput")
    tap_shapes = {
        "xnT": [128, NEC, T], "wall": [128, NT, NEXP], "kbf": [128, NPAIR, T],
        "V": [128, NT, E], "UT": [128, NPAIR, T], "scales": [128, NPAIR, NT],
        "acc": [128, NT, E], "qbf0": [128, T], "e0b": [128, T], "e1b": [128, T],
        "sums0": [128, NT], "s1a": [128, NT], "s1b": [128, NT],
        "hT": [128, NEC, T], "neglam": [128, 1],
    }
    for tname in _TAPS:
        taps[tname] = dram("tap_" + tname, tap_shapes[tname], F32,
                           kind="ExternalOutput")

    es = ExitStack()
    S = lambda name: SemT(es.enter_context(nc.semaphore(name)))

    sX = S("sX"); sLN = S("sLN"); sSm = S("sSm"); sRs = S("sRs"); sXn = S("sXn")
    sTp = S("sTp"); sEv = S("sEv")
    sG = S("sG"); sGv = S("sGv")
    sKm = S("sKm"); sKc = S("sKc"); sVm = S("sVm"); sVc = S("sVc")
    sWq = S("sWq"); sQm = S("sQm"); sQc = S("sQc"); sQr = S("sQr"); sKr = S("sKr"); sSub = S("sSub")
    sS0 = S("sS0"); sS1a = S("sS1a"); sS1b = S("sS1b")
    sSt0 = S("sSt0"); sSt1 = S("sSt1"); sExp = S("sExp")
    sRho1 = S("sRho1"); sRl = S("sRl"); sRho2 = S("sRho2")
    sDiff = S("sDiff"); sTr = S("sTr"); sTe = S("sTe")
    sPv = S("sPv"); sSq = S("sSq"); sMm = S("sMm"); sMc = S("sMc")
    sScT = S("sScT"); sSm2 = S("sSm2"); sAct = S("sAct"); sRst = S("sRst")
    sWo = S("sWo"); sWm = S("sWm"); sWacc = S("sWacc")
    sW1L = S("sW1L"); sW2L = S("sW2L"); sM1h = S("sM1h"); sRelu = S("sRelu")
    sM2y = S("sM2y"); sAcc = S("sAcc"); sB2L = S("sB2L")
    sXr = S("sXr"); sY = S("sY"); sB0 = S("sB0"); sB0g = S("sB0g"); sTap = S("sTap")
    sXa = [S(f"sXa{i}") for i in range(4)]
    sWqa = [S(f"sWqa{i}") for i in range(2)]
    sWoa = [S(f"sWoa{i}") for i in range(2)]
    sW1a = [S(f"sW1a{i}") for i in range(2)]
    sW2a = [S(f"sW2a{i}") for i in range(2)]
    sB2a = [S(f"sB2a{i}") for i in range(2)]
    sXra = [S(f"sXra{i}") for i in range(4)]
    sYa = [S(f"sYa{i}") for i in range(4)]
    sCv = S("sCv"); sCs = S("sCs")

    def mkW(eng, chain_sem):
        def W(instr, sem=None):
            s = sem if sem is not None else chain_sem
            s.inc(instr)
            eng.wait_ge(s.h, s.n)
            return instr
        return W

    def emit_taps(available):
        live = [(n, taps[n], available[n]) for n in taps if n in available]
        if not live:
            return
        with nc.Block("taps") as blk:
            @blk.gpsimd
            def _(g):
                for name, hap, src in live:
                    sTap.inc(g.dma_start(hap[:], src[:]), 16)
                g.wait_ge(sTap.h, sTap.n)

    sb = lambda *a, **k: es.enter_context(nc.sbuf_tensor(*a, **k))

    xnT = sb("xnT", [128, NEC, T], BF16)
    wall = sb("wall", [128, NT, NEXP], F32)
    scales = sb("scales", [128, NPAIR, NT], F32)
    neglam = sb("neglam", [128, 1], F32)
    iden_bf = sb("iden_bf", [128, 128], BF16)
    ones_bf = sb("ones_bf", [1, 512], BF16)
    inv128_bf = sb("inv128_bf", [128, 1], BF16)
    b1c_sb = sb("b1c_sb", [128, NEXP * NEC], F32)
    cvec_sb = sb("cvec_sb", [128, 4], F32)

    arena = es.enter_context(nc.psum_tensor("arena", [128, 3584], F32))
    psT = es.enter_context(nc.psum_tensor("psT", [128, 8, 128], BF16))

    esA = ExitStack()
    sbA = lambda *a, **k: esA.enter_context(nc.sbuf_tensor(*a, **k))
    kbf = sbA("kbf", [128, NPAIR, T], BF16)
    kbf_sub = sbA("kbf_sub", [128, NPAIR, 256], BF16)
    Vsb = sbA("V", [128, NT, E], BF16)
    cq_bf = sbA("cq_bf", [128, T], BF16)
    sq_bf = sbA("sq_bf", [128, T], BF16)
    qraw = sbA("qraw", [128, T], BF16)
    t1b = sbA("t1b", [128, T], BF16)
    usw = sbA("usw", [128, T], BF16)

    esA1 = ExitStack()
    sbA1 = lambda *a, **k: esA1.enter_context(nc.sbuf_tensor(*a, **k))
    ck_bf = sbA1("ck_bf", [128, T], BF16)
    sk_bf = sbA1("sk_bf", [128, T], BF16)
    wkt_bf = sbA1("wkt_bf", [128, NEC, E], BF16)
    wvt_bf = sbA1("wvt_bf", [128, NEC, E], BF16)
    xt = [sbA1(f"xt{i}", [128, E], F32) for i in range(4)]
    xnb = [sbA1(f"xnb{i}", [128, E], BF16) for i in range(2)]
    ln_s1 = sbA1("ln_s1", [128, 1], F32)
    ln_s2 = sbA1("ln_s2", [128, 1], F32)
    ln_mean = sbA1("ln_mean", [128, 1], F32)
    ln_m2 = sbA1("ln_m2", [128, 1], F32)
    ln_var = sbA1("ln_var", [128, 1], F32)
    ln_ln = sbA1("ln_ln", [128, 1], F32)
    ln_rs = sbA1("ln_rs", [128, 1], F32)
    gsb = sbA1("gsb", [128, NEXP], F32)
    top8 = sbA1("top8", [128, NEXP], F32)
    gesb = sbA1("gesb", [128, NEXP], F32)
    wg_bf = sbA1("wg_bf", [128, NEC * NEXP], BF16)
    bgbc_sb = sbA1("bgbc_sb", [128, NEXP], F32)
    lam_sb = sbA1("lam_sb", [128, 256], F32)
    lprod = sbA1("lprod", [128, 64], F32)
    ld12 = sbA1("ld12", [128, 2], F32)
    le12 = sbA1("le12", [128, 2], F32)

    with nc.Block("b0") as blk:
        @blk.sync
        def _(sy):
            sB0.inc(sy.dma_start(bgbc_sb[:], bgbc_h[:]), 16)
            sB0.inc(sy.dma_start(lam_sb[:], lam_h[:]), 16)
            sB0.inc(sy.dma_start(b1c_sb[:], b1c_h[:]), 16)
            sB0.inc(sy.dma_start(cvec_sb[:], cvec_h[:]), 16)
            sy.wait_ge(sB0.h, sB0.n)

        @blk.gpsimd
        def _(g):
            for dst, src in ((iden_bf, iden_h), (ones_bf, ones_h),
                             (inv128_bf, inv128_h), (wg_bf, wg_h)):
                sB0g.inc(g.dma_start(dst[:], src[:]), 16)
            g.wait_ge(sB0g.h, sB0g.n)

        @blk.vector
        def _(v):
            W = mkW(v, sCv)
            v.wait_ge(sB0.h, 64)
            W(v.tensor_mul(lprod[:], lam_sb[:, 0:64], lam_sb[:, 64:128]))
            W(v.reduce_sum(ld12[:, 0:1], lprod[:], axis=AX))
            W(v.tensor_mul(lprod[:], lam_sb[:, 128:192], lam_sb[:, 192:256]))
            W(v.reduce_sum(ld12[:, 1:2], lprod[:], axis=AX), sGv)

        @blk.scalar
        def _(s):
            s.wait_ge(sGv.h, 1)
            Ws = mkW(s, sCs)
            Ws(s.activation(le12[:], ld12[:], AF.Exp), sRs)

        @blk.vector
        def _(v):
            W = mkW(v, sCv)
            v.wait_ge(sRs.h, 1)
            W(v.tensor_sub(neglam[:], le12[:, 1:2], le12[:, 0:1]), sGv)
            W(v.tensor_scalar(neglam[:], neglam[:], LI, None, OP.subtract))

    with nc.Block("b0b") as blk:
        @blk.gpsimd
        def _(g):
            for dst, src in ((cq_bf, cq_h), (sq_bf, sq_h), (ck_bf, ck_h),
                             (sk_bf, sk_h), (wkt_bf, wkt_h), (wvt_bf, wvt_h)):
                sB0g.inc(g.dma_start(dst[:], src[:]), 16)
            g.wait_ge(sB0g.h, sB0g.n)

        @blk.sync
        def _(sy):
            for m in range(4):
                sXa[m].inc(sy.dma_start(xt[m][:], x_h[m * 128:(m + 1) * 128, :]), 16)

    if _STAGE < 1:
        emit_taps({})
        esA1.close(); esA.close(); es.close()
        return taps

    gv0 = sGv.n

    def gate_comb2(v, W, m):
        psG = arena[:, 2048 + (m % 2) * 512:2048 + (m % 2) * 512 + 8]
        v.wait_ge(sG.h, m + 1)
        W(v.tensor_add(gsb[:], psG, bgbc_sb[:]), sGv)
        W(v.tensor_scalar(gesb[:], gsb[:], 1.0, None, OP.mult, OP.max,
                          accum_out=top8[:, 0:1]), sGv)
        W(v.tensor_scalar(gesb[:], gsb[:], top8[:, 0:1], None, OP.is_ge), sGv)
        W(v.scalar_tensor_tensor(gesb[:], gesb[:], -1e30, gsb[:],
                                 OP.mult, OP.add), sGv)
        W(v.tensor_scalar(gesb[:], gesb[:], 1.0, None, OP.mult, OP.max,
                          accum_out=top8[:, 1:2]), sGv)
        W(v.tensor_scalar(gesb[:], gsb[:], top8[:, 1:2], None, OP.is_ge), sGv)
        W(v.tensor_mul(wall[:, m, :], gsb[:], gesb[:]), sGv)

    with nc.Block("p1") as blk:
        @blk.sync
        def _(sy):
            for m in range(4, NT):
                sy.wait_ge(sXn.h, m - 3)
                sXa[m % 4].inc(sy.dma_start(xt[m % 4][:],
                                            x_h[m * 128:(m + 1) * 128, :]), 16)

        @blk.vector
        def _(v):
            W = mkW(v, sCv)
            for m in range(NT):
                v.wait_ge(sXa[m % 4].h, 16 * (m // 4 + 1))
                if m >= 2:
                    v.wait_ge(sTp.h, m - 1)
                W(v.reduce_sum(ln_s1[:], xt[m % 4][:], axis=AX))
                W(v.scalar_tensor_tensor(
                    xnb[m % 2][:], xt[m % 4][:], 1.0, xt[m % 4][:],
                    OP.bypass, OP.mult, accum_out=ln_s2[:]), sLN)
                W(v.tensor_scalar(ln_mean[:], ln_s1[:], 1.0 / E, None, OP.mult))
                W(v.tensor_mul(ln_m2[:], ln_mean[:], ln_mean[:]))
                W(v.scalar_tensor_tensor(
                    ln_var[:], ln_s2[:], 1.0 / E, ln_m2[:], OP.mult, OP.subtract),
                  sSm)
                v.wait_ge(sRs.h, 1 + m + 1)
                W(v.tensor_scalar(xnb[m % 2][:], xt[m % 4][:], ln_mean[:],
                                  ln_rs[:], OP.subtract, OP.mult), sXn)
                if m >= 1:
                    v.wait_ge(sTp.h, m)
                    W(v.tensor_copy(xnT[:, :, (m - 1) * 128:m * 128], psT[:]), sEv)

        @blk.scalar
        def _(s):
            Ws = mkW(s, sCs)
            def vevac(s, Ws, m):
                slot = m % 2
                psV = arena[:, 2048:3072] if slot == 0 else arena[:, 0:1024]
                s.wait_ge(sVm.h, m + 1)
                Ws(s.activation(Vsb[:, m, :], psV, AF.Identity), sVc)
            for m in range(NT):
                s.wait_ge(sSm.h, m + 1)
                Ws(s.activation(ln_ln[:], ln_var[:], AF.Ln, bias=cvec_sb[:, 0:1]),
                   sAct)
                Ws(s.activation(ln_rs[:], ln_ln[:], AF.Exp, scale=cvec_sb[:, 1:2]),
                   sRs)
                if m >= 2:
                    vevac(s, Ws, m - 2)
            vevac(s, Ws, NT - 2)
            vevac(s, Ws, NT - 1)

        @blk.tensor
        def _(t):
            def vmm(m):
                slot = m % 2
                psV = arena[:, 2048:3072] if slot == 0 else arena[:, 0:1024]
                t.wait_ge(sEv.h, m + 1)
                if m >= 2:
                    t.wait_ge(sVc.h, m - 1)
                for jc in range(2):
                    for ec in range(NEC):
                        ins = t.matmul(psV[:, jc * 512:(jc + 1) * 512],
                                       xnT[:, ec, m * 128:(m + 1) * 128],
                                       wvt_bf[:, ec, jc * 512:(jc + 1) * 512],
                                       start=(ec == 0), stop=(ec == NEC - 1))
                sVm.inc(ins)
            for m in range(NT):
                t.wait_ge(sXn.h, m + 1)
                if m >= 1:
                    t.wait_ge(sEv.h, m)
                for k in range(NEC):
                    ins = t.transpose(psT[:, k, :],
                                      xnb[m % 2][:, k * 128:(k + 1) * 128], iden_bf[:])
                sTp.inc(ins)
                if m >= 1:
                    vmm(m - 1)
            vmm(NT - 1)

        @blk.vector
        def _(v):
            W = mkW(v, sCv)
            v.wait_ge(sTp.h, NT)
            sEv.inc(v.tensor_copy(xnT[:, :, (NT - 1) * 128:NT * 128], psT[:]))
            v.wait_ge(sEv.h, sEv.n)

    if _STAGE < 2:
        emit_taps(dict(xnT=xnT, neglam=neglam))
        esA1.close(); esA.close(); es.close()
        return taps

    if _STAGE < 3:
        emit_taps(dict(xnT=xnT, wall=wall, neglam=neglam))
        esA1.close(); esA.close(); es.close()
        return taps

    with nc.Block("p2") as blk:
        @blk.tensor
        def _(t):
            psK = arena[:, 0:2048]
            for p in range(NPAIR):
                if p >= 1:
                    t.wait_ge(sKc.h, p)
                for tc in range(4):
                    for ec in range(NEC):
                        ins = t.matmul(psK[:, tc * 512:(tc + 1) * 512],
                                       wkt_bf[:, ec, p * 128:(p + 1) * 128],
                                       xnT[:, ec, tc * 512:(tc + 1) * 512],
                                       start=(ec == 0), stop=(ec == NEC - 1))
                sKm.inc(ins)
                for m in (2 * p, 2 * p + 1):
                    psG = arena[:, 2048 + (m % 2) * 512:2048 + (m % 2) * 512 + 8]
                    if m >= 2:
                        t.wait_ge(sGv.h, gv0 + 7 * (m - 1))
                    for dc in range(NEC):
                        ins = t.matmul(psG, xnT[:, dc, m * 128:(m + 1) * 128],
                                       wg_bf[:, dc * 8:(dc + 1) * 8],
                                       start=(dc == 0), stop=(dc == NEC - 1))
                    sG.inc(ins)

        @blk.vector
        def _(v):
            W = mkW(v, sCv)
            psK = arena[:, 0:2048]
            for p in range(NPAIR):
                v.wait_ge(sKm.h, p + 1)
                W(v.tensor_copy(qraw[:], psK), sKc)
                W(v.tensor_mul(t1b[:], qraw[:], ck_bf[:]))
                W(v.tensor_mul(usw[0:32, :], qraw[32:64, :], sk_bf[32:64, :]))
                W(v.tensor_mul(usw[32:64, :], qraw[0:32, :], sk_bf[0:32, :]))
                W(v.tensor_mul(usw[64:96, :], qraw[96:128, :], sk_bf[96:128, :]))
                W(v.tensor_mul(usw[96:128, :], qraw[64:96, :], sk_bf[64:96, :]))
                W(v.tensor_add(kbf[:, p, :], t1b[:], usw[:]), sKr)
                W(v.tensor_copy(kbf_sub[:, p, :], kbf[:, p, 0:2048:8]))
                for gm in (2 * p, 2 * p + 1):
                    gate_comb2(v, W, gm)

    if _STAGE < 4:
        emit_taps(dict(xnT=xnT, wall=wall, kbf=kbf, V=Vsb, neglam=neglam))
        esA1.close(); esA.close(); es.close()
        return taps

    esA1.close()
    esU = ExitStack()
    UT = esU.enter_context(nc.sbuf_tensor("UT", [128, NPAIR, T], BF16, side="right"))

    esA2 = ExitStack()
    sbA2 = lambda *a, **k: esA2.enter_context(nc.sbuf_tensor(*a, **k))
    qbf = [sbA2(f"qbf{i}", [128, T], BF16) for i in range(2)]
    e0b = [sbA2(f"e0b{i}", [128, T], BF16) for i in range(2)]
    e1b = [sbA2(f"e1b{i}", [128, T], BF16) for i in range(2)]
    diffb = [sbA2(f"diffb{i}", [128, T], BF16) for i in range(3)]
    diffT = [sbA2(f"diffT{i}", [128, NT, 128], BF16) for i in range(2)]
    sums0 = sbA2("sums0", [128, NT], F32)
    nb_all = sbA2("nb_all", [128, 2 * NT], F32)
    csums = [sbA2(f"csums{i}", [128, 4], F32) for i in range(2)]
    sjunk = sbA2("sjunk", [128, 256], BF16)
    s01 = sbA2("s01", [128, 2], F32)
    s1t = sbA2("s1t", [128, 1], F32)
    ab = sbA2("ab", [128, NT], F32)
    rhot = sbA2("rhot", [128, 1], F32)
    nrho = [sbA2(f"nrho{i}", [128, 1], F32) for i in range(2)]
    sqb = sbA2("sqb", [128, 128], BF16)
    s0sq = sbA2("s0sq", [128, NT], F32)
    innerb = sbA2("innerb", [128, NT], F32)
    lnb = sbA2("lnb", [128, NT], F32)
    rstb = sbA2("rstb", [128, NT], F32)
    wqb = [sbA2(f"wqb{i}", [128, NEC, 128], BF16) for i in range(2)]

    psQ = arena[:, 0:2048]
    SLOT = [arena[:, 0:1024], arena[:, 1024:2048], arena[:, 2048:3072]]
    psU = arena[:, 3072:3200]
    psMsq = arena[0:1, 3200:3328]
    psSc = arena[:, 3328:3344]

    def p3_items():
        items = [("sub", 0), ("sub", 1)]
        for m in range(NT):
            if m + 2 < NT:
                items.append(("sub", m + 2))
            for c in range(4):
                items.append(("main", m, c))
        return items

    def p3_plan():
        plans = []
        slot_state = [None, None, None]
        n_sub = 0
        n_main = 0
        for p in range(NPAIR):
            plan = []
            for j, it in enumerate(p3_items()):
                k = len(plans) * 80 + j
                slot = k % 3
                waits = []
                prev = slot_state[slot]
                if prev is not None:
                    waits.append(prev)
                if j < 3:
                    waits.append(("sQc", p + 1))
                if it[0] == "sub":
                    n_sub += 1
                    slot_state[slot] = ("sSt0", 2 * n_sub)
                    plan.append((it, slot, waits, n_sub, None))
                else:
                    n_main += 1
                    slot_state[slot] = ("sExp", n_main)
                    plan.append((it, slot, waits, None, n_main))
            plans.append(plan)
        return plans

    P3PLANS = p3_plan()

    def tail_pieces(t, p, m, piece):
        mm = NT * p + m
        if piece == 0:
            t.wait_ge(sDiff.h, mm + 1)
            if mm >= 1:
                t.wait_ge(sTe.h, 2 * mm)
            for sc in range(8):
                ins = t.transpose(psT[:, sc, :],
                                  diffb[m % 3][:, sc * 128:(sc + 1) * 128],
                                  iden_bf[:])
            sTr.inc(ins)
        elif piece == 1:
            t.wait_ge(sTe.h, 2 * mm + 1)
            for sc in range(8):
                ins = t.transpose(psT[:, sc, :],
                                  diffb[m % 3][:, (8 + sc) * 128:(9 + sc) * 128],
                                  iden_bf[:])
            sTr.inc(ins)
        elif piece == 2:
            t.wait_ge(sTe.h, 2 * mm + 1)
            if m == 0 and p >= 1:
                t.wait_ge(sSm2.h, p)
            if mm >= 1:
                t.wait_ge(sSq.h, mm)
            for sc in range(8):
                t.matmul(psU[:], Vsb[:, sc, p * 128:(p + 1) * 128],
                         diffT[m % 2][:, sc, :],
                         start=(sc == 0), stop=False)
        elif piece == 3:
            t.wait_ge(sTe.h, 2 * mm + 2)
            for sc in range(8, NT):
                ins = t.matmul(psU[:], Vsb[:, sc, p * 128:(p + 1) * 128],
                               diffT[m % 2][:, sc, :],
                               start=False, stop=(sc == NT - 1))
            sPv.inc(ins)
        else:
            t.wait_ge(sSq.h, mm + 1)
            sMm.inc(t.matmul(psSc[:, m:m + 1], sqb[:], inv128_bf[:],
                             start=True, stop=True))

    with nc.Block("p3") as blk:
        @blk.gpsimd
        def _(g):
            sWqa[0].inc(g.dma_start(wqb[0][:], wqt_h[:, :, 0:128]), 16)
            for p in range(1, NPAIR):
                if p >= 2:
                    g.wait_ge(sQm.h, p - 1)
                sWqa[p % 2].inc(g.dma_start(wqb[p % 2][:],
                                            wqt_h[:, :, p * 128:(p + 1) * 128]), 16)

        @blk.tensor
        def _(t):
            semmap = {"sQc": sQc, "sSt0": sSt0, "sExp": sExp}

            def emit_item(item):
                (it, slot, waits, nsub, nmain) = item
                for (semname, val) in waits:
                    t.wait_ge(semmap[semname].h, val)
                if it[0] == "sub":
                    m = it[1]
                    if m < 2:
                        t.wait_ge(sQr.h, p + 1)
                    for h in range(2):
                        ins = t.matmul(
                            SLOT[slot][:, h * 512:h * 512 + 256],
                            qbf[p % 2][h * 64:(h + 1) * 64,
                                       m * 128:(m + 1) * 128],
                            kbf_sub[h * 64:(h + 1) * 64, p, :],
                            start=True, stop=True, tile_position=(64 * h, 0))
                    sSub.inc(ins)
                else:
                    m, c = it[1], it[2]
                    h = c // 2
                    half = c % 2
                    for sc in range(2):
                        col = half * 1024 + sc * 512
                        ins = t.matmul(
                            SLOT[slot][:, sc * 512:(sc + 1) * 512],
                            qbf[p % 2][h * 64:(h + 1) * 64,
                                       m * 128:(m + 1) * 128],
                            kbf[h * 64:(h + 1) * 64, p, col:col + 512],
                            start=True, stop=True, tile_position=(64 * h, 0))
                    sS0.inc(ins)

            for p in range(NPAIR):
                plan = P3PLANS[p]
                t.wait_ge(sWqa[p % 2].h, 16 * (p // 2 + 1))
                if p >= 1:
                    t.wait_ge(sExp.h, 64 * p)
                    t.wait_ge(sSt0.h, 32 * p)
                for tc in range(4):
                    for ec in range(NEC):
                        ins = t.matmul(psQ[:, tc * 512:(tc + 1) * 512],
                                       wqb[p % 2][:, ec, :],
                                       xnT[:, ec, tc * 512:(tc + 1) * 512],
                                       start=(ec == 0), stop=(ec == NEC - 1))
                sQm.inc(ins)
                by_tile = [[] for _ in range(NT)]
                head = []
                for item in plan:
                    it = item[0]
                    if it[0] == "sub" and it[1] < 2:
                        head.append(item)
                    elif it[0] == "sub":
                        by_tile[it[1] - 2].append(item)
                    else:
                        by_tile[it[1]].append(item)
                for item in head:
                    emit_item(item)
                for m in range(NT):
                    items = by_tile[m]
                    mains = items[-4:]
                    subs = items[:-4]
                    for item in subs:
                        emit_item(item)
                    emit_item(mains[0])
                    if m >= 2:
                        tail_pieces(t, p, m - 2, 0)
                    emit_item(mains[1])
                    if m >= 2:
                        tail_pieces(t, p, m - 2, 1)
                    emit_item(mains[2])
                    if m >= 2:
                        tail_pieces(t, p, m - 2, 2)
                    emit_item(mains[3])
                    if m >= 2:
                        tail_pieces(t, p, m - 2, 3)
                    if m >= 3:
                        tail_pieces(t, p, m - 3, 4)
                for piece in range(4):
                    tail_pieces(t, p, NT - 2, piece)
                tail_pieces(t, p, NT - 3, 4)
                for piece in range(4):
                    tail_pieces(t, p, NT - 1, piece)
                tail_pieces(t, p, NT - 2, 4)
                tail_pieces(t, p, NT - 1, 4)

        @blk.vector
        def _(v):
            W = mkW(v, sCv)
            nsub_g = 0
            for p in range(NPAIR):
                plan = P3PLANS[p]
                v.wait_ge(sQm.h, p + 1)
                W(v.tensor_copy(qraw[:], psQ), sQc)
                W(v.tensor_mul(t1b[:], qraw[:], cq_bf[:]))
                W(v.tensor_mul(usw[0:32, :], qraw[32:64, :], sq_bf[32:64, :]))
                W(v.tensor_mul(usw[32:64, :], qraw[0:32, :], sq_bf[0:32, :]))
                W(v.tensor_mul(usw[64:96, :], qraw[96:128, :], sq_bf[96:128, :]))
                W(v.tensor_mul(usw[96:128, :], qraw[64:96, :], sq_bf[64:96, :]))
                W(v.tensor_add(qbf[p % 2][:], t1b[:], usw[:]), sQr)
                subslot = {it[1]: slot for (it, slot, w_, ns, nm) in plan
                           if it[0] == "sub"}
                for m in range(NT):
                    mm = NT * p + m
                    if m >= 2:
                        v.wait_ge(sTr.h, 2 * (mm - 2) + 1)
                        W(v.tensor_copy(diffT[(m - 2) % 2][:, 0:8, :], psT[:]),
                          sTe)
                        v.wait_ge(sTr.h, 2 * (mm - 2) + 2)
                        W(v.tensor_copy(diffT[(m - 2) % 2][:, 8:16, :], psT[:]),
                          sTe)
                    for msub in ([0, 1] if m == 0 else []) + \
                            ([m + 2] if m + 2 < NT else []):
                        nsub_g += 1
                        v.wait_ge(sSub.h, nsub_g)
                        if p >= 1:
                            v.wait_ge(sExp.h, 64 * (p - 1) + 4 * (msub + 1))
                        sl = subslot[msub]
                        for h in range(2):
                            W(v.tensor_scalar(
                                sjunk[:], SLOT[sl][:, h * 512:h * 512 + 256],
                                -1.0, None, OP.mult, OP.min,
                                accum_out=nb_all[:, h * NT + msub:
                                                 h * NT + msub + 1]), sSt0)
                    v.wait_ge(sExp.h, 64 * p + 4 * m + 4)
                    W(v.tensor_tensor(s01[:], csums[m % 2][:, 0:4:2],
                                      csums[m % 2][:, 1:4:2], OP.add))
                    W(v.tensor_copy(sums0[:, m:m + 1], s01[:, 0:1]))
                    W(v.tensor_scalar(s1t[:], s01[:, 1:2], 2.0 ** -40, None,
                                      OP.mult))
                    W(v.reciprocal(rhot[:], s1t[:]))
                    W(v.tensor_mul(rhot[:], rhot[:], s01[:, 0:1]))
                    if mm >= 3:
                        v.wait_ge(sTr.h, 2 * (mm - 3) + 2)
                    W(v.tensor_scalar(nrho[m % 2][:], rhot[:], neglam[:],
                                      2.0 ** -40, OP.mult, OP.mult))
                    W(v.tensor_scalar(t1b[:], e1b[m % 2][:], nrho[m % 2][:],
                                      None, OP.mult))
                    W(v.tensor_add(diffb[m % 3][:], t1b[:], e0b[m % 2][:]), sDiff)
                for mf in (NT - 2, NT - 1):
                    mmf = NT * p + mf
                    v.wait_ge(sTr.h, 2 * mmf + 1)
                    W(v.tensor_copy(diffT[mf % 2][:, 0:8, :], psT[:]), sTe)
                    v.wait_ge(sTr.h, 2 * mmf + 2)
                    W(v.tensor_copy(diffT[mf % 2][:, 8:16, :], psT[:]), sTe)
                v.wait_ge(sMm.h, NT * (p + 1))
                W(v.tensor_scalar(s0sq[:], sums0[:], 2.0 ** -40, None, OP.mult))
                W(v.reciprocal(ab[:], s0sq[:]))
                W(v.tensor_mul(s0sq[:], ab[:], ab[:]))
                W(v.tensor_tensor(innerb[:], psSc[:], s0sq[:], OP.mult))
                W(v.tensor_scalar(innerb[:], innerb[:], 2.0 ** -30, None, OP.mult),
                  sSm2)
                v.wait_ge(sRst.h, p + 1)
                W(v.tensor_mul(innerb[:], ab[:], rstb[:]))
                W(v.tensor_scalar(scales[:, p, :], innerb[:],
                                  (1.0 - LI) * 2.0 ** -40, None, OP.mult))

        @blk.scalar
        def _(s):
            Ws = mkW(s, sCs)
            for p in range(NPAIR):
                plan = P3PLANS[p]
                for (it, slot, waits, nsub, nmain) in plan:
                    if it[0] != "main":
                        continue
                    m, c = it[1], it[2]
                    mm = NT * p + m
                    h = c // 2
                    half = c % 2
                    s.wait_ge(sS0.h, nmain)
                    s.wait_ge(sSt0.h, 2 * (NT * p + m + 1))
                    if mm >= 2 and c == 0:
                        s.wait_ge(sDiff.h, mm - 1)
                    ebuf = e0b[m % 2] if h == 0 else e1b[m % 2]
                    Ws(s.activation(ebuf[:, half * 1024:(half + 1) * 1024],
                                    SLOT[slot][:], AF.Exp,
                                    bias=nb_all[:, h * NT + m:h * NT + m + 1],
                                    accum_out=csums[m % 2][:, c:c + 1]), sExp)
                    if c == 3 and m >= 2:
                        s.wait_ge(sPv.h, mm - 1)
                        Ws(s.activation(UT[:, p, (m - 2) * 128:(m - 1) * 128],
                                        psU[:], AF.Identity))
                        if mm >= 3:
                            s.wait_ge(sMm.h, mm - 2)
                        Ws(s.activation(sqb[:], psU[:], AF.Square,
                                        scale=2.0 ** -25), sSq)
                for mf in (NT - 2, NT - 1):
                    mmf = NT * p + mf
                    s.wait_ge(sPv.h, mmf + 1)
                    Ws(s.activation(UT[:, p, mf * 128:(mf + 1) * 128], psU[:],
                                    AF.Identity))
                    s.wait_ge(sMm.h, mmf)
                    Ws(s.activation(sqb[:], psU[:], AF.Square, scale=2.0 ** -25),
                       sSq)
                s.wait_ge(sSm2.h, p + 1)
                Ws(s.activation(lnb[:], innerb[:], AF.Ln, bias=cvec_sb[:, 0:1]),
                   sAct)
                Ws(s.activation(rstb[:], lnb[:], AF.Exp, scale=cvec_sb[:, 1:2]),
                   sRst)

    if _STAGE < 5:
        emit_taps(dict(xnT=xnT, wall=wall, kbf=kbf, V=Vsb, UT=UT, scales=scales,
                       sums0=sums0, qbf0=qbf[0],
                       e0b=e0b[0], e1b=e1b[0], neglam=neglam))
        esA2.close(); esU.close(); esA.close(); es.close()
        return taps

    esA2.close(); esA.close()
    esC = ExitStack()
    sbC = lambda *a, **k: esC.enter_context(nc.sbuf_tensor(*a, **k))
    acc = sbC("acc", [128, NT, E], F32)
    wob = sbC("wob", [128, E], BF16)
    esD = ExitStack()
    sbD = lambda *a, **k: esD.enter_context(nc.sbuf_tensor(*a, **k))
    w1b = sbD("w1b", [128, NEC, E], BF16)
    w2b = sbD("w2b", [128, NEC, E], BF16)
    hT = sbD("hT", [128, NEC, T], BF16)
    b2e = sbD("b2e", [1, E], BF16)
    xr = [sbD(f"xr{i}", [128, E], F32) for i in range(2)]

    with nc.Block("p45") as blk:
        @blk.sync
        def _(sy):
            for m in range(NT):
                if m >= 2:
                    sy.wait_ge(sYa[m % 2].h, 16 * (m // 2))
                sXra[m % 2].inc(sy.dma_start(xr[m % 2][:],
                                             x_h[m * 128:(m + 1) * 128, :]), 16)

        @blk.gpsimd
        def _(g):
            sWoa[0].inc(g.dma_start(wob[:], wot_h[:, 0, :]), 16)
            sW1a[0].inc(g.dma_start(w1b[:], w1_h[0]), 16)
            sB2a[0].inc(g.dma_start(b2e[:], b2_h[0:1, 0:E]), 16)
            sW2a[0].inc(g.dma_start(w2b[:], w2_h[0]), 16)
            for e in range(1, NEXP):
                g.wait_ge(sWm.h, 32 * e)
                sWoa[0].inc(g.dma_start(wob[:], wot_h[:, e, :]), 16)
                g.wait_ge(sM1h.h, 16 * e)
                sW1a[0].inc(g.dma_start(w1b[:], w1_h[e]), 16)
                g.wait_ge(sM2y.h, 32 * e)
                sB2a[0].inc(g.dma_start(b2e[:], b2_h[0:1, e * E:(e + 1) * E]),
                            16)
                sW2a[0].inc(g.dma_start(w2b[:], w2_h[e]), 16)

        @blk.tensor
        def _(t):
            for e in range(NEXP):
                t.wait_ge(sW1a[0].h, 16 * (e + 1))
                t.wait_ge(sWoa[0].h, 16 * (e + 1))
                for hd in range(NEC):
                    for th in range(2):
                        grp = hd * 2 + th
                        i = e * 16 + grp
                        slot = grp % 2
                        psH = arena[:, slot * 1024:(slot + 1) * 1024]
                        if i >= 2:
                            t.wait_ge(sRelu.h, i - 1)
                        for half in range(2):
                            col = th * 1024 + half * 512
                            for dc in range(NEC):
                                ins = t.matmul(
                                    psH[:, half * 512:(half + 1) * 512],
                                    w1b[:, dc, hd * 128:(hd + 1) * 128],
                                    xnT[:, dc, col:col + 512],
                                    start=(dc == 0), stop=(dc == NEC - 1))
                        sM1h.inc(ins)
                        for j in (2 * grp, 2 * grp + 1):
                            wj = e * 32 + j
                            m, oc = j // 2, j % 2
                            sl = j % 3
                            psW = arena[:, 2048 + sl * 512:
                                        2048 + (sl + 1) * 512]
                            if j == 0 and e >= 1:
                                t.wait_ge(sAcc.h, 32 * e)
                            if wj >= 3:
                                t.wait_ge(sWacc.h, wj - 2)
                            sWm.inc(t.matmul(
                                psW, UT[:, e, m * 128:(m + 1) * 128],
                                wob[:, oc * 512:(oc + 1) * 512],
                                start=True, stop=True))
                t.wait_ge(sRelu.h, 16 * (e + 1))
                t.wait_ge(sW2a[0].h, 16 * (e + 1))
                t.wait_ge(sB2a[0].h, 16 * (e + 1))
                t.wait_ge(sWacc.h, 32 * (e + 1))
                for m in range(NT):
                    for oc in range(2):
                        i = e * 32 + m * 2 + oc
                        slot = (m * 2 + oc) % 2
                        psY = arena[:, 2048 + slot * 512:
                                    2048 + (slot + 1) * 512]
                        if i >= 2:
                            t.wait_ge(sAcc.h, i - 1)
                        for hc in range(NEC):
                            t.matmul(psY, hT[:, hc, m * 128:(m + 1) * 128],
                                     w2b[:, hc, oc * 512:(oc + 1) * 512],
                                     start=(hc == 0), stop=False)
                        ins = t.matmul(psY, ones_bf[0:1, 0:128],
                                       b2e[0:1, oc * 512:(oc + 1) * 512],
                                       start=False, stop=True)
                        sM2y.inc(ins)

        @blk.scalar
        def _(s):
            Ws = mkW(s, sCs)
            for e in range(NEXP):
                for hd in range(NEC):
                    for th in range(2):
                        i = e * 16 + hd * 2 + th
                        slot = (hd * 2 + th) % 2
                        psH = arena[:, slot * 1024:(slot + 1) * 1024]
                        s.wait_ge(sM1h.h, i + 1)
                        if e >= 1:
                            s.wait_ge(sM2y.h, 32 * e)
                        Ws(s.activation(
                            hT[:, hd, th * 1024:(th + 1) * 1024], psH, AF.Relu,
                            bias=b1c_sb[:, e * NEC + hd:e * NEC + hd + 1]),
                           sRelu)
            for m in range(NT):
                s.wait_ge(sXn.h, NT + m + 1)
                sYa[m % 2].inc(s.dma_start(y_h[m * 128:(m + 1) * 128, :],
                                           xr[m % 2][:]), 16)
            for i in range(2):
                s.wait_ge(sYa[i].h, sYa[i].n)

        @blk.vector
        def _(v):
            W = mkW(v, sCv)
            for e in range(NEXP):
                for j in range(32):
                    m, oc = j // 2, j % 2
                    sl = j % 3
                    psW = arena[:, 2048 + sl * 512:2048 + (sl + 1) * 512]
                    v.wait_ge(sWm.h, e * 32 + j + 1)
                    dst = acc[:, m, oc * 512:(oc + 1) * 512]
                    if e == 0:
                        ins = v.tensor_scalar(dst, psW, scales[:, 0, m:m + 1],
                                              None, OP.mult)
                    else:
                        ins = v.scalar_tensor_tensor(
                            dst, psW, scales[:, e, m:m + 1], dst,
                            OP.mult, OP.add)
                    W(ins, sWacc)
                for m in range(NT):
                    for oc in range(2):
                        i = e * 32 + m * 2 + oc
                        slot = (m * 2 + oc) % 2
                        psY = arena[:, 2048 + slot * 512:
                                    2048 + (slot + 1) * 512]
                        v.wait_ge(sM2y.h, i + 1)
                        dst = acc[:, m, oc * 512:(oc + 1) * 512]
                        W(v.scalar_tensor_tensor(
                            dst, psY, wall[:, m, e:e + 1], dst, OP.mult,
                            OP.add), sAcc)
                        if e == NEXP - 1 and oc == 1:
                            v.wait_ge(sXra[m % 2].h, 16 * (m // 2 + 1))
                            W(v.tensor_add(xr[m % 2][:], xr[m % 2][:],
                                           acc[:, m, :]), sXn)
    esU.close()

    if _STAGE < 7:
        emit_taps(dict(acc=acc, hT=hT, wall=wall))
        esD.close(); esC.close(); es.close()
        return taps

    emit_taps(dict(acc=acc))
    esD.close(); esC.close(); es.close()
    return taps


def kernel(**inputs):
    x_full = np.asarray(inputs["L0_attn_x"], dtype=np.float32)
    d = _host_prep(inputs)

    nc = bass.Bass("TRN2", target_bir_lowering=False)
    build(nc)

    in_maps = []
    for c in range(8):
        m = dict(d)
        m["x"] = np.ascontiguousarray(x_full[c])
        in_maps.append(m)

    res = run_bass_kernel_spmd(nc, in_maps, core_ids=list(range(8)))
    out = np.stack([res.results[c]["y"] for c in range(8)], axis=0)
    if _TAPS:
        kernel.last_taps = [{t: res.results[c]["tap_" + t] for t in _TAPS}
                            for c in range(8)]
    return out
